# version 01
# baseline (speedup 1.0000x reference)
import math
import numpy as np
import ml_dtypes
import concourse.bass as bass
import concourse.mybir as mybir
from concourse.bass_utils import run_bass_kernel_spmd

F32 = mybir.dt.float32
BF16 = mybir.dt.bfloat16
U32 = mybir.dt.uint32
AF = mybir.ActivationFunctionType
OP = mybir.AluOpType
AX = mybir.AxisListType

T = 4096
D = 1024
NTT = 8
TTW = 512
NKC = 8

GELU_OK = True



class P:
    COMPUTE = ("pe", "act", "dve", "pool")
    SELF_ORDERED = ("pe",)
    QUEUES = ("qsp", "qact", "qpool")
    QUEUE_ENG = {"qsp": "sp", "qact": "act", "qpool": "pool"}

    NCHAIN = 8

    def __init__(self, nc):
        self.nc = nc
        self.q = {e: [] for e in ("pe", "act", "dve", "pool", "sp")}
        self.sems = {}
        self.count = {}
        sem_names = list(self.COMPUTE) + ["cc0", "cc1"]
        for qn in self.QUEUES:
            for c in range(self.NCHAIN):
                sem_names.append(qn + str(c))
        for s in sem_names:
            self.sems[s] = None
            self.count[s] = 0
        self.qi = {qn: 0 for qn in self.QUEUES}
        self.writers = {}
        self.readers = {}
        self.observed = {e: {} for e in ("pe", "act", "dve", "pool", "sp")}
        self.dram = {}
        self.sbuf = {}
        self.n_emitted = 0

    def dram_tensor(self, name, shape, dtype, kind="Internal"):
        t = self.nc.dram_tensor(name, list(shape), dtype, kind=kind)
        self.dram[name] = t
        return t

    def sbuf_tensor(self, name, shape, dtype):
        t = self.nc.alloc_sbuf_tensor(name, list(shape), dtype)
        self.sbuf[name] = t
        return t

    def _tok(self, sem_name):
        return (sem_name, self.count[sem_name])

    def _need_wait(self, eng, tok, my_chain=None):
        sem_name, val = tok
        if val <= 0:
            return None
        if sem_name == eng and eng in self.SELF_ORDERED:
            return None
        if my_chain is not None and sem_name == my_chain:
            return None
        got = self.observed[eng].get(sem_name, 0)
        if got >= val:
            return None
        return (sem_name, val)

    def emit(self, eng, build, reads=(), writes=(), queue=None, tag="", sreads=()):
        waits = {}
        my_chain = None
        if queue is not None:
            i = self.qi[queue]
            self.qi[queue] = i + 1
            my_chain = queue + str(i % self.NCHAIN)
        def add_tok(tok):
            w = self._need_wait(eng, tok, my_chain=my_chain)
            if w:
                waits[w[0]] = max(waits.get(w[0], 0), w[1])
        rs = [(self._key(b), sk) for (b, sk) in reads]
        srs = [(self._key(b), sk) for (b, sk) in sreads]
        ws = [(self._key(b), sk) for (b, sk) in writes]
        for (b, sk) in rs:
            for tok in self._write_toks(b, sk):
                add_tok(tok)
        for (b, sk) in srs:
            for tok in self._write_toks(b, sk):
                sem_name, val = tok
                if val <= 0:
                    continue
                if my_chain is not None and sem_name == my_chain:
                    continue
                got = self.observed[eng].get(sem_name, 0)
                if got < val:
                    waits[sem_name] = max(waits.get(sem_name, 0), val)
        rs = rs + srs
        for (b, sk) in ws:
            for tok in self._write_toks(b, sk):
                add_tok(tok)
            for tok in self._read_toks(b, sk):
                add_tok(tok)
        sem_name = my_chain if queue else eng
        step = 16 if queue else 1
        if queue is not None and self.count[sem_name] > 0:
            waits[sem_name] = max(waits.get(sem_name, 0), self.count[sem_name])
        self.count[sem_name] += step
        val_after = self.count[sem_name]
        for (b, sk) in ws:
            d = self.writers.setdefault(b, {})
            if sk == "a":
                d.clear()
            d[sk] = (sem_name, val_after)
            rd = self.readers.setdefault(b, {})
            if sk == "a":
                rd.clear()
            else:
                rd.pop(sk, None)
        for (b, sk) in rs:
            d = self.readers.setdefault(b, {}).setdefault(sk, {})
            d[sem_name] = val_after if False else max(d.get(sem_name, 0), val_after)
        wl = sorted(waits.items())
        obs = self.observed[eng]
        for s, v in wl:
            obs[s] = max(obs.get(s, 0), v)
        pp = self
        def thunk(engine_obj):
            for s, v in wl:
                engine_obj.wait_ge(pp.sems[s], v)
            inst = build(engine_obj)
            inst.then_inc(pp.sems[sem_name], step)
        self.q[eng].append(thunk)
        self.n_emitted += 1

    def _key(self, buf):
        if isinstance(buf, str):
            return buf
        return getattr(buf, "name", None) or str(id(buf))

    def _write_toks(self, b, sk):
        d = self.writers.get(b)
        if not d:
            return []
        if sk == "a":
            return list(d.values())
        toks = []
        if "a" in d:
            toks.append(d["a"])
        if sk in d:
            toks.append(d[sk])
        return toks

    def _read_toks(self, b, sk):
        d = self.readers.get(b)
        if not d:
            return []
        keys = list(d.keys()) if sk == "a" else (["a", sk] if "a" in d else [sk])
        toks = []
        for k in keys:
            if k in d:
                for s, v in d[k].items():
                    toks.append((s, v))
        return toks

    def dma(self, queue, out, in_, reads=(), writes=(), **kw):
        eng = self.QUEUE_ENG[queue]
        self.emit(eng, lambda e: e.dma_start(out=out, in_=in_, **kw),
                  reads=reads, writes=writes, queue=queue)

    def finalize_sems(self, stack):
        for s in list(self.sems.keys()):
            self.sems[s] = stack.enter_context(self.nc.semaphore("sem_" + s))

    def run_blocks(self):
        nc = self.nc
        with nc.Block() as block:
            @block.tensor
            def _(e):
                for f in self.q["pe"]:
                    f(e)

            @block.scalar
            def _(e):
                for f in self.q["act"]:
                    f(e)

            @block.vector
            def _(e):
                for f in self.q["dve"]:
                    f(e)

            @block.gpsimd
            def _(e):
                for f in self.q["pool"]:
                    f(e)

            @block.sync
            def _(e):
                for f in self.q["sp"]:
                    f(e)



class Ctx:
    pass


def setup_ctx(p, stack):
    nc = p.nc
    b = Ctx()
    b.U1 = p.sbuf_tensor("U1", [128, 33280], BF16)
    b.vaug = lambda: b.U1
    b.WSB_OFF = 0
    b.Y1_OFF = 16384
    b.S = p.sbuf_tensor("Sf", [128, 8192], F32)
    b.XB = p.sbuf_tensor("XB", [128, 8192], BF16)
    b.qsb = p.sbuf_tensor("qsb", [128, 4096], BF16)
    b.kaug = p.sbuf_tensor("kaug", [128, 4096], BF16)
    b.pt = p.sbuf_tensor("pt", [128, 1536], BF16)
    b.ptc = p.sbuf_tensor("ptc", [128, 512], BF16)
    b.vfut = p.sbuf_tensor("vfut", [128, 1040], BF16)
    b.attsq = p.sbuf_tensor("attsq", [128, 512], BF16)
    b.bsb = p.sbuf_tensor("bsb", [128, 512], F32)
    b.bsb2 = p.sbuf_tensor("bsb2", [128, 512], F32)
    b.osb = p.sbuf_tensor("osb", [128, 2048], BF16)
    b.osf = p.sbuf_tensor("osf", [128, 2048], F32)
    b.wstg = p.sbuf_tensor("wstg", [128, 1024], F32)
    b.bias = p.sbuf_tensor("biasc", [128, 64], F32)
    b.gb = p.sbuf_tensor("gbc", [128, 16], F32)
    b.wfm = p.sbuf_tensor("wfm", [16, 4096], BF16)
    b.stash = p.sbuf_tensor("stash", [128, 640], F32)
    b.rows = p.sbuf_tensor("rows", [1, 2048], F32)
    b.rowbf = p.sbuf_tensor("rowbf", [1, 1024], BF16)
    b.tiny = p.sbuf_tensor("tiny", [1, 64], F32)
    b.wgt = p.sbuf_tensor("wgt", [128, 80], BF16)
    b.mi32 = p.sbuf_tensor("mi32", [128, 8], U32)
    b.b2s = p.sbuf_tensor("b2s", [12, 1024], BF16)
    b.ones_red = p.sbuf_tensor("ones_red", [128, 2], BF16)
    b.ones_redf = p.sbuf_tensor("ones_redf", [128, 2], F32)
    b.ones_row = p.sbuf_tensor("ones_row", [1, 128], F32)
    b.ones_rowb = p.sbuf_tensor("ones_rowb", [1, 128], BF16)
    b.rope = p.sbuf_tensor("ropec", [128, 1024], F32)
    b.rtmp = p.sbuf_tensor("rtmp", [128, 1024], F32)
    b.dec = p.sbuf_tensor("decc", [128, 32], F32)
    b.idb = p.sbuf_tensor("idb", [128, 128], BF16)
    b.idf = p.sbuf_tensor("idf", [128, 128], F32)
    b.iota8 = p.sbuf_tensor("iota8", [128, 8], F32)
    b.sut = p.sbuf_tensor("sut", [128, 128], F32)
    b.cmask = p.sbuf_tensor("cmask", [128, 2048], BF16)
    b.cminv = p.sbuf_tensor("cminv", [128, 2048], BF16)
    b.coremask = p.sbuf_tensor("coremask", [1, 64], F32)
    b.zero512 = p.sbuf_tensor("zero512", [128, 512], BF16)
    b.ps = {}
    for nm in ["pa", "pb", "po", "po2", "pc", "pr", "pg"]:
        b.ps[nm] = stack.enter_context(nc.psum_tensor(nm, [128, 512], F32))
    b.pst = stack.enter_context(nc.psum_tensor("pst", [128, 1024], BF16))
    return b


def load_consts(p, b, cin):
    for nm, dst in [("c_idb", b.idb), ("c_idf", b.idf), ("c_iota8", b.iota8),
                    ("c_sut", b.sut), ("c_cmask", b.cmask),
                    ("c_cminv", b.cminv), ("c_coremask", b.coremask)]:
        h = cin[nm]
        p.dma("qsp", dst[:, :], h.ap()[:, :], reads=(), writes=[(dst, "a")])
    p.emit("dve", lambda e: e.memset(b.ones_red[:, :], 1.0), writes=[(b.ones_red, "a")])
    p.emit("dve", lambda e: e.memset(b.ones_redf[:, :], 1.0), writes=[(b.ones_redf, "a")])
    p.emit("dve", lambda e: e.memset(b.ones_row[:, :], 1.0), writes=[(b.ones_row, "a")])
    p.emit("dve", lambda e: e.memset(b.ones_rowb[:, :], 1.0), writes=[(b.ones_rowb, "a")])
    p.emit("dve", lambda e: e.memset(b.zero512[:, :], 0.0), writes=[(b.zero512, "a")])


def dk(handle, r0, r1, c0, c1):
    nm = handle.name if hasattr(handle, "name") else str(handle)
    out = []
    for r in range(r0 // 64, (r1 + 63) // 64):
        for c in range(c0 // 512, (c1 + 511) // 512):
            out.append((nm, ("b", r, c)))
    return out


def u1g(lo, hi):
    return [("U1", ("g", g)) for g in range(lo // 2048, (hi + 2047) // 2048)]



def load_w(p, b, w_dram, Din, oc0, ocw, tag="w"):
    nkc = Din // 128
    assert nkc * ocw <= 16384
    base = b.WSB_OFF
    for kc in range(nkc):
        for c0 in range(0, ocw, 1024):
            cw = min(1024, ocw - c0)
            src = w_dram.ap()[kc * 128:(kc + 1) * 128, oc0 + c0: oc0 + c0 + cw]
            p.dma("qsp", b.wstg[:, 0:cw], src,
                  writes=[(b.wstg, "a")])
            dst = b.U1[:, base + kc * ocw + c0: base + kc * ocw + c0 + cw]
            p.emit("dve", lambda e, s=b.wstg[:, 0:cw], d=dst: e.tensor_copy(d, s),
                   reads=[(b.wstg, "a")],
                   writes=u1g(base + kc * ocw + c0, base + kc * ocw + c0 + cw))
    return base


def w_tile(b, base, ocw, kc, j0, m):
    off = base + kc * ocw + j0
    return b.U1[:, off: off + m]



def gemm_fm(p, b, out_dram, in_dram, w_dram, Din, Dout, *,
            bias_pcol=None,
            act=None,
            scale=1.0, bias_scale=None,
            out_dtype="bf16",
            resid_dram=None,
            out_row0=0,
            in_row0=0,
            tag="g"):
    nkc = Din // 128
    noc = Dout // 128
    if bias_pcol is not None:
        p.dma("qsp", b.bias[:, 0:noc], bias_pcol.ap()[:, 0:noc],
              writes=[(b.bias, "a")])
        if bias_scale is not None:
            p.emit("dve", lambda e: e.tensor_scalar(b.bias[:, 0:noc], b.bias[:, 0:noc],
                                                    bias_scale, None, OP.mult),
                   reads=[(b.bias, "a")], writes=[(b.bias, "a")])
    ocg = max(1, min(Dout, 16384 * 128 // Din))
    ocg = (ocg // 128) * 128
    psr = ["pa", "pb", "pc", "pr"]
    pi = 0
    for g0 in range(0, Dout, ocg):
        gw = min(ocg, Dout - g0)
        base = load_w(p, b, w_dram, Din, g0, gw, tag=tag)
        wkeys = u1g(base, base + nkc * gw)
        for tt in range(NTT):
            assert nkc <= 16
            par = (tt % 2) if nkc <= 8 else 0
            xoff = par * 4096
            for kc in range(nkc):
                src = in_dram.ap()[in_row0 + kc * 128: in_row0 + (kc + 1) * 128,
                                   tt * TTW:(tt + 1) * TTW]
                p.dma("qsp", b.XB[:, xoff + kc * TTW: xoff + (kc + 1) * TTW], src,
                      reads=dk(in_dram, in_row0 + kc * 128, in_row0 + (kc + 1) * 128,
                               tt * TTW, (tt + 1) * TTW),
                      writes=[(b.XB, ("g", xoff // 512 + kc))])
            for j0 in range(0, gw, 128):
                oc = (g0 + j0) // 128
                ps = b.ps[psr[pi % 4]]
                pi += 1
                for kc in range(nkc):
                    lhsT = w_tile(b, base, gw, kc, j0, 128)
                    rhs = b.XB[:, xoff + kc * TTW: xoff + (kc + 1) * TTW]
                    p.emit("pe",
                           lambda e, o=ps[:, :], l=lhsT, r=rhs, s=(kc == 0),
                                  t=(kc == nkc - 1):
                               e.matmul(o, l, r, start=s, stop=t),
                           reads=wkeys + [(b.XB, ("g", xoff // 512 + kc))],
                           writes=[(ps, "a")])
                ob = (pi % 4)
                if resid_dram is not None:
                    rsl = b.S[:, (oc % 4) * 512: (oc % 4) * 512 + 512]
                    p.dma("qsp", rsl,
                          resid_dram.ap()[out_row0 + oc * 128: out_row0 + oc * 128 + 128,
                                          tt * TTW:(tt + 1) * TTW],
                          reads=dk(resid_dram, out_row0 + oc * 128,
                                   out_row0 + oc * 128 + 128, tt * TTW, (tt + 1) * TTW),
                          writes=[(b.S, ("g", oc % 4))])
                    osl = b.osf[:, ob * 512:(ob + 1) * 512]
                    if bias_pcol is not None:
                        p.emit("dve",
                               lambda e, o=osl, i=ps[:, :], sc=b.bias[:, oc:oc + 1]:
                                   e.tensor_scalar(o, i, sc, None, OP.add),
                               reads=[(ps, "a"), (b.bias, "a")],
                               writes=[(b.osf, ("o", ob))])
                        p.emit("dve",
                               lambda e, o=osl, i=osl, r=rsl: e.tensor_add(o, i, r),
                               reads=[(b.S, ("g", oc % 4)), (b.osf, ("o", ob))],
                               writes=[(b.osf, ("o", ob))])
                    else:
                        p.emit("dve",
                               lambda e, o=osl, i=ps[:, :], r=rsl: e.tensor_add(o, i, r),
                               reads=[(ps, "a"), (b.S, ("g", oc % 4))],
                               writes=[(b.osf, ("o", ob))])
                    p.dma("qact", out_dram.ap()[out_row0 + oc * 128: out_row0 + oc * 128 + 128,
                                                tt * TTW:(tt + 1) * TTW],
                          osl, reads=[(b.osf, ("o", ob))],
                          writes=dk(out_dram, out_row0 + oc * 128,
                                    out_row0 + oc * 128 + 128, tt * TTW, (tt + 1) * TTW))
                else:
                    osl = (b.osb if out_dtype == "bf16" else b.osf)[:, ob * 512:(ob + 1) * 512]
                    okey = (b.osb if out_dtype == "bf16" else b.osf, ("o", ob))
                    fn = act if act is not None else AF.Copy
                    if bias_pcol is not None and act is None:
                        p.emit("dve",
                               lambda e, o=osl, i=ps[:, :], sc=scale,
                                      ba=b.bias[:, oc:oc + 1]:
                                   e.tensor_scalar(o, i, sc, ba, OP.mult, OP.add)
                                   if sc != 1.0 else
                                   e.tensor_scalar(o, i, ba, None, OP.add),
                               reads=[(ps, "a"), (b.bias, "a")], writes=[okey])
                    elif bias_pcol is not None:
                        p.emit("act",
                               lambda e, o=osl, i=ps[:, :], f=fn, sc=scale,
                                      ba=b.bias[:, oc:oc + 1]:
                                   e.activation(out=o, in_=i, func=f, bias=ba, scale=sc),
                               reads=[(ps, "a"), (b.bias, "a")], writes=[okey])
                    else:
                        p.emit("act",
                               lambda e, o=osl, i=ps[:, :], f=fn, sc=scale:
                                   e.activation(out=o, in_=i, func=f, scale=sc),
                               reads=[(ps, "a")], writes=[okey])
                    p.dma("qact", out_dram.ap()[out_row0 + oc * 128: out_row0 + oc * 128 + 128,
                                                tt * TTW:(tt + 1) * TTW],
                          osl, reads=[okey],
                          writes=dk(out_dram, out_row0 + oc * 128,
                                    out_row0 + oc * 128 + 128, tt * TTW, (tt + 1) * TTW))



def norm_fm(p, b, out_bf, in_f32, Dn, *, eps=1e-5, kind="ln",
            gamma_pcol=None, beta_pcol=None, out_f32=None,
            final_resid=None, final_out=None, tag="ln"):
    nkc = Dn // 128
    if gamma_pcol is not None:
        p.dma("qsp", b.gb[:, 0:nkc], gamma_pcol.ap()[:, 0:nkc], writes=[(b.gb, "g")])
    if beta_pcol is not None:
        p.dma("qsp", b.gb[:, 8:8 + nkc], beta_pcol.ap()[:, 0:nkc], writes=[(b.gb, "b")])
    for tt in range(NTT):
        for kc in range(nkc):
            p.dma("qsp", b.S[:, (kc % 8) * 512:(kc % 8) * 512 + 512],
                  in_f32.ap()[kc * 128:(kc + 1) * 128, tt * TTW:(tt + 1) * TTW],
                  reads=dk(in_f32, kc * 128, (kc + 1) * 128, tt * TTW, (tt + 1) * TTW),
                  writes=[(b.S, ("g", kc))])
        psq = b.ps["pr"]
        pss = b.ps["pc"]
        for kc in range(nkc):
            xsl = b.S[:, (kc % 8) * 512:(kc % 8) * 512 + 512]
            xbk = b.XB[:, kc * 512: kc * 512 + 512]
            p.emit("dve", lambda e, o=xbk, i=xsl: e.tensor_copy(o, i),
                   reads=[(b.S, ("g", kc))], writes=[(b.XB, ("g", kc))])
            sqk = b.XB[:, 4096 + (kc % 8) * 512: 4096 + (kc % 8) * 512 + 512]
            p.emit("dve", lambda e, o=sqk, i=xbk: e.tensor_mul(o, i, i),
                   reads=[(b.XB, ("g", kc))], writes=[(b.XB, ("g", 8 + kc))])
            if kind == "ln":
                p.emit("pe", lambda e, o=pss[0:1, :], l=b.ones_red[:, 0:1], r=xbk,
                              s=(kc == 0), t=(kc == nkc - 1):
                           e.matmul(o, l, r, start=s, stop=t),
                       reads=[(b.XB, ("g", kc)), (b.ones_red, "a")],
                       writes=[(pss, "a")])
            p.emit("pe", lambda e, o=psq[0:1, :], l=b.ones_red[:, 0:1], r=sqk,
                          s=(kc == 0), t=(kc == nkc - 1):
                       e.matmul(o, l, r, start=s, stop=t),
                   reads=[(b.XB, ("g", 8 + kc)), (b.ones_red, "a")],
                   writes=[(psq, "a")])
        mrow = b.rows[0:1, 0:512]
        vrow = b.rows[0:1, 512:1024]
        rrow = b.rows[0:1, 1024:1536]
        if kind == "ln":
            p.emit("dve", lambda e: e.tensor_scalar(mrow, pss[0:1, :], 1.0 / Dn, None, OP.mult),
                   reads=[(pss, "a")], writes=[("rows", "m")])
            p.emit("dve", lambda e: e.tensor_mul(vrow, mrow, mrow),
                   reads=[("rows", "m")], writes=[("rows", "v")])
            p.emit("dve", lambda e: e.scalar_tensor_tensor(vrow, psq[0:1, :], 1.0 / Dn, vrow,
                                                           OP.mult, OP.subtract),
                   reads=[(psq, "a"), ("rows", "v")], writes=[("rows", "v")])
        else:
            p.emit("dve", lambda e: e.tensor_scalar(vrow, psq[0:1, :], 1.0 / Dn, None, OP.mult),
                   reads=[(psq, "a")], writes=[("rows", "v")])
        p.emit("dve", lambda e: e.tensor_scalar(rrow, vrow, eps, None, OP.add),
               reads=[("rows", "v")], writes=[("rows", "r")])
        p.emit("act", lambda e: e.activation(out=rrow, in_=rrow, func=AF.Sqrt),
               reads=[("rows", "r")], writes=[("rows", "r")])
        p.emit("dve", lambda e: e.reciprocal(rrow, rrow),
               reads=[("rows", "r")], writes=[("rows", "r")])
        pbm = b.ps["pa"]
        pbr = b.ps["pb"]
        if kind == "ln":
            p.emit("pe", lambda e: e.matmul(pbm[:, :], b.ones_row[0:1, :], mrow,
                                            start=True, stop=True),
                   reads=[("rows", "m"), (b.ones_row, "a")], writes=[(pbm, "a")])
        p.emit("pe", lambda e: e.matmul(pbr[:, :], b.ones_row[0:1, :], rrow,
                                        start=True, stop=True),
               reads=[("rows", "r"), (b.ones_row, "a")], writes=[(pbr, "a")])
        for kc in range(nkc):
            xsl = b.S[:, (kc % 8) * 512:(kc % 8) * 512 + 512]
            ot = b.osf[:, (kc % 4) * 512:(kc % 4) * 512 + 512]
            if kind == "ln":
                p.emit("dve", lambda e, o=ot, i=xsl: e.tensor_sub(o, i, pbm[:, :]),
                       reads=[(b.S, ("g", kc)), (pbm, "a")], writes=[(b.osf, ("o", kc % 4))])
                src2 = ot
            else:
                src2 = xsl
            obf = b.osb[:, (kc % 4) * 512:(kc % 4) * 512 + 512]
            if gamma_pcol is None:
                p.emit("dve", lambda e, o=obf, i=src2, k=kc:
                           e.tensor_mul(o, i, pbr[:, :]),
                       reads=[(b.osf, ("o", kc % 4)) if kind == "ln" else (b.S, ("g", kc)),
                              (pbr, "a")],
                       writes=[(b.osb, ("o", kc % 4))])
            else:
                p.emit("dve", lambda e, o=ot, i=src2:
                           e.tensor_mul(o, i, pbr[:, :]),
                       reads=[(b.osf, ("o", kc % 4)) if kind == "ln" else (b.S, ("g", kc)),
                              (pbr, "a")],
                       writes=[(b.osf, ("o", kc % 4))])
                if beta_pcol is not None:
                    p.emit("dve", lambda e, o=obf, i=ot, k=kc:
                               e.tensor_scalar(o, i, b.gb[:, k:k + 1], b.gb[:, 8 + k:9 + k],
                                               OP.mult, OP.add),
                           reads=[(b.osf, ("o", kc % 4)), (b.gb, "g"), (b.gb, "b")],
                           writes=[(b.osb, ("o", kc % 4))])
                else:
                    p.emit("dve", lambda e, o=obf, i=ot, k=kc:
                               e.tensor_scalar(o, i, b.gb[:, k:k + 1], None, OP.mult),
                           reads=[(b.osf, ("o", kc % 4)), (b.gb, "g")],
                           writes=[(b.osb, ("o", kc % 4))])
            if final_out is not None:
                rsl = b.S[:, 6144 + (kc % 4) * 512: 6144 + (kc % 4) * 512 + 512]
                p.dma("qsp", rsl,
                      final_resid.ap()[kc * 128:(kc + 1) * 128, tt * TTW:(tt + 1) * TTW],
                      reads=dk(final_resid, kc * 128, (kc + 1) * 128, tt * TTW,
                               (tt + 1) * TTW),
                      writes=[(b.S, ("g", 12 + kc % 4))])
                of3 = b.osf[:, (kc % 4) * 512:(kc % 4) * 512 + 512]
                p.emit("dve", lambda e, o=of3, i=obf, r=rsl: e.tensor_add(o, i, r),
                       reads=[(b.osb, ("o", kc % 4)), (b.S, ("g", 12 + kc % 4))],
                       writes=[(b.osf, ("o", kc % 4))])
                p.dma("qact", final_out.ap()[kc * 128:(kc + 1) * 128, tt * TTW:(tt + 1) * TTW],
                      of3, reads=[(b.osf, ("o", kc % 4))],
                      writes=dk(final_out, kc * 128, (kc + 1) * 128, tt * TTW,
                                (tt + 1) * TTW))
            else:
                p.dma("qact", out_bf.ap()[kc * 128:(kc + 1) * 128, tt * TTW:(tt + 1) * TTW],
                      obf, reads=[(b.osb, ("o", kc % 4))],
                      writes=dk(out_bf, kc * 128, (kc + 1) * 128, tt * TTW, (tt + 1) * TTW))
            if out_f32 is not None:
                of2 = b.osf[:, (kc % 4) * 512:(kc % 4) * 512 + 512]
                p.emit("dve", lambda e, o=of2, i=obf: e.tensor_copy(o, i),
                       reads=[(b.osb, ("o", kc % 4))], writes=[(b.osf, ("o", kc % 4))])
                p.dma("qact", out_f32.ap()[kc * 128:(kc + 1) * 128, tt * TTW:(tt + 1) * TTW],
                      of2, reads=[(b.osf, ("o", kc % 4))],
                      writes=dk(out_f32, kc * 128, (kc + 1) * 128, tt * TTW, (tt + 1) * TTW))



def make_consts_host(core_id):
    c = {}
    c["c_idb"] = np.eye(128, dtype=ml_dtypes.bfloat16)
    c["c_idf"] = np.eye(128, dtype=np.float32)
    c["c_iota8"] = np.tile(np.arange(8, dtype=np.float32), (128, 1))
    c["c_sut"] = np.triu(np.ones((128, 128), np.float32), 1)
    cm = np.zeros((128, 4 * 512), ml_dtypes.bfloat16)
    ii = np.arange(128)[:, None]
    jj = np.arange(512)[None, :]
    for v in range(4):
        cm[:, v * 512:(v + 1) * 512] = (ii + v * 128 <= jj).astype(ml_dtypes.bfloat16)
    c["c_cmask"] = cm
    c["c_cminv"] = (1.0 - cm.astype(np.float32)).astype(ml_dtypes.bfloat16)
    msk = np.zeros((1, 64), np.float32)
    for cc in range(core_id):
        msk[0, cc * 8:(cc + 1) * 8] = 1.0
    c["c_coremask"] = msk
    return c

CONST_SPECS = [("c_idb", [128, 128], BF16), ("c_idf", [128, 128], F32),
               ("c_iota8", [128, 8], F32), ("c_sut", [128, 128], F32),
               ("c_cmask", [128, 2048], BF16), ("c_cminv", [128, 2048], BF16),
               ("c_coremask", [1, 64], F32)]


def pcol(v):
    v = np.asarray(v, np.float32)
    n = (v.shape[0] + 127) // 128
    out = np.zeros((128, n), np.float32)
    out[:, :] = v.reshape(n, 128).T
    return out


class P_drain:
    pass


def p_drain(p):
    vals = {}
    for qn in p.QUEUES:
        for c in range(p.NCHAIN):
            nm = qn + str(c)
            if p.count[nm] > 0:
                vals[nm] = p.count[nm]
    pp = p
    def thunk(e):
        for q, v in vals.items():
            e.wait_ge(pp.sems[q], v)
    p.q["pool"].append(thunk)


def build_nc(emit_fn, io_spec, core_id=0):
    from contextlib import ExitStack
    import os
    nc = bass.Bass("TRN2", target_bir_lowering=False, debug=False,
                   detect_race_conditions=bool(os.environ.get("RACEDET")))
    stack = ExitStack()
    with stack:
        p = P(nc)
        t = {}
        for name, shape, dtype, kind in io_spec:
            t[name] = p.dram_tensor(name, shape, dtype, kind=kind)
        for name, shape, dtype in CONST_SPECS:
            t[name] = p.dram_tensor(name, shape, dtype, kind="ExternalInput")
        p.finalize_sems(stack)
        b = setup_ctx(p, stack)
        load_consts(p, b, t)
        emit_fn(p, b, t)
        p_drain(p)
        p.run_blocks()
    return nc, p



def attention(p, b, *, out_dram, q_fm=None, k_fm=None, v_tm, mode,
              q_prep=None, k_prep=None, nheads=16, shift=40.0, tag="att"):
    NKB = T // 128
    p.emit("dve", lambda e: e.memset(
        bass.AP(b.U1, 64, [[33280, 128], [1040, NKB], [65, 16], [1, 1]]), 1.0),
        writes=u1g(0, 33280))
    for kb in range(NKB):
        stg = b.XB[:, (kb % 4) * 1024:(kb % 4) * 1024 + 1024]
        skeys = [(b.XB, ("g", (kb % 4) * 2)), (b.XB, ("g", (kb % 4) * 2 + 1))]
        p.dma("qsp", stg, v_tm.ap()[kb * 128:(kb + 1) * 128, 0:1024],
              reads=dk(v_tm, kb * 128, (kb + 1) * 128, 0, 1024), writes=skeys)
        dst = bass.AP(b.U1, kb * 1040, [[33280, 128], [65, 16], [1, 64]])
        src = bass.AP(b.XB, (kb % 4) * 1024, [[8192, 128], [64, 16], [1, 64]])
        p.emit("dve", lambda e, d=dst, s=src: e.tensor_copy(d, s),
               reads=skeys, writes=u1g(kb * 1040, kb * 1040 + 1040))
    p.emit("dve", lambda e: e.memset(b.kaug[64:65, :], 1.0),
           writes=[(b.kaug, "one")])
    vkeys_all = u1g(0, 33280)

    for h in range(nheads):
        if k_prep is None:
            p.dma("qsp", b.kaug[0:64, :], k_fm.ap()[h * 64:(h + 1) * 64, :],
                  reads=dk(k_fm, h * 64, (h + 1) * 64, 0, T),
                  writes=[(b.kaug, "hd")])
        else:
            k_prep(h)
        if q_prep is None:
            p.dma("qsp", b.qsb[0:64, :], q_fm.ap()[h * 64:(h + 1) * 64, :],
                  reads=dk(q_fm, h * 64, (h + 1) * 64, 0, T),
                  writes=[(b.qsb, "hd")])
        else:
            q_prep(h)
        pr = b.ps["pr"]
        for kt in range(8):
            ksl = b.kaug[0:64, kt * 512:(kt + 1) * 512]
            p.emit("dve", lambda e, k=ksl: e.tensor_mul(b.attsq[0:64, 0:512], k, k),
                   reads=[(b.kaug, "hd")], writes=[(b.attsq, "a")])
            p.emit("pe", lambda e: e.matmul(pr[0:1, :], b.ones_red[0:64, 0:1],
                                            b.attsq[0:64, 0:512], start=True, stop=True),
                   reads=[(b.attsq, "a"), (b.ones_red, "a")], writes=[(pr, "a")])
            p.emit("dve", lambda e, kt=kt: e.reduce_max(b.tiny[0:1, kt:kt + 1],
                                                        pr[0:1, :], axis=AX.X),
                   reads=[(pr, "a")], writes=[(b.tiny, ("km", kt))])
        p.emit("dve", lambda e: e.reduce_max(b.tiny[0:1, 8:9], b.tiny[0:1, 0:8],
                                             axis=AX.X),
               reads=[(b.tiny, ("km", kt)) for kt in range(8)],
               writes=[(b.tiny, "k2")])
        p.emit("act", lambda e: e.activation(out=b.tiny[0:1, 9:10], in_=b.tiny[0:1, 8:9],
                                             func=AF.Sqrt),
               reads=[(b.tiny, "k2")], writes=[(b.tiny, "ks")])
        for qt in range(NTT):
            qsl = b.qsb[0:64, qt * 512:(qt + 1) * 512]
            p.emit("dve", lambda e, q=qsl: e.tensor_mul(b.attsq[0:64, 0:512], q, q),
                   reads=[(b.qsb, "hd")], writes=[(b.attsq, "a")])
            p.emit("pe", lambda e: e.matmul(pr[0:1, :], b.ones_red[0:64, 0:1],
                                            b.attsq[0:64, 0:512], start=True, stop=True),
                   reads=[(b.attsq, "a"), (b.ones_red, "a")], writes=[(pr, "a")])
            brow = b.rows[0:1, 512:1024]
            p.emit("act", lambda e: e.activation(out=brow, in_=pr[0:1, :], func=AF.Sqrt),
                   reads=[(pr, "a")], writes=[("rows", "b")])
            p.emit("dve", lambda e: e.tensor_scalar(brow, brow, b.tiny[0:1, 9:10],
                                                    None, OP.mult),
                   reads=[("rows", "b"), (b.tiny, "ks")], writes=[("rows", "b")])
            p.emit("dve", lambda e, qt=qt: e.tensor_scalar(
                       b.qsb[64:65, qt * 512:(qt + 1) * 512], brow, -1.0, shift,
                       OP.mult, OP.add),
                   reads=[("rows", "b")], writes=[(b.qsb, ("nb", qt))])
        if mode == "zmask":
            p.emit("dve", lambda e: e.memset(b.vfut[:, :], 0.0),
                   writes=[(b.vfut, "a")])
            fut_loaded = NKB
        for qt in range(NTT - 1, -1, -1):
            kb_hi = NKB if mode == "none" else 4 * qt + 4
            if mode == "zmask":
                for kb in range(kb_hi, fut_loaded):
                    p.emit("dve", lambda e, kb=kb: e.tensor_add(
                               b.vfut[:, :], b.vfut[:, :],
                               b.U1[:, kb * 1040:(kb + 1) * 1040]),
                           reads=[(b.vfut, "a")] + u1g(kb * 1040, (kb + 1) * 1040),
                           writes=[(b.vfut, "a")])
                fut_loaded = kb_hi
                pc = b.ps["pc"]
                p.emit("dve", lambda e, qt=qt: e.tensor_copy(
                           b.rowbf[0:1, 0:512], b.qsb[64:65, qt * 512:(qt + 1) * 512]),
                       reads=[(b.qsb, ("nb", qt))], writes=[(b.rowbf, "a")])
                p.emit("pe", lambda e: e.matmul(
                           pc[:, :], b.ones_rowb[0:1, :],
                           b.rowbf[0:1, 0:512], start=True, stop=True),
                       reads=[(b.rowbf, "a"), (b.ones_rowb, "a")],
                       writes=[(pc, "a")])
                p.emit("act", lambda e: e.activation(out=b.ptc[:, :], in_=pc[:, :],
                                                     func=AF.Exp),
                       reads=[(pc, "a")], writes=[(b.ptc, "a")])
            po = b.ps["po"] if qt % 2 == 0 else b.ps["po2"]
            first = True
            last_kb = kb_hi - 1
            for kb in range(kb_hi):
                par = kb % 2
                ps_st = b.ps["pa"] if par == 0 else b.ps["pb"]
                ptt = b.pt[:, par * 512:(par + 1) * 512]
                ptkey = (b.pt, ("t", par))
                p.emit("pe", lambda e, kb=kb, qt=qt, o=ps_st: e.matmul(
                           o[:, :], b.kaug[0:65, kb * 128:(kb + 1) * 128],
                           b.qsb[0:65, qt * 512:(qt + 1) * 512],
                           start=True, stop=True),
                       reads=[(b.kaug, "hd"), (b.kaug, "one"), (b.qsb, "hd"),
                              (b.qsb, ("nb", qt))],
                       writes=[(ps_st, "a")])
                p.emit("act", lambda e, o=ptt, i=ps_st: e.activation(
                           out=o, in_=i[:, :], func=AF.Exp),
                       reads=[(ps_st, "a")], writes=[ptkey])
                if mode != "none" and kb >= 4 * qt:
                    v = kb - 4 * qt
                    msk = b.cmask[:, v * 512:(v + 1) * 512]
                    if mode == "causal":
                        p.emit("dve", lambda e, o=ptt, m=msk: e.tensor_mul(o, o, m),
                               reads=[ptkey, (b.cmask, "a")], writes=[ptkey])
                    else:
                        iv = b.cminv[:, v * 512:(v + 1) * 512]
                        p.emit("dve", lambda e, m=iv: e.tensor_mul(
                                   b.attsq[:, 0:512], b.ptc[:, :], m),
                               reads=[(b.cminv, "a"), (b.ptc, "a")],
                               writes=[(b.attsq, "a")])
                        p.emit("dve", lambda e, o=ptt, m=msk: e.tensor_mul(o, o, m),
                               reads=[ptkey, (b.cmask, "a")], writes=[ptkey])
                        p.emit("dve", lambda e, o=ptt: e.tensor_add(
                                   o, o, b.attsq[:, 0:512]),
                               reads=[ptkey, (b.attsq, "a")], writes=[ptkey])
                stop_here = (kb == last_kb) and (mode != "zmask")
                p.emit("pe", lambda e, kb=kb, o=po, pt_=ptt, s=first, st=stop_here, h=h:
                           e.matmul(
                           o[0:65, :], b.U1[:, kb * 1040 + h * 65: kb * 1040 + h * 65 + 65],
                           pt_, start=s, stop=st),
                       reads=[ptkey] + u1g(kb * 1040 + h * 65, kb * 1040 + h * 65 + 65),
                       writes=[(po, "a")])
                first = False
            if mode == "zmask":
                p.emit("pe", lambda e, o=po, h=h, s=first: e.matmul(
                           o[0:65, :], b.vfut[:, h * 65: h * 65 + 65], b.ptc[:, :],
                           start=s, stop=True),
                       reads=[(b.vfut, "a"), (b.ptc, "a")], writes=[(po, "a")])
            invz = b.rows[0:1, 1024:1536]
            p.emit("dve", lambda e, o=invz, i=po: e.reciprocal(o, i[64:65, :]),
                   reads=[(po, "a")], writes=[("rows", "z")])
            pc2 = b.ps["pg"]
            p.emit("pe", lambda e, o=pc2: e.matmul(o[0:64, :], b.ones_row[0:1, 0:64],
                                                   invz, start=True, stop=True),
                   reads=[("rows", "z"), (b.ones_row, "a")], writes=[(pc2, "a")])
            p.emit("dve", lambda e: e.tensor_copy(b.bsb[0:64, :], pc2[0:64, :]),
                   reads=[(pc2, "a")], writes=[(b.bsb, "a")])
            ob = qt % 4
            osl = b.osb[:, ob * 512:(ob + 1) * 512]
            p.emit("dve", lambda e, o=osl, i=po: e.tensor_mul(o[0:64, :], i[0:64, :],
                                                              b.bsb[0:64, :]),
                   reads=[(po, "a"), (b.bsb, "a")], writes=[(b.osb, ("o", ob))])
            p.dma("qact", out_dram.ap()[h * 64:(h + 1) * 64, qt * 512:(qt + 1) * 512],
                  b.osb[0:64, ob * 512:(ob + 1) * 512], reads=[(b.osb, ("o", ob))],
                  writes=dk(out_dram, h * 64, (h + 1) * 64, qt * 512, (qt + 1) * 512))



def gemm_tm(p, b, in_dram, w_dram, Din, Dout, *,
            out_fm=None, out_tm=None, bias_row=None, rope=None, rope_tab=None,
            scale=1.0, tag="gt"):
    nkc = Din // 128
    assert nkc <= 8
    if bias_row is not None:
        p.dma("qsp", b.rows[0:1, 0:Dout], bias_row.ap()[0:1, 0:Dout],
              writes=[("rows", "br")])
        p.emit("dve", lambda e: e.tensor_copy(b.rowbf[0:1, 0:Dout], b.rows[0:1, 0:Dout]),
               reads=[("rows", "br")], writes=[(b.rowbf, "a")])
    for g0 in range(0, Dout, 512):
        gw = min(512, Dout - g0)
        base = load_w(p, b, w_dram, Din, g0, gw, tag=tag)
        wkeys = u1g(base, base + nkc * gw)
        for tt in range(NTT):
            par = tt % 2
            xoff = par * 4096
            for kc in range(nkc):
                src = in_dram.ap()[kc * 128:(kc + 1) * 128, tt * TTW:(tt + 1) * TTW]
                p.dma("qsp", b.XB[:, xoff + kc * TTW: xoff + (kc + 1) * TTW], src,
                      reads=dk(in_dram, kc * 128, (kc + 1) * 128, tt * TTW, (tt + 1) * TTW),
                      writes=[(b.XB, ("g", xoff // 512 + kc))])
            if rope is not None:
                pass
            for sub in range(4):
                t0 = tt * TTW + sub * 128
                ps = b.ps["pa"] if (sub % 2 == 0) else b.ps["pb"]
                if bias_row is not None:
                    p.emit("pe", lambda e, o=ps, g0=g0, gw=gw: e.matmul(
                               o[:, 0:gw], b.ones_rowb[0:1, :],
                               b.rowbf[0:1, g0:g0 + gw], start=True, stop=False),
                           reads=[(b.rowbf, "a"), (b.ones_rowb, "a")], writes=[(ps, "a")])
                for kc in range(nkc):
                    lhsT = b.XB[:, xoff + kc * TTW + sub * 128: xoff + kc * TTW + sub * 128 + 128]
                    rhs = w_tile(b, base, gw, kc, 0, gw)
                    p.emit("pe", lambda e, o=ps, l=lhsT, r=rhs, gw=gw,
                                  s=(kc == 0 and bias_row is None), t=(kc == nkc - 1):
                               e.matmul(o[:, 0:gw], l, r, start=s, stop=t),
                           reads=wkeys + [(b.XB, ("g", xoff // 512 + kc))],
                           writes=[(ps, "a")])
                ob = sub % 2
                osl = b.osb[:, ob * 512: ob * 512 + gw]
                okey = (b.osb, ("o", ob))
                if rope is None:
                    p.emit("act", lambda e, o=osl, i=ps, gw=gw, sc=scale: e.activation(
                               out=o, in_=i[:, 0:gw], func=AF.Copy, scale=sc),
                           reads=[(ps, "a")], writes=[okey])
                else:
                    import os as _os
                    rmode = _os.environ.get("ROPE_MODE", "full")
                    p.dma("qsp", b.rope[:, 0:512], rope_tab.ap()[t0:t0 + 128, 0:512],
                          writes=[(b.rope, "a")])
                    if rmode == "copy":
                        p.emit("act", lambda e, o=osl, i=ps, gw=gw: e.activation(
                                   out=o, in_=i[:, 0:gw], func=AF.Copy),
                               reads=[(ps, "a"), (b.rope, "a")], writes=[okey])
                    cos = b.rope[:, 0:256]
                    sin = b.rope[:, 256:512]
                    if rope == "il":
                        qe = bass.AP(ps, 0, [[512, 128], [2, 256]])
                        qo = bass.AP(ps, 1, [[512, 128], [2, 256]])
                        oe = bass.AP(b.osb, ob * 512, [[2048, 128], [2, 256]])
                        oo = bass.AP(b.osb, ob * 512 + 1, [[2048, 128], [2, 256]])
                    else:
                        qe = bass.AP(ps, 0, [[512, 128], [64, 8], [1, 32]])
                        qo = bass.AP(ps, 32, [[512, 128], [64, 8], [1, 32]])
                        oe = bass.AP(b.osb, ob * 512, [[2048, 128], [64, 8], [1, 32]])
                        oo = bass.AP(b.osb, ob * 512 + 32, [[2048, 128], [64, 8], [1, 32]])
                    if rmode == "copy":
                        if out_tm is not None:
                            p.dma("qact", out_tm.ap()[t0:t0 + 128, g0:g0 + gw],
                                  b.osb[:, ob * 512: ob * 512 + gw], reads=[okey],
                                  writes=dk(out_tm, t0, t0 + 128, g0, g0 + gw))
                        continue
                    cs3 = bass.AP(b.rope, 0, [[1024, 128], [32, 8], [1, 32]]) \
                        if rope == "half" else cos
                    sn3 = bass.AP(b.rope, 256, [[1024, 128], [32, 8], [1, 32]]) \
                        if rope == "half" else sin
                    r0 = b.rtmp[:, 0:256]
                    r1 = b.rtmp[:, 256:512]
                    r2 = b.rtmp[:, 512:768]
                    r3 = b.rtmp[:, 768:1024]
                    rkey = (b.rtmp, "a")
                    p.emit("dve", lambda e, a=r0, x=qe, c=cs3: e.tensor_mul(a, x, c),
                           reads=[(ps, "a"), (b.rope, "a")], writes=[rkey])
                    p.emit("dve", lambda e, a=r1, x=qo, c=sn3: e.tensor_mul(a, x, c),
                           reads=[(ps, "a"), (b.rope, "a")], writes=[rkey])
                    p.emit("dve", lambda e, a=r2, x=qe, c=sn3: e.tensor_mul(a, x, c),
                           reads=[(ps, "a"), (b.rope, "a")], writes=[rkey])
                    p.emit("dve", lambda e, a=r3, x=qo, c=cs3: e.tensor_mul(a, x, c),
                           reads=[(ps, "a"), (b.rope, "a")], writes=[rkey])
                    if rmode == "muls":
                        p.emit("act", lambda e, o=osl, i=ps, gw=gw: e.activation(
                                   out=o, in_=i[:, 0:gw], func=AF.Copy),
                               reads=[(ps, "a"), rkey], writes=[okey])
                    else:
                        p.emit("dve", lambda e, o=oe, a=r0, c=r1: e.tensor_sub(o, a, c),
                               reads=[rkey], writes=[okey])
                        p.emit("dve", lambda e, o=oo, a=r2, c=r3: e.tensor_add(o, a, c),
                               reads=[rkey], writes=[okey])
                if out_tm is not None:
                    p.dma("qact", out_tm.ap()[t0:t0 + 128, g0:g0 + gw],
                          b.osb[:, ob * 512: ob * 512 + gw], reads=[okey],
                          writes=dk(out_tm, t0, t0 + 128, g0, g0 + gw))
                if out_fm is not None:
                    for fsub in range(gw // 128):
                        tp = b.pst[:, 0:128]
                        p.emit("pe", lambda e, o=tp, i=b.osb[:, ob * 512 + fsub * 128:
                                                             ob * 512 + fsub * 128 + 128]:
                                   e.transpose(o, i, b.idb[:, :]),
                               reads=[okey, (b.idb, "a")], writes=[(b.pst, "t")])
                        ok2 = (b.osb, ("o", 2 + fsub % 2))
                        od = b.osb[:, 1024 + (fsub % 2) * 512: 1024 + (fsub % 2) * 512 + 128]
                        p.emit("dve", lambda e, o=od, i=tp: e.tensor_copy(o, i),
                               reads=[(b.pst, "t")], writes=[ok2])
                        p.dma("qact", out_fm.ap()[g0 + fsub * 128: g0 + fsub * 128 + 128,
                                                  t0:t0 + 128],
                              od, reads=[ok2],
                              writes=dk(out_fm, g0 + fsub * 128, g0 + fsub * 128 + 128,
                                        t0, t0 + 128))


def addpass(p, b, out_dram, a_dram, b_dram, nrow=1024):
    for kc in range(nrow // 128):
        for tt in range(NTT):
            asl = b.S[:, (tt % 4) * 512:(tt % 4) * 512 + 512]
            bsl = b.S[:, 2048 + (tt % 4) * 512: 2048 + (tt % 4) * 512 + 512]
            p.dma("qsp", asl, a_dram.ap()[kc * 128:(kc + 1) * 128, tt * TTW:(tt + 1) * TTW],
                  reads=dk(a_dram, kc * 128, (kc + 1) * 128, tt * TTW, (tt + 1) * TTW),
                  writes=[(b.S, ("g", tt % 4))])
            p.dma("qsp", bsl, b_dram.ap()[kc * 128:(kc + 1) * 128, tt * TTW:(tt + 1) * TTW],
                  reads=dk(b_dram, kc * 128, (kc + 1) * 128, tt * TTW, (tt + 1) * TTW),
                  writes=[(b.S, ("g", 4 + tt % 4))])
            ob = tt % 4
            osl = b.osf[:, ob * 512:(ob + 1) * 512]
            p.emit("dve", lambda e, o=osl, x=asl, y=bsl: e.tensor_add(o, x, y),
                   reads=[(b.S, ("g", tt % 4)), (b.S, ("g", 4 + tt % 4))],
                   writes=[(b.osf, ("o", ob))])
            p.dma("qact", out_dram.ap()[kc * 128:(kc + 1) * 128, tt * TTW:(tt + 1) * TTW],
                  osl, reads=[(b.osf, ("o", ob))],
                  writes=dk(out_dram, kc * 128, (kc + 1) * 128, tt * TTW, (tt + 1) * TTW))



def _bc(b, col, n):
    return bass.AP(b.stash, col, [[640, 128], [0, n]])


def routing(p, b, t, *, x_bf, wg_dram, E, variant, bias_row=None,
            capacity=None, coef_dram=None, cc_idx=0, cc_in=None, cc_out=None,
            tag="rt"):
    wrows = t["wrows"]
    nkc = 8
    for kc in range(nkc):
        p.dma("qsp", b.wstg[:, 0:E], wg_dram.ap()[kc * 128:(kc + 1) * 128, 0:E],
              writes=[(b.wstg, "a")])
        p.emit("dve", lambda e, kc=kc: e.tensor_copy(b.wgt[:, kc * E:(kc + 1) * E],
                                                     b.wstg[:, 0:E]),
               reads=[(b.wstg, "a")], writes=[(b.wgt, "a")])
        if coef_dram is not None:
            p.dma("qsp", b.wstg[:, 0:2], coef_dram.ap()[kc * 128:(kc + 1) * 128, 0:2],
                  writes=[(b.wstg, "a")])
            p.emit("dve", lambda e, kc=kc: e.tensor_copy(b.wgt[:, 64 + kc * 2:64 + kc * 2 + 2],
                                                         b.wstg[:, 0:2]),
                   reads=[(b.wstg, "a")], writes=[(b.wgt, "c")])
    if bias_row is not None:
        p.dma("qsp", b.rows[0:1, 1792:1792 + E], bias_row.ap()[0:1, 0:E],
              writes=[("rows", "gb")])
        p.emit("dve", lambda e: e.tensor_copy(b.rowbf[0:1, 512:512 + E],
                                              b.rows[0:1, 1792:1792 + E]),
               reads=[("rows", "gb")], writes=[(b.rowbf, "gb")])
    cap_mode = capacity is not None
    if cap_mode:
        p.emit("dve", lambda e: e.memset(b.rows[0:1, 1536:1544], 0.0),
               writes=[("rows", "cnt")])
    if E < 8:
        p.emit("dve", lambda e: e.memset(b.stash[:, 320 + E:328], -3.0e38),
               writes=[(b.stash, "lg")])
    for tt in range(NTT):
        for kc in range(nkc):
            src = x_bf.ap()[kc * 128:(kc + 1) * 128, tt * TTW:(tt + 1) * TTW]
            p.dma("qsp", b.XB[:, kc * TTW:(kc + 1) * TTW], src,
                  reads=dk(x_bf, kc * 128, (kc + 1) * 128, tt * TTW, (tt + 1) * TTW),
                  writes=[(b.XB, ("g", kc))])
        for sub in range(4):
            t32 = tt * 4 + sub
            pg = b.ps["pg"]
            if bias_row is not None:
                p.emit("pe", lambda e, E=E: e.matmul(pg[:, 0:E], b.ones_rowb[0:1, :],
                                                     b.rowbf[0:1, 512:512 + E],
                                                     start=True, stop=False),
                       reads=[(b.rowbf, "gb"), (b.ones_rowb, "a")], writes=[(pg, "a")])
            for kc in range(nkc):
                lhsT = b.XB[:, kc * TTW + sub * 128: kc * TTW + sub * 128 + 128]
                p.emit("pe", lambda e, l=lhsT, kc=kc, E=E,
                              s=(kc == 0 and bias_row is None), st=(kc == nkc - 1):
                           e.matmul(pg[:, 0:E], l, b.wgt[:, kc * E:(kc + 1) * E],
                                    start=s, stop=st),
                       reads=[(b.XB, ("g", kc)), (b.wgt, "a")], writes=[(pg, "a")])
            lg = b.stash[:, 320:328]
            p.emit("dve", lambda e, E=E: e.tensor_copy(lg[:, 0:E], pg[:, 0:E]),
                   reads=[(pg, "a")], writes=[(b.stash, "lg")])
            if coef_dram is not None:
                pc = b.ps["pc"]
                for kc in range(nkc):
                    lhsT = b.XB[:, kc * TTW + sub * 128: kc * TTW + sub * 128 + 128]
                    p.emit("pe", lambda e, l=lhsT, kc=kc:
                               e.matmul(pc[:, 0:2], l, b.wgt[:, 64 + kc * 2:64 + kc * 2 + 2],
                                        start=(kc == 0), stop=(kc == nkc - 1)),
                           reads=[(b.XB, ("g", kc)), (b.wgt, "c")], writes=[(pc, "a")])
                cl = b.stash[:, 336:340]
                p.emit("dve", lambda e: e.tensor_copy(cl[:, 3:4], pc[:, 1:2]),
                       reads=[(pc, "a")], writes=[(b.stash, "clt")])
                p.emit("dve", lambda e: e.tensor_sub(cl[:, 0:1], pc[:, 0:1], cl[:, 3:4]),
                       reads=[(pc, "a"), (b.stash, "clt")], writes=[(b.stash, "cl")])
                p.emit("act", lambda e: e.activation(out=cl[:, 1:2], in_=cl[:, 0:1],
                                                     func=AF.Sigmoid),
                       reads=[(b.stash, "cl")], writes=[(b.stash, "cl2")])
                p.emit("dve", lambda e, t32=t32: e.tensor_scalar(
                           b.stash[:, 288 + 0:289] if False else b.stash[:, 416 + t32 // 4 * 0:417] if False else cl[:, 2:3],
                           cl[:, 1:2], -1.0, 1.0, OP.mult, OP.add),
                       reads=[(b.stash, "cl2")], writes=[(b.stash, "cl")])
                p.emit("dve", lambda e, t32=t32: e.tensor_copy(
                           b.stash[:, 448 + t32:449 + t32], cl[:, 1:2]),
                       reads=[(b.stash, "cl2")], writes=[(b.stash, ("c1", t32))])
                p.emit("dve", lambda e, t32=t32: e.tensor_copy(
                           b.stash[:, 480 + t32 * 0 + 0:1] if False else b.stash[:, 512 + t32:513 + t32], cl[:, 2:3]),
                       reads=[(b.stash, "cl")], writes=[(b.stash, ("c2", t32))])
            m8 = b.stash[:, 328:336]
            p.emit("dve", lambda e: e.max(m8, lg[:, 0:8]),
                   reads=[(b.stash, "lg")], writes=[(b.stash, "m8")])
            ohsl = b.stash[:, t32 * 8:(t32 + 1) * 8]
            p.emit("dve", lambda e, o=ohsl, E=E: e.tensor_tensor(
                       o[:, 0:E], lg[:, 0:E], _bc(b, 328, E), OP.is_equal),
                   reads=[(b.stash, "lg"), (b.stash, "m8")],
                   writes=[(b.stash, ("oh", t32))])
            import os as _os
            if _os.environ.get("RT_DBG") and t32 == 5:
                p.dma("qact", t["rtdbg"].ap()[:, 0:640], b.stash[:, 0:640],
                      reads=[(b.stash, sk) for sk in
                             [("oh", 5), "lg", "m8", "wt"]],
                      writes=dk(t["rtdbg"], 0, 128, 0, 640))
            if variant == "top1":
                wt = b.stash[:, 424:436]
                p.emit("dve", lambda e, o=wt, i=ohsl, E=E: e.tensor_copy(o[:, 0:E], i[:, 0:E]),
                       reads=[(b.stash, ("oh", t32))], writes=[(b.stash, "wt")])
                _route_transpose(p, b, t32, E, 0, wrows)
            elif variant == "top2":
                sc = b.stash[:, 352:368]
                p.emit("dve", lambda e: e.tensor_sub(sc[:, 9:10], m8[:, 1:2], m8[:, 0:1]),
                       reads=[(b.stash, "m8")], writes=[(b.stash, "d21")])
                p.emit("act", lambda e: e.activation(out=sc[:, 10:11], in_=sc[:, 9:10],
                                                     func=AF.Exp),
                       reads=[(b.stash, "d21")], writes=[(b.stash, "e21")])
                p.emit("dve", lambda e: e.tensor_scalar(sc[:, 11:12], sc[:, 10:11],
                                                        1.0, None, OP.add),
                       reads=[(b.stash, "e21")], writes=[(b.stash, "s21")])
                p.emit("dve", lambda e: e.reciprocal(sc[:, 12:13], sc[:, 11:12]),
                       reads=[(b.stash, "s21")], writes=[(b.stash, "w1")])
                p.emit("dve", lambda e: e.tensor_scalar(sc[:, 13:14], sc[:, 12:13],
                                                        -1.0, 1.0, OP.mult, OP.add),
                       reads=[(b.stash, "w1")], writes=[(b.stash, "w2")])
                wt = b.stash[:, 424:436]
                p.emit("dve", lambda e, o=wt, i=ohsl, E=E: e.tensor_tensor(
                           o[:, 0:E], i[:, 0:E], _bc(b, 364, E), OP.mult),
                       reads=[(b.stash, ("oh", t32)), (b.stash, "w1")],
                       writes=[(b.stash, "wt")])
                oh2 = b.stash[:, 368:376]
                p.emit("dve", lambda e, o=oh2, E=E: e.tensor_tensor(
                           o[:, 0:E], lg[:, 0:E], _bc(b, 329, E), OP.is_equal),
                       reads=[(b.stash, "lg"), (b.stash, "m8")],
                       writes=[(b.stash, "oh2")])
                p.emit("dve", lambda e, o=oh2, E=E: e.tensor_tensor(
                           o[:, 0:E], o[:, 0:E], _bc(b, 365, E), OP.mult),
                       reads=[(b.stash, "oh2"), (b.stash, "w2")],
                       writes=[(b.stash, "oh2")])
                p.emit("dve", lambda e, o=wt, i=oh2, E=E: e.tensor_add(
                           o[:, 0:E], o[:, 0:E], i[:, 0:E]),
                       reads=[(b.stash, "oh2"), (b.stash, "wt")], writes=[(b.stash, "wt")])
                _route_transpose(p, b, t32, E, 0, wrows)
            else:
                neg = b.stash[:, 360:361]
                p.emit("dve", lambda e: e.tensor_scalar(neg, m8[:, 0:1], -1.0, None, OP.mult),
                       reads=[(b.stash, "m8")], writes=[(b.stash, "neg")])
                e8 = b.stash[:, 368:376]
                p.emit("act", lambda e, E=E: e.activation(out=e8[:, 0:E], in_=lg[:, 0:E],
                                                          func=AF.Exp, bias=neg),
                       reads=[(b.stash, "lg"), (b.stash, "neg")], writes=[(b.stash, "e8")])
                ssum = b.stash[:, 361:362]
                p.emit("dve", lambda e, E=E: e.reduce_sum(ssum, e8[:, 0:E], axis=AX.X),
                       reads=[(b.stash, "e8")], writes=[(b.stash, "ss")])
                rr = b.stash[:, 362:363]
                p.emit("dve", lambda e: e.reciprocal(rr, ssum),
                       reads=[(b.stash, "ss")], writes=[(b.stash, "rr")])
                if variant == "top1cap":
                    tmp8 = b.stash[:, 376:384]
                    p.emit("dve", lambda e, i=ohsl, E=E: e.tensor_mul(
                               tmp8[:, 0:E], e8[:, 0:E], i[:, 0:E]),
                           reads=[(b.stash, "e8"), (b.stash, ("oh", t32))],
                           writes=[(b.stash, "t8")])
                    p.emit("dve", lambda e, E=E: e.reduce_sum(b.stash[:, 363:364],
                                                              tmp8[:, 0:E], axis=AX.X),
                           reads=[(b.stash, "t8")], writes=[(b.stash, "g1s")])
                    p.emit("dve", lambda e, t32=t32: e.tensor_mul(
                               b.stash[:, 256 + t32:257 + t32], b.stash[:, 363:364], rr),
                           reads=[(b.stash, "g1s"), (b.stash, "rr")],
                           sreads=[(b.stash, "rr")],
                           writes=[(b.stash, ("g1", t32))])
                else:
                    p.emit("dve", lambda e, t32=t32: e.tensor_copy(
                               b.stash[:, 256 + t32:257 + t32], rr),
                           reads=[(b.stash, "rr")], writes=[(b.stash, ("g1", t32))])
            if cap_mode:
                pr = b.ps["pr"]
                ohf = ohsl
                p.emit("pe", lambda e, i=ohf, E=E: e.matmul(pr[0:1, 0:E],
                                                            b.ones_redf[:, 0:1], i[:, 0:E],
                                                            start=True, stop=True),
                       reads=[(b.stash, ("oh", t32)), (b.ones_redf, "a")],
                       writes=[(pr, "a")])
                p.emit("dve", lambda e, E=E: e.tensor_add(b.rows[0:1, 1536:1536 + E],
                                                          b.rows[0:1, 1536:1536 + E],
                                                          pr[0:1, 0:E]),
                       reads=[(pr, "a"), ("rows", "cnt")], writes=[("rows", "cnt")])
    if not cap_mode:
        return
    p.emit("dve", lambda e, E=E: e.tensor_copy(b.osf[0:1, 0:E], b.rows[0:1, 1536:1536 + E]),
           reads=[("rows", "cnt")], writes=[(b.osf, ("o", 0))])
    p.dma("qact", cc_in.ap()[0:1, 0:E], b.osf[0:1, 0:E], reads=[(b.osf, ("o", 0))],
          writes=[(cc_in.name, "a")])
    ccs = "cc0" if cc_idx == 0 else "cc1"
    pp = p
    ccin_tok = p.writers.get(cc_in.name, {}).get("a")
    def cc_full(e):
        if ccin_tok:
            e.wait_ge(pp.sems[ccin_tok[0]], ccin_tok[1])
        e.collective_compute(
            "AllGather", OP.bypass, replica_groups=[list(range(8))],
            ins=[cc_in.ap()[0:1, 0:E]],
            outs=[cc_out.ap()[0:8, 0:E]],
        ).then_inc(pp.sems[ccs], 1)
    p.q["pool"].append(cc_full)
    p.writers.setdefault(cc_out.name, {})["a"] = (ccs, 1)
    p.count[ccs] = 1
    p.dma("qsp", b.rows[0:1, 1600:1600 + 8 * E], cc_out.ap()[0:8, 0:E],
          reads=[(cc_out.name, "a")], writes=[("rows", "allg")])
    p.emit("dve", lambda e, E=E: e.tensor_mul(b.rows[0:1, 1664:1664 + 8 * E],
                                              b.rows[0:1, 1600:1600 + 8 * E],
                                              b.coremask[0:1, 0:8 * E]),
           reads=[("rows", "allg"), (b.coremask, "a")], writes=[("rows", "mskd")])
    p.emit("dve", lambda e, E=E: e.tensor_reduce(
               b.rows[0:1, 1728:1728 + E],
               bass.AP(b.rows, 1664, [[2048, 1], [1, E], [E, 8]]),
               axis=AX.X, op=OP.add),
           reads=[("rows", "mskd")], writes=[("rows", "offs")])
    p.emit("dve", lambda e, E=E: e.tensor_copy(b.rows[0:1, 1736:1736 + E],
                                               b.rows[0:1, 1728:1728 + E]),
           reads=[("rows", "offs")], writes=[("rows", "run")])
    for tt in range(NTT):
        for sub in range(4):
            t32 = tt * 4 + sub
            ohsl = b.stash[:, t32 * 8:(t32 + 1) * 8]
            pg = b.ps["pg"]
            p.emit("pe", lambda e, i=ohsl, E=E: e.matmul(pg[:, 0:E], b.sut[:, :],
                                                         i[:, 0:E], start=True, stop=False),
                   reads=[(b.stash, ("oh", t32)), (b.sut, "a")], writes=[(pg, "a")])
            p.emit("pe", lambda e, E=E: e.matmul(pg[:, 0:E], b.ones_row[0:1, :],
                                                 b.rows[0:1, 1736:1736 + E],
                                                 start=False, stop=True),
                   reads=[("rows", "run"), (b.ones_row, "a")], writes=[(pg, "a")])
            pr = b.ps["pr"]
            p.emit("pe", lambda e, i=ohsl, E=E: e.matmul(pr[0:1, 0:E], b.ones_redf[:, 0:1],
                                                         i[:, 0:E], start=True, stop=True),
                   reads=[(b.stash, ("oh", t32)), (b.ones_redf, "a")], writes=[(pr, "a")])
            p.emit("dve", lambda e, E=E: e.tensor_add(b.rows[0:1, 1736:1736 + E],
                                                      b.rows[0:1, 1736:1736 + E],
                                                      pr[0:1, 0:E]),
                   reads=[(pr, "a"), ("rows", "run")], writes=[("rows", "run")])
            sc = b.stash[:, 376:384]
            p.emit("dve", lambda e, i=ohsl, E=E: e.tensor_mul(sc[:, 0:E], pg[:, 0:E],
                                                              i[:, 0:E]),
                   reads=[(pg, "a"), (b.stash, ("oh", t32))], writes=[(b.stash, "po8")])
            p.emit("dve", lambda e, E=E: e.reduce_sum(b.stash[:, 364:365], sc[:, 0:E],
                                                      axis=AX.X),
                   reads=[(b.stash, "po8")], writes=[(b.stash, "pos")])
            p.emit("dve", lambda e, cap=capacity: e.tensor_scalar(
                       b.stash[:, 365:366], b.stash[:, 364:365], float(cap), None, OP.is_lt),
                   reads=[(b.stash, "pos")], writes=[(b.stash, "kept")])
            wt = b.stash[:, 424:436]
            if variant == "top1cap":
                p.emit("dve", lambda e, t32=t32: e.tensor_mul(
                           b.stash[:, 366:367], b.stash[:, 256 + t32:257 + t32],
                           b.stash[:, 365:366]),
                       reads=[(b.stash, ("g1", t32)), (b.stash, "kept")],
                       writes=[(b.stash, "wv")])
                if coef_dram is not None:
                    p.emit("dve", lambda e, t32=t32: e.tensor_mul(
                               b.stash[:, 366:367], b.stash[:, 366:367],
                               b.stash[:, 512 + t32:513 + t32]),
                           reads=[(b.stash, ("c2", t32)), (b.stash, "wv")],
                           writes=[(b.stash, "wv")])
                p.emit("dve", lambda e, i=ohsl, E=E: e.tensor_tensor(
                           wt[:, 0:E], i[:, 0:E], _bc(b, 366, E), OP.mult),
                       reads=[(b.stash, ("oh", t32)), (b.stash, "wv")],
                       writes=[(b.stash, "wt")])
                if coef_dram is not None:
                    p.emit("dve", lambda e, t32=t32: e.tensor_copy(
                               b.stash[:, 416:417], b.stash[:, 448 + t32:449 + t32]),
                           reads=[(b.stash, ("c1", t32))], writes=[(b.stash, "xtr")])
                    _route_transpose(p, b, t32, E, 1, wrows)
                else:
                    _route_transpose(p, b, t32, E, 0, wrows)
            else:
                p.emit("dve", lambda e, t32=t32: e.tensor_mul(
                           b.stash[:, 366:367], b.stash[:, 256 + t32:257 + t32],
                           b.stash[:, 365:366]),
                       reads=[(b.stash, ("g1", t32)), (b.stash, "kept")],
                       writes=[(b.stash, "wv")])
                p.emit("dve", lambda e, i=ohsl, E=E: e.tensor_tensor(
                           wt[:, 0:E], i[:, 0:E], _bc(b, 366, E), OP.mult),
                       reads=[(b.stash, ("oh", t32)), (b.stash, "wv")],
                       writes=[(b.stash, "wt")])
                p.emit("dve", lambda e: e.tensor_scalar(
                           b.stash[:, 367:368], b.stash[:, 365:366], -1.0, 1.0,
                           OP.mult, OP.add),
                       reads=[(b.stash, "kept")], writes=[(b.stash, "ikept")])
                p.emit("dve", lambda e, t32=t32: e.tensor_mul(
                           b.stash[:, 416:417], b.stash[:, 256 + t32:257 + t32],
                           b.stash[:, 367:368]),
                       reads=[(b.stash, ("g1", t32)), (b.stash, "ikept")],
                       writes=[(b.stash, "xtr")])
                _route_transpose(p, b, t32, E, 1, wrows)


def _route_transpose(p, b, t32, E, extra, wrows=None):
    wt = b.stash[:, 424:436]
    ncol = E + extra
    if extra:
        p.emit("dve", lambda e: e.tensor_copy(wt[:, E:E + 1], b.stash[:, 416:417]),
               reads=[(b.stash, "xtr")], writes=[(b.stash, "wt")])
    pc = b.ps["pc"]
    p.emit("pe", lambda e, n=ncol: e.transpose(pc[0:n, 0:128], wt[:, 0:n], b.idf[:, :]),
           reads=[(b.stash, "wt"), (b.idf, "a")], writes=[(pc, "a")])
    p.emit("dve", lambda e, n=ncol, t32=t32: e.tensor_copy(
               b.wfm[0:n, t32 * 128:(t32 + 1) * 128], pc[0:n, 0:128]),
           reads=[(pc, "a")], writes=[(b.wfm, ("w", t32))])
    if wrows is not None:
        p.dma("qact", wrows.ap()[0:ncol, t32 * 128:(t32 + 1) * 128],
              b.wfm[0:ncol, t32 * 128:(t32 + 1) * 128],
              reads=[(b.wfm, ("w", t32))],
              writes=dk(wrows, 0, ncol, t32 * 128, (t32 + 1) * 128))



def moe_dense(p, b, t, *, x_bf, Et, Dh, act, w1_dram, w2_dram, b1_pcol, b2_pk,
              acc_dram, out_dram, final_adds=(), l3f_pass=False, nrows_b=None,
              Din=1024, Dout=1024, tag="moe"):
    nkc = Din // 128
    noc = Dout // 128
    if nrows_b is None:
        nrows_b = Et
    if b2_pk is not None:
        for r in range(nrows_b):
            p.dma("qsp", b.wstg[0:1, 0:Dout], b2_pk.ap()[r:r + 1, 0:Dout],
                  writes=[(b.wstg, "a")])
            p.emit("dve", lambda e: e.tensor_copy(b.rowbf[0:1, 0:Dout],
                                                  b.wstg[0:1, 0:Dout]),
                   reads=[(b.wstg, "a")], writes=[(b.rowbf, "a")])
            p.dma("qsp", b.b2s[r:r + 1, 0:Dout], b.rowbf[0:1, 0:Dout],
                  reads=[(b.rowbf, "a")], writes=[(b.b2s, "a")])
    ngq = (Dh // 1024) if Dh else 1
    W1B = 0
    W2B = 8192
    Y1B = 16384
    steps = [(e, g) for e in range(Et) for g in range(ngq)]
    for si, (e, g) in enumerate(steps):
        first = si == 0
        last = si == len(steps) - 1
        if Dh:
            for kc in range(nkc):
                p.dma("qsp", b.wstg[:, 0:1024],
                      w1_dram.ap()[e * Din + kc * 128: e * Din + (kc + 1) * 128,
                                   g * 1024:(g + 1) * 1024],
                      writes=[(b.wstg, "a")])
                dst = b.U1[:, W1B + kc * 1024: W1B + (kc + 1) * 1024]
                p.emit("dve", lambda e2, s=b.wstg[:, 0:1024], d=dst: e2.tensor_copy(d, s),
                       reads=[(b.wstg, "a")],
                       writes=u1g(W1B + kc * 1024, W1B + (kc + 1) * 1024))
            if b1_pcol is not None:
                p.dma("qsp", b.bias[:, 0:8],
                      b1_pcol.ap()[:, e * (Dh // 128) + g * 8: e * (Dh // 128) + g * 8 + 8],
                      writes=[(b.bias, "a")])
            for ic in range(8):
                p.dma("qsp", b.wstg[:, 0:Dout],
                      w2_dram.ap()[e * Dh + g * 1024 + ic * 128: e * Dh + g * 1024 + (ic + 1) * 128,
                                   0:Dout],
                      writes=[(b.wstg, "a")])
                dst = b.U1[:, W2B + ic * Dout: W2B + (ic + 1) * Dout]
                p.emit("dve", lambda e2, s=b.wstg[:, 0:Dout], d=dst: e2.tensor_copy(d, s),
                       reads=[(b.wstg, "a")],
                       writes=u1g(W2B + ic * Dout, W2B + (ic + 1) * Dout))
        else:
            for kc in range(nkc):
                p.dma("qsp", b.wstg[:, 0:Dout],
                      w2_dram.ap()[e * Din + kc * 128: e * Din + (kc + 1) * 128, 0:Dout],
                      writes=[(b.wstg, "a")])
                dst = b.U1[:, W2B + kc * Dout: W2B + (kc + 1) * Dout]
                p.emit("dve", lambda e2, s=b.wstg[:, 0:Dout], d=dst: e2.tensor_copy(d, s),
                       reads=[(b.wstg, "a")],
                       writes=u1g(W2B + kc * Dout, W2B + (kc + 1) * Dout))
        w1keys = u1g(W1B, W1B + nkc * 1024)
        w2keys = u1g(W2B, W2B + 8 * Dout)
        for tt in range(NTT):
            for kc in range(nkc):
                src = x_bf.ap()[kc * 128:(kc + 1) * 128, tt * TTW:(tt + 1) * TTW]
                p.dma("qsp", b.XB[:, kc * TTW:(kc + 1) * TTW], src,
                      reads=dk(x_bf, kc * 128, (kc + 1) * 128, tt * TTW, (tt + 1) * TTW),
                      writes=[(b.XB, ("g", kc))])
            p.dma("qsp", b.rowbf[0:1, 0:512],
                  t["wrows"].ap()[e:e + 1, tt * TTW:(tt + 1) * TTW],
                  reads=dk(t["wrows"], e, e + 1, tt * TTW, (tt + 1) * TTW),
                  writes=[(b.rowbf, "a")])
            po = b.ps["po"]
            p.emit("pe", lambda e2: e2.matmul(po[:, :], b.ones_rowb[0:1, :],
                                              b.rowbf[0:1, 0:512], start=True, stop=True),
                   reads=[(b.rowbf, "a"), (b.ones_rowb, "a")], writes=[(po, "a")])
            p.emit("dve", lambda e2: e2.tensor_copy(b.bsb[:, :], po[:, :]),
                   reads=[(po, "a")], writes=[(b.bsb, "a")])
            if l3f_pass and first:
                p.dma("qsp", b.rowbf[0:1, 512:1024],
                      t["wrows"].ap()[Et:Et + 1, tt * TTW:(tt + 1) * TTW],
                      reads=dk(t["wrows"], Et, Et + 1, tt * TTW, (tt + 1) * TTW),
                      writes=[(b.rowbf, "p")])
                p.emit("pe", lambda e2: e2.matmul(po[:, :], b.ones_rowb[0:1, :],
                                                  b.rowbf[0:1, 512:1024],
                                                  start=True, stop=True),
                       reads=[(b.rowbf, "p"), (b.ones_rowb, "a")], writes=[(po, "a")])
                p.emit("dve", lambda e2: e2.tensor_copy(b.bsb2[:, :], po[:, :]),
                       reads=[(po, "a")], writes=[(b.bsb2, "a")])
            if Dh:
                for ic in range(8):
                    ps1 = b.ps["pa"] if ic % 2 == 0 else b.ps["pb"]
                    for kc in range(nkc):
                        lhsT = b.U1[:, W1B + kc * 1024 + ic * 128: W1B + kc * 1024 + ic * 128 + 128]
                        rhs = b.XB[:, kc * TTW:(kc + 1) * TTW]
                        p.emit("pe", lambda e2, o=ps1, l=lhsT, r=rhs, s=(kc == 0),
                                      st=(kc == nkc - 1):
                                   e2.matmul(o[:, :], l, r, start=s, stop=st),
                               reads=w1keys + [(b.XB, ("g", kc))], writes=[(ps1, "a")])
                    ysl = b.U1[:, Y1B + ic * 512: Y1B + (ic + 1) * 512]
                    if b1_pcol is not None:
                        p.emit("act", lambda e2, o=ysl, i=ps1, ic=ic: e2.activation(
                                   out=o, in_=i[:, :], func=act, bias=b.bias[:, ic:ic + 1]),
                               reads=[(ps1, "a"), (b.bias, "a")],
                               writes=u1g(Y1B + ic * 512, Y1B + (ic + 1) * 512))
                    else:
                        p.emit("act", lambda e2, o=ysl, i=ps1: e2.activation(
                                   out=o, in_=i[:, :], func=act),
                               reads=[(ps1, "a")],
                               writes=u1g(Y1B + ic * 512, Y1B + (ic + 1) * 512))
                y1keys = u1g(Y1B, Y1B + 4096)
            for oc in range(noc):
                ps2 = b.ps["pc"] if oc % 2 == 0 else b.ps["pr"]
                if Dh:
                    for ic in range(8):
                        lhsT = b.U1[:, W2B + ic * Dout + oc * 128: W2B + ic * Dout + oc * 128 + 128]
                        rhs = b.U1[:, Y1B + ic * 512: Y1B + (ic + 1) * 512]
                        p.emit("pe", lambda e2, o=ps2, l=lhsT, r=rhs, s=(ic == 0),
                                      st=(ic == 7):
                                   e2.matmul(o[:, :], l, r, start=s, stop=st),
                               reads=w2keys + y1keys, writes=[(ps2, "a")])
                else:
                    for kc in range(nkc):
                        lhsT = b.U1[:, W2B + kc * Dout + oc * 128: W2B + kc * Dout + oc * 128 + 128]
                        rhs = b.XB[:, kc * TTW:(kc + 1) * TTW]
                        p.emit("pe", lambda e2, o=ps2, l=lhsT, r=rhs, s=(kc == 0),
                                      st=(kc == nkc - 1):
                                   e2.matmul(o[:, :], l, r, start=s, stop=st),
                               reads=w2keys + [(b.XB, ("g", kc))], writes=[(ps2, "a")])
                ob = oc % 4
                tsl = b.osf[:, ob * 512:(ob + 1) * 512]
                p.emit("dve", lambda e2, o=tsl, i=ps2: e2.tensor_mul(o, i[:, :], b.bsb[:, :]),
                       reads=[(ps2, "a"), (b.bsb, "a")], writes=[(b.osf, ("o", ob))])
                if first:
                    if b2_pk is not None:
                        pg = b.ps["pg"]
                        p.emit("pe", lambda e2, oc=oc, tt=tt, n=nrows_b: e2.matmul(
                                   pg[:, :], b.b2s[0:n, oc * 128:(oc + 1) * 128],
                                   b.wfm[0:n, tt * TTW:(tt + 1) * TTW],
                                   start=True, stop=True),
                               reads=[(b.b2s, "a")] +
                                     [(b.wfm, ("w", tt * 4 + s)) for s in range(4)],
                               writes=[(pg, "a")])
                        p.emit("dve", lambda e2, o=tsl: e2.tensor_add(o, o, pg[:, :]),
                               reads=[(pg, "a"), (b.osf, ("o", ob))],
                               writes=[(b.osf, ("o", ob))])
                    if l3f_pass:
                        xsl = b.XB[:, oc * TTW:(oc + 1) * TTW]
                        psl = b.S[:, 2048 + (ob % 4) * 512: 2048 + (ob % 4) * 512 + 512]
                        p.emit("dve", lambda e2, o=psl, x=xsl: e2.tensor_mul(o, x, b.bsb2[:, :]),
                               reads=[(b.XB, ("g", oc)), (b.bsb2, "a")],
                               writes=[(b.S, ("g", 4 + ob))])
                        p.emit("dve", lambda e2, o=tsl, x2=psl: e2.tensor_add(o, o, x2),
                               reads=[(b.S, ("g", 4 + ob)), (b.osf, ("o", ob))],
                               writes=[(b.osf, ("o", ob))])
                else:
                    asl = b.S[:, 4096 + ob * 512: 4096 + (ob + 1) * 512]
                    p.dma("qsp", asl,
                          acc_dram.ap()[oc * 128:(oc + 1) * 128, tt * TTW:(tt + 1) * TTW],
                          reads=dk(acc_dram, oc * 128, (oc + 1) * 128, tt * TTW,
                                   (tt + 1) * TTW),
                          writes=[(b.S, ("g", 8 + ob))])
                    p.emit("dve", lambda e2, o=tsl, a=asl: e2.tensor_add(o, o, a),
                           reads=[(b.S, ("g", 8 + ob)), (b.osf, ("o", ob))],
                           writes=[(b.osf, ("o", ob))])
                if last:
                    for fi, fd in enumerate(final_adds):
                        fsl = b.S[:, 6144 + ((ob + fi) % 4) * 512: 6144 + ((ob + fi) % 4) * 512 + 512]
                        p.dma("qsp", fsl,
                              fd.ap()[oc * 128:(oc + 1) * 128, tt * TTW:(tt + 1) * TTW],
                              reads=dk(fd, oc * 128, (oc + 1) * 128, tt * TTW, (tt + 1) * TTW),
                              writes=[(b.S, ("g", 12 + (ob + fi) % 4))])
                        p.emit("dve", lambda e2, o=tsl, a=fsl: e2.tensor_add(o, o, a),
                               reads=[(b.S, ("g", 12 + (ob + fi) % 4)), (b.osf, ("o", ob))],
                               writes=[(b.osf, ("o", ob))])
                    tgt = out_dram
                else:
                    tgt = acc_dram
                p.dma("qact", tgt.ap()[oc * 128:(oc + 1) * 128, tt * TTW:(tt + 1) * TTW],
                      tsl, reads=[(b.osf, ("o", ob))],
                      writes=dk(tgt, oc * 128, (oc + 1) * 128, tt * TTW, (tt + 1) * TTW))



def ssm(p, b, t, pj2, A_pcol, C_pcol, out_bf):
    p.dma("qsp", b.dec[:, 16:24], A_pcol.ap()[:, 0:8], writes=[(b.dec, "A")])
    p.dma("qsp", b.dec[:, 8:16], C_pcol.ap()[:, 0:8], writes=[(b.dec, "C")])
    p.emit("act", lambda e: e.activation(out=b.dec[:, 0:8], in_=b.dec[:, 16:24],
                                         func=AF.Sigmoid),
           reads=[(b.dec, "A")], writes=[(b.dec, "d")])
    for cc in range(8):
        p.dma("qsp", b.S[:, 0:4096], pj2.ap()[cc * 128:(cc + 1) * 128, 0:T],
              reads=dk(pj2, cc * 128, (cc + 1) * 128, 0, T),
              writes=[(b.S, ("g", gg)) for gg in range(8)])
        dec_ap = bass.AP(b.dec, cc, [[32, 128], [0, T]])
        p.emit("dve", lambda e, d=dec_ap: e.tensor_tensor_scan(
                   b.S[:, 4096:8192], d, b.S[:, 0:4096], 0.0, OP.mult, OP.add),
               reads=[(b.S, ("g", gg)) for gg in range(8)] + [(b.dec, "d")],
               writes=[(b.S, ("g", gg)) for gg in range(8, 16)])
        p.dma("qsp", b.S[:, 0:4096], pj2.ap()[1024 + cc * 128: 1024 + (cc + 1) * 128, 0:T],
              reads=dk(pj2, 1024 + cc * 128, 1024 + (cc + 1) * 128, 0, T),
              writes=[(b.S, ("g", gg)) for gg in range(8)])
        p.emit("act", lambda e: e.activation(out=b.S[:, 0:4096], in_=b.S[:, 0:4096],
                                             func=AF.Sigmoid),
               reads=[(b.S, ("g", gg)) for gg in range(8)],
               writes=[(b.S, ("g", gg)) for gg in range(8)])
        for tt in range(NTT):
            ysl = b.osf[:, (tt % 4) * 512:(tt % 4) * 512 + 512]
            p.emit("dve", lambda e, o=ysl, tt=tt: e.tensor_mul(
                       o, b.S[:, 4096 + tt * 512:4096 + (tt + 1) * 512],
                       b.S[:, tt * 512:(tt + 1) * 512]),
                   reads=[(b.S, ("g", 8 + tt)), (b.S, ("g", tt))],
                   writes=[(b.osf, ("o", tt % 4))])
            obf = b.osb[:, (tt % 4) * 512:(tt % 4) * 512 + 512]
            p.emit("dve", lambda e, o=obf, i=ysl, cc=cc: e.tensor_scalar(
                       o, i, b.dec[:, 8 + cc:9 + cc], None, OP.mult),
                   reads=[(b.osf, ("o", tt % 4)), (b.dec, "C")],
                   writes=[(b.osb, ("o", tt % 4))])
            p.dma("qact", out_bf.ap()[cc * 128:(cc + 1) * 128, tt * TTW:(tt + 1) * TTW],
                  obf, reads=[(b.osb, ("o", tt % 4))],
                  writes=dk(out_bf, cc * 128, (cc + 1) * 128, tt * TTW, (tt + 1) * TTW))



def emit_l0a(p, b, t):
    y, lnb = t["y"], t["lnb"]
    norm_fm(p, b, lnb, y, 1024)
    gemm_tm(p, b, lnb, t["w_l0a_q"], 1024, 1024, out_fm=t["qf"],
            rope="il", rope_tab=t["c_ropeA_q"])
    gemm_tm(p, b, lnb, t["w_l0a_k"], 1024, 1024, out_fm=t["kf"],
            rope="il", rope_tab=t["c_ropeA_k"])
    gemm_tm(p, b, lnb, t["w_l0a_v"], 1024, 1024, out_tm=t["vtm"])
    attention(p, b, out_dram=t["ao"], q_fm=t["qf"], k_fm=t["kf"], v_tm=t["vtm"],
              mode="none")
    gemm_fm(p, b, y, t["ao"], t["w_l0a_o"], 1024, 1024, resid_dram=y)


def emit_l0f(p, b, t):
    y, lnb, lnf, aob = t["y"], t["lnb"], t["lnf"], t["aob"]
    norm_fm(p, b, lnb, y, 1024, out_f32=lnf)
    norm_fm(p, b, aob, lnf, 1024, eps=1e-12, gamma_pcol=t["p_l0f_nw"],
            beta_pcol=t["p_l0f_nb"])
    routing(p, b, t, x_bf=aob, wg_dram=t["w_l0f_g"], E=8, variant="top1cap",
            capacity=4096, coef_dram=t["w_l0f_c"], cc_idx=0,
            cc_in=t["ccin0"], cc_out=t["ccout0"])
    moe_dense(p, b, t, x_bf=aob, Et=9, Dh=4096, act=AF.Gelu_apprx_tanh,
              w1_dram=t["w_l0f_1"], w2_dram=t["w_l0f_2"], b1_pcol=t["p_l0f_b1"],
              b2_pk=t["b_l0f_2"], acc_dram=t["macc"], out_dram=y,
              final_adds=(lnf, y), nrows_b=9)


def emit_dbg_rt(p, b, t):
    y, lnb = t["y"], t["lnb"]
    norm_fm(p, b, lnb, y, 1024)
    routing(p, b, t, x_bf=lnb, wg_dram=t["w_l1f_g"], E=4, variant="top1",
            bias_row=t["r_l1f_gb"])


def emit_l1a(p, b, t):
    y, lnb = t["y"], t["lnb"]
    norm_fm(p, b, lnb, y, 1024)
    gemm_fm(p, b, t["shf"], lnb, t["w_l1a_sh"], 1024, 128,
            bias_pcol=t["p_l1a_shb"])
    gemm_fm(p, b, t["qpf"], lnb, t["w_l1a_q"], 1024, 768,
            bias_pcol=t["p_l1a_bq"], scale=0.125, bias_scale=0.125)
    gemm_fm(p, b, t["kpf"], lnb, t["w_l1a_k"], 1024, 768,
            bias_pcol=t["p_l1a_bk"])
    gemm_tm(p, b, lnb, t["w_l1a_v"], 1024, 1024, out_tm=t["vtm"],
            bias_row=t["r_l1a_bv"])

    def q_prep(h):
        p.dma("qsp", b.qsb[0:16, :], t["shf"].ap()[0:16, :],
              reads=dk(t["shf"], 0, 16, 0, T), writes=[(b.qsb, "hd")])
        p.emit("act", lambda e: e.activation(out=b.qsb[0:16, :], in_=b.qsb[0:16, :],
                                             func=AF.Copy, scale=0.125),
               reads=[(b.qsb, "hd")], writes=[(b.qsb, "hd")])
        p.dma("qsp", b.qsb[16:64, :], t["qpf"].ap()[h * 48:(h + 1) * 48, :],
              reads=dk(t["qpf"], h * 48, (h + 1) * 48, 0, T), writes=[(b.qsb, "hd")])

    def k_prep(h):
        p.dma("qsp", b.kaug[0:16, :], t["shf"].ap()[0:16, :],
              reads=dk(t["shf"], 0, 16, 0, T), writes=[(b.kaug, "hd")])
        p.dma("qsp", b.kaug[16:64, :], t["kpf"].ap()[h * 48:(h + 1) * 48, :],
              reads=dk(t["kpf"], h * 48, (h + 1) * 48, 0, T), writes=[(b.kaug, "hd")])

    attention(p, b, out_dram=t["ao"], v_tm=t["vtm"], mode="causal",
              q_prep=q_prep, k_prep=k_prep)
    gemm_fm(p, b, y, t["ao"], t["w_l1a_o"], 1024, 1024,
            bias_pcol=t["p_l1a_bo"], resid_dram=y)


def emit_l1f(p, b, t):
    y, lnb, u = t["y"], t["lnb"], t["u"]
    norm_fm(p, b, lnb, y, 1024, out_f32=u)
    norm_fm(p, b, t["r1b"], u, 1024, kind="rms", eps=1e-6, gamma_pcol=t["p_l1f_n1"])
    gemm_tm(p, b, t["r1b"], t["w_l1f_q"], 1024, 1024, out_fm=t["qf"],
            rope="half", rope_tab=t["c_ropeB_q"])
    gemm_fm(p, b, t["latr"], t["r1b"], t["w_l1f_dkv"], 1024, 512, out_dtype="f32")
    norm_fm(p, b, t["latb"], t["latr"], 512, eps=1e-5, gamma_pcol=t["p_l1f_lng"],
            beta_pcol=t["p_l1f_lnb"])
    gemm_tm(p, b, t["latb"], t["w_l1f_uk"], 512, 1024, out_fm=t["kf"],
            rope="half", rope_tab=t["c_ropeB_k"])
    gemm_tm(p, b, t["latb"], t["w_l1f_uv"], 512, 1024, out_tm=t["vtm"])
    attention(p, b, out_dram=t["ao"], q_fm=t["qf"], k_fm=t["kf"], v_tm=t["vtm"],
              mode="causal")
    gemm_fm(p, b, u, t["ao"], t["w_l1f_o"], 1024, 1024, resid_dram=u)
    norm_fm(p, b, t["r2b"], u, 1024, kind="rms", eps=1e-6, gamma_pcol=t["p_l1f_n2"])
    routing(p, b, t, x_bf=t["r2b"], wg_dram=t["w_l1f_g"], E=4, variant="top1",
            bias_row=t["r_l1f_gb"])
    moe_dense(p, b, t, x_bf=t["r2b"], Et=4, Dh=4096, act=AF.Relu,
              w1_dram=t["w_l1f_1"], w2_dram=t["w_l1f_2"], b1_pcol=t["p_l1f_b1"],
              b2_pk=t["b_l1f_2"], acc_dram=t["macc"], out_dram=u, final_adds=(u,))
    addpass(p, b, y, y, u)


def emit_l2a(p, b, t):
    y, lnb = t["y"], t["lnb"]
    norm_fm(p, b, lnb, y, 1024)
    gemm_fm(p, b, t["pj2"], lnb, t["w_l2a_in"], 1024, 2048,
            bias_pcol=t["p_l2a_bin"], out_dtype="f32")
    ssm(p, b, t, t["pj2"], t["p_l2a_A"], t["p_l2a_C"], t["sdy"])
    gemm_fm(p, b, y, t["sdy"], t["w_l2a_out"], 1024, 1024,
            bias_pcol=t["p_l2a_bout"], resid_dram=y)


def emit_l2f(p, b, t):
    y, lnb = t["y"], t["lnb"]
    norm_fm(p, b, lnb, y, 1024)
    routing(p, b, t, x_bf=lnb, wg_dram=t["w_l2f_g"], E=8, variant="top2")
    moe_dense(p, b, t, x_bf=lnb, Et=8, Dh=2048, act=AF.Silu,
              w1_dram=t["w_l2f_1"], w2_dram=t["w_l2f_2"], b1_pcol=None,
              b2_pk=None, acc_dram=t["macc"], out_dram=y, final_adds=(y,))


def emit_l3a(p, b, t):
    y, lnb = t["y"], t["lnb"]
    norm_fm(p, b, lnb, y, 1024)
    gemm_fm(p, b, t["qf"], lnb, t["w_l3a_q"], 1024, 1024, scale=0.125)
    gemm_fm(p, b, t["kf"], lnb, t["w_l3a_k"], 1024, 1024)
    gemm_tm(p, b, lnb, t["w_l3a_v"], 1024, 1024, out_tm=t["vtm"])
    attention(p, b, out_dram=t["ao"], q_fm=t["qf"], k_fm=t["kf"], v_tm=t["vtm"],
              mode="causal")
    gemm_fm(p, b, y, t["ao"], t["w_l3a_o"], 1024, 1024, resid_dram=y)


def emit_l3f(p, b, t):
    y, lnb, lnf = t["y"], t["lnb"], t["lnf"]
    norm_fm(p, b, lnb, y, 1024, out_f32=lnf)
    gemm_fm(p, b, t["qf"], lnb, t["w_l3f_q"], 1024, 1024,
            bias_pcol=t["p_l3f_bq"], scale=1.0 / 32.0, bias_scale=1.0 / 32.0)
    gemm_fm(p, b, t["kf"], lnb, t["w_l3f_k"], 1024, 1024, bias_pcol=t["p_l3f_bk"])
    gemm_tm(p, b, lnb, t["w_l3f_v"], 1024, 1024, out_tm=t["vtm"],
            bias_row=t["r_l3f_bv"])
    attention(p, b, out_dram=t["ao"], q_fm=t["qf"], k_fm=t["kf"], v_tm=t["vtm"],
              mode="zmask")
    routing(p, b, t, x_bf=t["ao"], wg_dram=t["w_l3f_g"], E=8, variant="switch",
            bias_row=t["r_l3f_gb"], capacity=4915, cc_idx=1,
            cc_in=t["ccin1"], cc_out=t["ccout1"])
    moe_dense(p, b, t, x_bf=t["ao"], Et=8, Dh=None, act=None,
              w1_dram=None, w2_dram=t["w_l3f_e"], b1_pcol=None,
              b2_pk=t["b_l3f_e"], acc_dram=t["macc"], out_dram=t["x2t"],
              final_adds=(lnf,), l3f_pass=True)
    norm_fm(p, b, lnb, t["x2t"], 1024, eps=1e-5, gamma_pcol=t["p_l3f_g"],
            beta_pcol=t["p_l3f_b"], final_resid=y, final_out=y)


def emit_l0a_v(p, b, t):
    y, lnb = t["y"], t["lnb"]
    norm_fm(p, b, lnb, y, 1024)
    gemm_tm(p, b, lnb, t["w_l0a_v"], 1024, 1024, out_tm=t["vtm"])


def emit_l0a_q(p, b, t):
    y, lnb = t["y"], t["lnb"]
    norm_fm(p, b, lnb, y, 1024)
    gemm_tm(p, b, lnb, t["w_l0a_q"], 1024, 1024, out_fm=t["qf"],
            rope="il", rope_tab=t["c_ropeA_q"])


def emit_dbg_ropetm(p, b, t):
    y, lnb = t["y"], t["lnb"]
    norm_fm(p, b, lnb, y, 1024)
    gemm_tm(p, b, lnb, t["w_l0a_q"], 1024, 1024, out_tm=t["vtm"],
            rope="il", rope_tab=t["c_ropeA_q"])


def emit_dbg_tp(p, b, t):
    y, lnb = t["y"], t["lnb"]
    norm_fm(p, b, lnb, y, 1024)
    gemm_tm(p, b, lnb, t["w_l0a_q"], 1024, 1024, out_fm=t["qf"])


def emit_l0a_qkv(p, b, t):
    y, lnb = t["y"], t["lnb"]
    norm_fm(p, b, lnb, y, 1024)
    gemm_tm(p, b, lnb, t["w_l0a_q"], 1024, 1024, out_fm=t["qf"],
            rope="il", rope_tab=t["c_ropeA_q"])
    gemm_tm(p, b, lnb, t["w_l0a_k"], 1024, 1024, out_fm=t["kf"],
            rope="il", rope_tab=t["c_ropeA_k"])
    gemm_tm(p, b, lnb, t["w_l0a_v"], 1024, 1024, out_tm=t["vtm"])


def emit_l0a_att(p, b, t):
    emit_l0a_qkv(p, b, t)
    attention(p, b, out_dram=t["ao"], q_fm=t["qf"], k_fm=t["kf"], v_tm=t["vtm"],
              mode="none")


BLOCK_EMIT = [("dbg_rt", emit_dbg_rt), ("dbg_ropetm", emit_dbg_ropetm), ("dbg_tp", emit_dbg_tp),
              ("l0a_v", emit_l0a_v), ("l0a_q", emit_l0a_q),
              ("l0a_qkv", emit_l0a_qkv), ("l0a_att", emit_l0a_att),
              ("l0a", emit_l0a), ("l0f", emit_l0f), ("l1a", emit_l1a),
              ("l1f", emit_l1f), ("l2a", emit_l2a), ("l2f", emit_l2f),
              ("l3a", emit_l3a), ("l3f", emit_l3f)]


WEIGHT_SPECS = [
    ("w_l0a_q", [1024, 1024]), ("w_l0a_k", [1024, 1024]),
    ("w_l0a_v", [1024, 1024]), ("w_l0a_o", [1024, 1024]),
    ("c_ropeA_q", [4096, 512]), ("c_ropeA_k", [4096, 512]),
    ("p_l0f_nw", [128, 8]), ("p_l0f_nb", [128, 8]),
    ("w_l0f_g", [1024, 8]), ("w_l0f_c", [1024, 2]),
    ("w_l0f_1", [9 * 1024, 4096]), ("p_l0f_b1", [128, 9 * 32]),
    ("w_l0f_2", [9 * 4096, 1024]), ("b_l0f_2", [9, 1024]),
    ("w_l1a_sh", [1024, 128]), ("p_l1a_shb", [128, 1]),
    ("w_l1a_q", [1024, 768]), ("p_l1a_bq", [128, 6]),
    ("w_l1a_k", [1024, 768]), ("p_l1a_bk", [128, 6]),
    ("w_l1a_v", [1024, 1024]), ("r_l1a_bv", [1, 1024]),
    ("w_l1a_o", [1024, 1024]), ("p_l1a_bo", [128, 8]),
    ("p_l1f_n1", [128, 8]), ("p_l1f_n2", [128, 8]),
    ("w_l1f_q", [1024, 1024]), ("c_ropeB_q", [4096, 512]), ("c_ropeB_k", [4096, 512]),
    ("w_l1f_dkv", [1024, 512]), ("p_l1f_lng", [128, 4]), ("p_l1f_lnb", [128, 4]),
    ("w_l1f_uk", [512, 1024]), ("w_l1f_uv", [512, 1024]),
    ("w_l1f_o", [1024, 1024]),
    ("w_l1f_g", [1024, 4]), ("r_l1f_gb", [1, 4]),
    ("w_l1f_1", [4 * 1024, 4096]), ("p_l1f_b1", [128, 4 * 32]),
    ("w_l1f_2", [4 * 4096, 1024]), ("b_l1f_2", [4, 1024]),
    ("w_l2a_in", [1024, 2048]), ("p_l2a_bin", [128, 16]),
    ("p_l2a_A", [128, 8]), ("p_l2a_C", [128, 8]),
    ("w_l2a_out", [1024, 1024]), ("p_l2a_bout", [128, 8]),
    ("w_l2f_g", [1024, 8]),
    ("w_l2f_1", [8 * 1024, 2048]), ("w_l2f_2", [8 * 2048, 1024]),
    ("w_l3a_q", [1024, 1024]), ("w_l3a_k", [1024, 1024]),
    ("w_l3a_v", [1024, 1024]), ("w_l3a_o", [1024, 1024]),
    ("w_l3f_q", [1024, 1024]), ("p_l3f_bq", [128, 8]),
    ("w_l3f_k", [1024, 1024]), ("p_l3f_bk", [128, 8]),
    ("w_l3f_v", [1024, 1024]), ("r_l3f_bv", [1, 1024]),
    ("w_l3f_g", [1024, 8]), ("r_l3f_gb", [1, 8]),
    ("w_l3f_e", [8 * 1024, 1024]), ("b_l3f_e", [8, 1024]),
    ("p_l3f_g", [128, 8]), ("p_l3f_b", [128, 8]),
]

SCRATCH_SPECS = [
    ("lnb", [1024, T], "bf"), ("lnf", [1024, T], "f32"), ("aob", [1024, T], "bf"),
    ("u", [1024, T], "f32"), ("r1b", [1024, T], "bf"), ("r2b", [1024, T], "bf"),
    ("qf", [1024, T], "bf"), ("kf", [1024, T], "bf"), ("vtm", [T, 1024], "bf"),
    ("ao", [1024, T], "bf"), ("shf", [128, T], "bf"), ("qpf", [768, T], "bf"),
    ("kpf", [768, T], "bf"), ("latr", [512, T], "f32"), ("latb", [512, T], "bf"),
    ("pj2", [2048, T], "f32"), ("sdy", [1024, T], "bf"), ("macc", [1024, T], "f32"),
    ("x2t", [1024, T], "f32"),
    ("wrows", [16, T], "bf"), ("rtdbg", [128, 640], "f32"), ("ccin0", [1, 8], "f32"), ("ccout0", [8, 8], "f32"),
    ("ccin1", [1, 8], "f32"), ("ccout1", [8, 8], "f32"),
]


def build_module(blocks, core_id=0):
    from contextlib import ExitStack
    import os
    nc = bass.Bass("TRN2", target_bir_lowering=False, debug=False,
                   detect_race_conditions=bool(os.environ.get("RACEDET")))
    stack = ExitStack()
    with stack:
        p = P(nc)
        t = {}
        t["xin"] = p.dram_tensor("xin", [1024, T], F32, kind="ExternalInput")
        t["y"] = p.dram_tensor("y", [1024, T], F32, kind="ExternalOutput")
        for name, shape in WEIGHT_SPECS:
            t[name] = p.dram_tensor(name, shape, F32, kind="ExternalInput")
        for name, shape, dt_ in SCRATCH_SPECS:
            t[name] = p.dram_tensor(name, shape, BF16 if dt_ == "bf" else F32,
                                    kind="Internal")
        for name, shape, dtype in CONST_SPECS:
            t[name] = p.dram_tensor(name, shape, dtype, kind="ExternalInput")
        p.finalize_sems(stack)
        b = setup_ctx(p, stack)
        load_consts(p, b, t)
        for kc in range(8):
            p.dma("qsp", t["y"].ap()[kc * 128:(kc + 1) * 128, :],
                  t["xin"].ap()[kc * 128:(kc + 1) * 128, :],
                  writes=dk(t["y"], kc * 128, (kc + 1) * 128, 0, T))
        import os as _os
        dbg_stop = _os.environ.get("DBG_STOP", "")
        for name, fn in BLOCK_EMIT:
            if name in blocks:
                fn(p, b, t)
        dbg = _os.environ.get("DBG_COPY", "")
        if dbg:
            src = t[dbg]
            nr = src.shape[0] if hasattr(src, "shape") else 1024
            ncol = src.shape[1] if hasattr(src, "shape") else T
            nrr = min(128, nr)
            for kc in range(max(1, min(nr // 128, 8))):
                for tt in range(min(NTT, ncol // TTW)):
                    sl = b.osf[0:nrr, (tt % 4) * 512:(tt % 4) * 512 + 512]
                    p.dma("qsp", b.osb[0:nrr, (tt % 4) * 512:(tt % 4) * 512 + 512]
                          if src.dtype == BF16 else sl,
                          src.ap()[kc * 128:kc * 128 + nrr, tt * TTW:(tt + 1) * TTW],
                          reads=dk(src, kc * 128, kc * 128 + nrr, tt * TTW, (tt + 1) * TTW),
                          writes=[(b.osb, ("o", tt % 4)) if src.dtype == BF16
                                  else (b.osf, ("o", tt % 4))])
                    if src.dtype == BF16:
                        p.emit("dve", lambda e, o=sl, i=b.osb[0:nrr, (tt % 4) * 512:(tt % 4) * 512 + 512]:
                                   e.tensor_copy(o, i),
                               reads=[(b.osb, ("o", tt % 4))], writes=[(b.osf, ("o", tt % 4))])
                    p.dma("qact", t["y"].ap()[kc * 128:kc * 128 + nrr, tt * TTW:(tt + 1) * TTW],
                          sl, reads=[(b.osf, ("o", tt % 4))],
                          writes=dk(t["y"], kc * 128, kc * 128 + nrr, tt * TTW, (tt + 1) * TTW))
        p_drain(p)
        p.run_blocks()
    return nc, p



def rope_tables_A():
    pos = np.arange(4096, dtype=np.float64)
    i = np.arange(32, dtype=np.float64)
    coss, sins = [], []
    for s in (1.0, 4.0, 16.0):
        freqs = 1.0 / (10000.0 * s) ** (2.0 * i / 64.0)
        ang = pos[:, None] * freqs[None, :]
        coss.append(np.cos(ang)); sins.append(np.sin(ang))
    cos = np.mean(coss, 0).astype(np.float32)
    sin = np.mean(sins, 0).astype(np.float32)
    cq = np.concatenate([np.tile(cos, (1, 8)) / 8.0, np.tile(sin, (1, 8)) / 8.0], 1)
    ck = np.concatenate([np.tile(cos, (1, 8)), np.tile(sin, (1, 8))], 1)
    return cq.astype(np.float32), ck.astype(np.float32)


def rope_tables_B():
    pos = np.arange(4096, dtype=np.float64)
    half = 32
    inv = 1.0 / 10000.0 ** (np.arange(half, dtype=np.float64) / half)
    ang = pos[:, None] * inv[None, :]
    cos = np.cos(ang).astype(np.float32)
    sin = np.sin(ang).astype(np.float32)
    cq = np.concatenate([np.tile(cos, (1, 8)) / 8.0, np.tile(sin, (1, 8)) / 8.0], 1)
    ck = np.concatenate([np.tile(cos, (1, 8)), np.tile(sin, (1, 8))], 1)
    return cq.astype(np.float32), ck.astype(np.float32)


def host_weights(i):
    w = {}
    f = np.float32
    A = lambda x: np.ascontiguousarray(np.asarray(x, f))
    w["w_l0a_q"] = A(i["L0_attn_Wq"]); w["w_l0a_k"] = A(i["L0_attn_Wk"])
    w["w_l0a_v"] = A(i["L0_attn_Wv"]); w["w_l0a_o"] = A(i["L0_attn_Wo"])
    w["c_ropeA_q"], w["c_ropeA_k"] = rope_tables_A()
    w["p_l0f_nw"] = pcol(i["L0_ffn_attn_nw"]); w["p_l0f_nb"] = pcol(i["L0_ffn_attn_nb"])
    w["w_l0f_g"] = A(i["L0_ffn_wg"]); w["w_l0f_c"] = A(i["L0_ffn_res_coef"])
    w1 = np.concatenate([np.asarray(i["L0_ffn_inter_w"], f),
                         np.asarray(i["L0_ffn_res_inter_w"], f)[None]], 0)
    w["w_l0f_1"] = A(w1.reshape(9 * 1024, 4096))
    b1 = np.concatenate([np.asarray(i["L0_ffn_inter_b"], f),
                         np.asarray(i["L0_ffn_res_inter_b"], f)[None]], 0)
    w["p_l0f_b1"] = np.concatenate([pcol(b1[e]) for e in range(9)], 1)
    w2 = np.concatenate([np.asarray(i["L0_ffn_output_w"], f),
                         np.asarray(i["L0_ffn_res_output_w"], f)[None]], 0)
    w["w_l0f_2"] = A(w2.reshape(9 * 4096, 1024))
    b2 = np.concatenate([np.asarray(i["L0_ffn_output_b"], f),
                         np.asarray(i["L0_ffn_res_output_b"], f)[None]], 0)
    w["b_l0f_2"] = A(b2)
    sh = np.zeros((1024, 128), f); sh[:, 0:16] = np.asarray(i["L1_attn_W_share"], f)
    w["w_l1a_sh"] = sh
    shb = np.zeros(128, f); shb[0:16] = np.asarray(i["L1_attn_b_share"], f)
    w["p_l1a_shb"] = pcol(shb)[:, 0:1]
    w["w_l1a_q"] = A(i["L1_attn_Wq"]); w["p_l1a_bq"] = pcol(i["L1_attn_bq"])
    w["w_l1a_k"] = A(i["L1_attn_Wk"]); w["p_l1a_bk"] = pcol(i["L1_attn_bk"])
    w["w_l1a_v"] = A(i["L1_attn_Wv"]); w["r_l1a_bv"] = A(i["L1_attn_bv"])[None, :]
    w["w_l1a_o"] = A(i["L1_attn_Wo"]); w["p_l1a_bo"] = pcol(i["L1_attn_bo"])
    w["p_l1f_n1"] = pcol(i["L1_ffn_norm1_w"]); w["p_l1f_n2"] = pcol(i["L1_ffn_norm2_w"])
    w["w_l1f_q"] = A(np.asarray(i["L1_ffn_Wq"], f).T)
    w["c_ropeB_q"], w["c_ropeB_k"] = rope_tables_B()
    w["w_l1f_dkv"] = A(np.asarray(i["L1_ffn_Wdkv"], f).T)
    w["p_l1f_lng"] = pcol(i["L1_ffn_ln_g"]); w["p_l1f_lnb"] = pcol(i["L1_ffn_ln_b"])
    w["w_l1f_uk"] = A(np.asarray(i["L1_ffn_Wuk"], f).T)
    w["w_l1f_uv"] = A(np.asarray(i["L1_ffn_Wuv"], f).T)
    w["w_l1f_o"] = A(np.asarray(i["L1_ffn_Wo"], f).T)
    w["w_l1f_g"] = A(np.asarray(i["L1_ffn_gate_w"], f).T)
    w["r_l1f_gb"] = A(i["L1_ffn_gate_b"])[None, :]
    W1 = np.asarray(i["L1_ffn_W1"], f)
    w["w_l1f_1"] = A(W1.transpose(0, 2, 1).reshape(4 * 1024, 4096))
    b1 = np.asarray(i["L1_ffn_b1"], f)
    w["p_l1f_b1"] = np.concatenate([pcol(b1[e]) for e in range(4)], 1)
    W2 = np.asarray(i["L1_ffn_W2"], f)
    w["w_l1f_2"] = A(W2.transpose(0, 2, 1).reshape(4 * 4096, 1024))
    w["b_l1f_2"] = A(i["L1_ffn_b2"])
    w["w_l2a_in"] = A(i["L2_attn_W_in"]); w["p_l2a_bin"] = pcol(i["L2_attn_b_in"])
    w["p_l2a_A"] = pcol(i["L2_attn_A"]); w["p_l2a_C"] = pcol(i["L2_attn_C"])
    w["w_l2a_out"] = A(i["L2_attn_W_out"]); w["p_l2a_bout"] = pcol(i["L2_attn_b_out"])
    w["w_l2f_g"] = A(i["L2_ffn_gate_w"])
    w["w_l2f_1"] = A(np.asarray(i["L2_ffn_w1"], f).reshape(8 * 1024, 2048))
    w["w_l2f_2"] = A(np.asarray(i["L2_ffn_w2"], f).reshape(8 * 2048, 1024))
    w["w_l3a_q"] = A(np.asarray(i["L3_attn_w_q"], f).reshape(1024, 1024))
    w["w_l3a_k"] = A(np.asarray(i["L3_attn_w_k"], f).reshape(1024, 1024))
    w["w_l3a_v"] = A(np.asarray(i["L3_attn_w_v"], f).reshape(1024, 1024))
    w["w_l3a_o"] = A(np.asarray(i["L3_attn_w_o"], f).transpose(1, 2, 0).reshape(1024, 1024))
    w["w_l3f_q"] = A(np.asarray(i["L3_ffn_Wq"], f).transpose(2, 0, 1).reshape(1024, 1024))
    w["p_l3f_bq"] = pcol(np.asarray(i["L3_ffn_bq"], f).reshape(1024))
    w["w_l3f_k"] = A(np.asarray(i["L3_ffn_Wk"], f).transpose(2, 0, 1).reshape(1024, 1024))
    w["p_l3f_bk"] = pcol(np.asarray(i["L3_ffn_bk"], f).reshape(1024))
    w["w_l3f_v"] = A(np.asarray(i["L3_ffn_Wv"], f).transpose(2, 0, 1).reshape(1024, 1024))
    w["r_l3f_bv"] = A(np.asarray(i["L3_ffn_bv"], f).reshape(1024))[None, :]
    w["w_l3f_g"] = A(np.asarray(i["L3_ffn_Wsw"], f).T)
    w["r_l3f_gb"] = A(i["L3_ffn_bsw"])[None, :]
    We = np.asarray(i["L3_ffn_We"], f)
    w["w_l3f_e"] = A(We.transpose(0, 2, 1).reshape(8 * 1024, 1024))
    w["b_l3f_e"] = A(i["L3_ffn_be"])
    w["p_l3f_g"] = pcol(i["L3_ffn_gamma"]); w["p_l3f_b"] = pcol(i["L3_ffn_beta"])
    return w


ALL_BLOCKS = ["l0a", "l0f", "l1a", "l1f", "l2a", "l2f", "l3a", "l3f"]


def kernel(**inputs):
    x = np.asarray(inputs["L0_attn_x"], np.float32)
    w = host_weights(inputs)
    nc, p = build_module(set(ALL_BLOCKS))
    in_maps = []
    for c in range(8):
        m = dict(w)
        m["xin"] = np.ascontiguousarray(x[c].T)
        m.update(make_consts_host(c))
        in_maps.append(m)
    res = run_bass_kernel_spmd(nc, in_maps, core_ids=list(range(8)))
    out = np.stack([np.ascontiguousarray(res.results[c]["y"].T) for c in range(8)], 0)
    return out


# version 02
# speedup vs baseline: 1.0380x; 1.0380x over previous
import math
import numpy as np
import ml_dtypes
import concourse.bass as bass
import concourse.mybir as mybir
from concourse.bass_utils import run_bass_kernel_spmd

F32 = mybir.dt.float32
BF16 = mybir.dt.bfloat16
U32 = mybir.dt.uint32
AF = mybir.ActivationFunctionType
OP = mybir.AluOpType
AX = mybir.AxisListType

T = 4096
D = 1024
NTT = 8
TTW = 512
NKC = 8

GELU_OK = True



class P:
    COMPUTE = ("pe", "act", "dve", "pool")
    SELF_ORDERED = ("pe",)
    QUEUES = ("qsp", "qact", "qpool")
    QUEUE_ENG = {"qsp": "sp", "qact": "act", "qpool": "pool"}

    NCHAIN = 8

    def __init__(self, nc):
        self.nc = nc
        self.q = {e: [] for e in ("pe", "act", "dve", "pool", "sp")}
        self.sems = {}
        self.count = {}
        sem_names = list(self.COMPUTE) + ["cc0", "cc1"]
        for qn in self.QUEUES:
            for c in range(self.NCHAIN):
                sem_names.append(qn + str(c))
        for s in sem_names:
            self.sems[s] = None
            self.count[s] = 0
        self.qi = {qn: 0 for qn in self.QUEUES}
        self.writers = {}
        self.readers = {}
        self.observed = {e: {} for e in ("pe", "act", "dve", "pool", "sp")}
        self.dram = {}
        self.sbuf = {}
        self.n_emitted = 0

    def dram_tensor(self, name, shape, dtype, kind="Internal"):
        t = self.nc.dram_tensor(name, list(shape), dtype, kind=kind)
        self.dram[name] = t
        return t

    def sbuf_tensor(self, name, shape, dtype):
        t = self.nc.alloc_sbuf_tensor(name, list(shape), dtype)
        self.sbuf[name] = t
        return t

    def _tok(self, sem_name):
        return (sem_name, self.count[sem_name])

    def _need_wait(self, eng, tok, my_chain=None):
        sem_name, val = tok
        if val <= 0:
            return None
        if sem_name == eng and eng in self.SELF_ORDERED:
            return None
        if my_chain is not None and sem_name == my_chain:
            return None
        got = self.observed[eng].get(sem_name, 0)
        if got >= val:
            return None
        return (sem_name, val)

    def emit(self, eng, build, reads=(), writes=(), queue=None, tag="", sreads=()):
        waits = {}
        my_chain = None
        if queue is not None:
            i = self.qi[queue]
            self.qi[queue] = i + 1
            my_chain = queue + str(i % self.NCHAIN)
        def add_tok(tok):
            w = self._need_wait(eng, tok, my_chain=my_chain)
            if w:
                waits[w[0]] = max(waits.get(w[0], 0), w[1])
        rs = [(self._key(b), sk) for (b, sk) in reads]
        srs = [(self._key(b), sk) for (b, sk) in sreads]
        ws = [(self._key(b), sk) for (b, sk) in writes]
        for (b, sk) in rs:
            for tok in self._write_toks(b, sk):
                add_tok(tok)
        for (b, sk) in srs:
            for tok in self._write_toks(b, sk):
                sem_name, val = tok
                if val <= 0:
                    continue
                if my_chain is not None and sem_name == my_chain:
                    continue
                got = self.observed[eng].get(sem_name, 0)
                if got < val:
                    waits[sem_name] = max(waits.get(sem_name, 0), val)
        rs = rs + srs
        for (b, sk) in ws:
            for tok in self._write_toks(b, sk):
                add_tok(tok)
            for tok in self._read_toks(b, sk):
                add_tok(tok)
        sem_name = my_chain if queue else eng
        step = 16 if queue else 1
        if queue is not None and self.count[sem_name] > 0:
            waits[sem_name] = max(waits.get(sem_name, 0), self.count[sem_name])
        self.count[sem_name] += step
        val_after = self.count[sem_name]
        for (b, sk) in ws:
            d = self.writers.setdefault(b, {})
            if sk == "a":
                d.clear()
            d[sk] = (sem_name, val_after)
            rd = self.readers.setdefault(b, {})
            if sk == "a":
                rd.clear()
            else:
                rd.pop(sk, None)
        for (b, sk) in rs:
            d = self.readers.setdefault(b, {}).setdefault(sk, {})
            d[sem_name] = val_after if False else max(d.get(sem_name, 0), val_after)
        wl = sorted(waits.items())
        obs = self.observed[eng]
        for s, v in wl:
            obs[s] = max(obs.get(s, 0), v)
        pp = self
        def thunk(engine_obj):
            for s, v in wl:
                engine_obj.wait_ge(pp.sems[s], v)
            inst = build(engine_obj)
            inst.then_inc(pp.sems[sem_name], step)
        self.q[eng].append(thunk)
        self.n_emitted += 1

    def _key(self, buf):
        if isinstance(buf, str):
            return buf
        return getattr(buf, "name", None) or str(id(buf))

    def _write_toks(self, b, sk):
        d = self.writers.get(b)
        if not d:
            return []
        if sk == "a":
            return list(d.values())
        toks = []
        if "a" in d:
            toks.append(d["a"])
        if sk in d:
            toks.append(d[sk])
        return toks

    def _read_toks(self, b, sk):
        d = self.readers.get(b)
        if not d:
            return []
        keys = list(d.keys()) if sk == "a" else (["a", sk] if "a" in d else [sk])
        toks = []
        for k in keys:
            if k in d:
                for s, v in d[k].items():
                    toks.append((s, v))
        return toks

    def dma(self, queue, out, in_, reads=(), writes=(), **kw):
        eng = self.QUEUE_ENG[queue]
        self.emit(eng, lambda e: e.dma_start(out=out, in_=in_, **kw),
                  reads=reads, writes=writes, queue=queue)

    def finalize_sems(self, stack):
        for s in list(self.sems.keys()):
            self.sems[s] = stack.enter_context(self.nc.semaphore("sem_" + s))

    def run_blocks(self):
        nc = self.nc
        with nc.Block() as block:
            @block.tensor
            def _(e):
                for f in self.q["pe"]:
                    f(e)

            @block.scalar
            def _(e):
                for f in self.q["act"]:
                    f(e)

            @block.vector
            def _(e):
                for f in self.q["dve"]:
                    f(e)

            @block.gpsimd
            def _(e):
                for f in self.q["pool"]:
                    f(e)

            @block.sync
            def _(e):
                for f in self.q["sp"]:
                    f(e)



class Ctx:
    pass


def setup_ctx(p, stack):
    nc = p.nc
    b = Ctx()
    b.U1 = p.sbuf_tensor("U1", [128, 33280], BF16)
    b.vaug = lambda: b.U1
    b.WSB_OFF = 0
    b.Y1_OFF = 16384
    b.S = p.sbuf_tensor("Sf", [128, 8192], F32)
    b.XB = p.sbuf_tensor("XB", [128, 8192], BF16)
    b.qsb = p.sbuf_tensor("qsb", [128, 4096], BF16)
    b.kaug = p.sbuf_tensor("kaug", [128, 4096], BF16)
    b.pt = p.sbuf_tensor("pt", [128, 1536], BF16)
    b.ptc = p.sbuf_tensor("ptc", [128, 512], BF16)
    b.vfut = p.sbuf_tensor("vfut", [128, 1040], BF16)
    b.attsq = p.sbuf_tensor("attsq", [128, 512], BF16)
    b.bsb = p.sbuf_tensor("bsb", [128, 512], F32)
    b.bsb2 = p.sbuf_tensor("bsb2", [128, 512], F32)
    b.osb = p.sbuf_tensor("osb", [128, 2048], BF16)
    b.osf = p.sbuf_tensor("osf", [128, 2048], F32)
    b.wstg = p.sbuf_tensor("wstg", [128, 2048], F32)
    b.wstg_i = [0]
    b.bias = p.sbuf_tensor("biasc", [128, 64], F32)
    b.gb = p.sbuf_tensor("gbc", [128, 16], F32)
    b.wfm = p.sbuf_tensor("wfm", [16, 4096], BF16)
    b.stash = p.sbuf_tensor("stash", [128, 640], F32)
    b.rows = p.sbuf_tensor("rows", [1, 2048], F32)
    b.rowbf = p.sbuf_tensor("rowbf", [1, 1024], BF16)
    b.tiny = p.sbuf_tensor("tiny", [1, 64], F32)
    b.wgt = p.sbuf_tensor("wgt", [128, 80], BF16)
    b.mi32 = p.sbuf_tensor("mi32", [128, 8], U32)
    b.b2s = p.sbuf_tensor("b2s", [12, 1024], BF16)
    b.ones_red = p.sbuf_tensor("ones_red", [128, 2], BF16)
    b.ones_redf = p.sbuf_tensor("ones_redf", [128, 2], F32)
    b.ones_row = p.sbuf_tensor("ones_row", [1, 128], F32)
    b.ones_rowb = p.sbuf_tensor("ones_rowb", [1, 128], BF16)
    b.rope = p.sbuf_tensor("ropec", [128, 1024], F32)
    b.rtmp = p.sbuf_tensor("rtmp", [128, 1024], F32)
    b.dec = p.sbuf_tensor("decc", [128, 32], F32)
    b.idb = p.sbuf_tensor("idb", [128, 128], BF16)
    b.idf = p.sbuf_tensor("idf", [128, 128], F32)
    b.iota8 = p.sbuf_tensor("iota8", [128, 8], F32)
    b.sut = p.sbuf_tensor("sut", [128, 128], F32)
    b.cmask = p.sbuf_tensor("cmask", [128, 2048], BF16)
    b.cminv = p.sbuf_tensor("cminv", [128, 2048], BF16)
    b.coremask = p.sbuf_tensor("coremask", [1, 64], F32)
    b.zero512 = p.sbuf_tensor("zero512", [128, 512], BF16)
    b.ps = {}
    for nm in ["pa", "pb", "po", "po2", "pc", "pr", "pg"]:
        b.ps[nm] = stack.enter_context(nc.psum_tensor(nm, [128, 512], F32))
    b.pst = stack.enter_context(nc.psum_tensor("pst", [128, 1024], BF16))
    return b


def load_consts(p, b, cin):
    for nm, dst in [("c_idb", b.idb), ("c_idf", b.idf), ("c_iota8", b.iota8),
                    ("c_sut", b.sut), ("c_cmask", b.cmask),
                    ("c_cminv", b.cminv), ("c_coremask", b.coremask)]:
        h = cin[nm]
        p.dma("qsp", dst[:, :], h.ap()[:, :], reads=(), writes=[(dst, "a")])
    p.emit("dve", lambda e: e.memset(b.ones_red[:, :], 1.0), writes=[(b.ones_red, "a")])
    p.emit("dve", lambda e: e.memset(b.ones_redf[:, :], 1.0), writes=[(b.ones_redf, "a")])
    p.emit("dve", lambda e: e.memset(b.ones_row[:, :], 1.0), writes=[(b.ones_row, "a")])
    p.emit("dve", lambda e: e.memset(b.ones_rowb[:, :], 1.0), writes=[(b.ones_rowb, "a")])
    p.emit("dve", lambda e: e.memset(b.zero512[:, :], 0.0), writes=[(b.zero512, "a")])


def dk(handle, r0, r1, c0, c1):
    nm = handle.name if hasattr(handle, "name") else str(handle)
    out = []
    for r in range(r0 // 64, (r1 + 63) // 64):
        for c in range(c0 // 512, (c1 + 511) // 512):
            out.append((nm, ("b", r, c)))
    return out


def u1g(lo, hi):
    return [("U1", ("g", g)) for g in range(lo // 2048, (hi + 2047) // 2048)]



def wstg_slot(b):
    i = b.wstg_i[0] % 2
    b.wstg_i[0] += 1
    return b.wstg[:, i * 1024:(i + 1) * 1024], ("s", i)


def load_w(p, b, w_dram, Din, oc0, ocw, tag="w"):
    nkc = Din // 128
    assert nkc * ocw <= 16384
    base = b.WSB_OFF
    for kc in range(nkc):
        for c0 in range(0, ocw, 1024):
            cw = min(1024, ocw - c0)
            src = w_dram.ap()[kc * 128:(kc + 1) * 128, oc0 + c0: oc0 + c0 + cw]
            stg, skey = wstg_slot(b)
            p.dma("qsp", stg[:, 0:cw], src, writes=[(b.wstg, skey)])
            dst = b.U1[:, base + kc * ocw + c0: base + kc * ocw + c0 + cw]
            p.emit("dve", lambda e, s=stg[:, 0:cw], d=dst: e.tensor_copy(d, s),
                   reads=[(b.wstg, skey)],
                   writes=u1g(base + kc * ocw + c0, base + kc * ocw + c0 + cw))
    return base


def w_tile(b, base, ocw, kc, j0, m):
    off = base + kc * ocw + j0
    return b.U1[:, off: off + m]



def gemm_fm(p, b, out_dram, in_dram, w_dram, Din, Dout, *,
            bias_pcol=None,
            act=None,
            scale=1.0, bias_scale=None,
            out_dtype="bf16",
            resid_dram=None,
            out_row0=0,
            in_row0=0,
            tag="g"):
    nkc = Din // 128
    noc = Dout // 128
    if bias_pcol is not None:
        p.dma("qsp", b.bias[:, 0:noc], bias_pcol.ap()[:, 0:noc],
              writes=[(b.bias, "a")])
        if bias_scale is not None:
            p.emit("dve", lambda e: e.tensor_scalar(b.bias[:, 0:noc], b.bias[:, 0:noc],
                                                    bias_scale, None, OP.mult),
                   reads=[(b.bias, "a")], writes=[(b.bias, "a")])
    ocg = max(1, min(Dout, 16384 * 128 // Din))
    ocg = (ocg // 128) * 128
    psr = ["pa", "pb", "pc", "pr"]
    pi = 0
    for g0 in range(0, Dout, ocg):
        gw = min(ocg, Dout - g0)
        base = load_w(p, b, w_dram, Din, g0, gw, tag=tag)
        wkeys = u1g(base, base + nkc * gw)
        for tt in range(NTT):
            assert nkc <= 16
            par = (tt % 2) if nkc <= 8 else 0
            xoff = par * 4096
            for kc in range(nkc):
                src = in_dram.ap()[in_row0 + kc * 128: in_row0 + (kc + 1) * 128,
                                   tt * TTW:(tt + 1) * TTW]
                p.dma("qsp", b.XB[:, xoff + kc * TTW: xoff + (kc + 1) * TTW], src,
                      reads=dk(in_dram, in_row0 + kc * 128, in_row0 + (kc + 1) * 128,
                               tt * TTW, (tt + 1) * TTW),
                      writes=[(b.XB, ("g", xoff // 512 + kc))])
            for j0 in range(0, gw, 128):
                oc = (g0 + j0) // 128
                ps = b.ps[psr[pi % 4]]
                pi += 1
                for kc in range(nkc):
                    lhsT = w_tile(b, base, gw, kc, j0, 128)
                    rhs = b.XB[:, xoff + kc * TTW: xoff + (kc + 1) * TTW]
                    p.emit("pe",
                           lambda e, o=ps[:, :], l=lhsT, r=rhs, s=(kc == 0),
                                  t=(kc == nkc - 1):
                               e.matmul(o, l, r, start=s, stop=t),
                           reads=wkeys + [(b.XB, ("g", xoff // 512 + kc))],
                           writes=[(ps, "a")])
                ob = (pi % 4)
                if resid_dram is not None:
                    rsl = b.S[:, (oc % 4) * 512: (oc % 4) * 512 + 512]
                    p.dma("qsp", rsl,
                          resid_dram.ap()[out_row0 + oc * 128: out_row0 + oc * 128 + 128,
                                          tt * TTW:(tt + 1) * TTW],
                          reads=dk(resid_dram, out_row0 + oc * 128,
                                   out_row0 + oc * 128 + 128, tt * TTW, (tt + 1) * TTW),
                          writes=[(b.S, ("g", oc % 4))])
                    osl = b.osf[:, ob * 512:(ob + 1) * 512]
                    if bias_pcol is not None:
                        p.emit("dve",
                               lambda e, o=osl, i=ps[:, :], sc=b.bias[:, oc:oc + 1]:
                                   e.tensor_scalar(o, i, sc, None, OP.add),
                               reads=[(ps, "a"), (b.bias, "a")],
                               writes=[(b.osf, ("o", ob))])
                        p.emit("dve",
                               lambda e, o=osl, i=osl, r=rsl: e.tensor_add(o, i, r),
                               reads=[(b.S, ("g", oc % 4)), (b.osf, ("o", ob))],
                               writes=[(b.osf, ("o", ob))])
                    else:
                        p.emit("dve",
                               lambda e, o=osl, i=ps[:, :], r=rsl: e.tensor_add(o, i, r),
                               reads=[(ps, "a"), (b.S, ("g", oc % 4))],
                               writes=[(b.osf, ("o", ob))])
                    p.dma("qact", out_dram.ap()[out_row0 + oc * 128: out_row0 + oc * 128 + 128,
                                                tt * TTW:(tt + 1) * TTW],
                          osl, reads=[(b.osf, ("o", ob))],
                          writes=dk(out_dram, out_row0 + oc * 128,
                                    out_row0 + oc * 128 + 128, tt * TTW, (tt + 1) * TTW))
                else:
                    osl = (b.osb if out_dtype == "bf16" else b.osf)[:, ob * 512:(ob + 1) * 512]
                    okey = (b.osb if out_dtype == "bf16" else b.osf, ("o", ob))
                    fn = act if act is not None else AF.Copy
                    if bias_pcol is not None and act is None:
                        p.emit("dve",
                               lambda e, o=osl, i=ps[:, :], sc=scale,
                                      ba=b.bias[:, oc:oc + 1]:
                                   e.tensor_scalar(o, i, sc, ba, OP.mult, OP.add)
                                   if sc != 1.0 else
                                   e.tensor_scalar(o, i, ba, None, OP.add),
                               reads=[(ps, "a"), (b.bias, "a")], writes=[okey])
                    elif bias_pcol is not None:
                        p.emit("act",
                               lambda e, o=osl, i=ps[:, :], f=fn, sc=scale,
                                      ba=b.bias[:, oc:oc + 1]:
                                   e.activation(out=o, in_=i, func=f, bias=ba, scale=sc),
                               reads=[(ps, "a"), (b.bias, "a")], writes=[okey])
                    else:
                        p.emit("act",
                               lambda e, o=osl, i=ps[:, :], f=fn, sc=scale:
                                   e.activation(out=o, in_=i, func=f, scale=sc),
                               reads=[(ps, "a")], writes=[okey])
                    p.dma("qact", out_dram.ap()[out_row0 + oc * 128: out_row0 + oc * 128 + 128,
                                                tt * TTW:(tt + 1) * TTW],
                          osl, reads=[okey],
                          writes=dk(out_dram, out_row0 + oc * 128,
                                    out_row0 + oc * 128 + 128, tt * TTW, (tt + 1) * TTW))



def norm_fm(p, b, out_bf, in_f32, Dn, *, eps=1e-5, kind="ln",
            gamma_pcol=None, beta_pcol=None, out_f32=None,
            final_resid=None, final_out=None, tag="ln"):
    nkc = Dn // 128
    if gamma_pcol is not None:
        p.dma("qsp", b.gb[:, 0:nkc], gamma_pcol.ap()[:, 0:nkc], writes=[(b.gb, "g")])
    if beta_pcol is not None:
        p.dma("qsp", b.gb[:, 8:8 + nkc], beta_pcol.ap()[:, 0:nkc], writes=[(b.gb, "b")])
    for tt in range(NTT):
        for kc in range(nkc):
            p.dma("qsp", b.S[:, (kc % 8) * 512:(kc % 8) * 512 + 512],
                  in_f32.ap()[kc * 128:(kc + 1) * 128, tt * TTW:(tt + 1) * TTW],
                  reads=dk(in_f32, kc * 128, (kc + 1) * 128, tt * TTW, (tt + 1) * TTW),
                  writes=[(b.S, ("g", kc))])
        psq = b.ps["pr"]
        pss = b.ps["pc"]
        for kc in range(nkc):
            xsl = b.S[:, (kc % 8) * 512:(kc % 8) * 512 + 512]
            xbk = b.XB[:, kc * 512: kc * 512 + 512]
            p.emit("dve", lambda e, o=xbk, i=xsl: e.tensor_copy(o, i),
                   reads=[(b.S, ("g", kc))], writes=[(b.XB, ("g", kc))])
            sqk = b.XB[:, 4096 + (kc % 8) * 512: 4096 + (kc % 8) * 512 + 512]
            p.emit("dve", lambda e, o=sqk, i=xbk: e.tensor_mul(o, i, i),
                   reads=[(b.XB, ("g", kc))], writes=[(b.XB, ("g", 8 + kc))])
            if kind == "ln":
                p.emit("pe", lambda e, o=pss[0:1, :], l=b.ones_red[:, 0:1], r=xbk,
                              s=(kc == 0), t=(kc == nkc - 1):
                           e.matmul(o, l, r, start=s, stop=t),
                       reads=[(b.XB, ("g", kc)), (b.ones_red, "a")],
                       writes=[(pss, "a")])
            p.emit("pe", lambda e, o=psq[0:1, :], l=b.ones_red[:, 0:1], r=sqk,
                          s=(kc == 0), t=(kc == nkc - 1):
                       e.matmul(o, l, r, start=s, stop=t),
                   reads=[(b.XB, ("g", 8 + kc)), (b.ones_red, "a")],
                   writes=[(psq, "a")])
        mrow = b.rows[0:1, 0:512]
        vrow = b.rows[0:1, 512:1024]
        rrow = b.rows[0:1, 1024:1536]
        if kind == "ln":
            p.emit("dve", lambda e: e.tensor_scalar(mrow, pss[0:1, :], 1.0 / Dn, None, OP.mult),
                   reads=[(pss, "a")], writes=[("rows", "m")])
            p.emit("dve", lambda e: e.tensor_mul(vrow, mrow, mrow),
                   reads=[("rows", "m")], writes=[("rows", "v")])
            p.emit("dve", lambda e: e.scalar_tensor_tensor(vrow, psq[0:1, :], 1.0 / Dn, vrow,
                                                           OP.mult, OP.subtract),
                   reads=[(psq, "a"), ("rows", "v")], writes=[("rows", "v")])
        else:
            p.emit("dve", lambda e: e.tensor_scalar(vrow, psq[0:1, :], 1.0 / Dn, None, OP.mult),
                   reads=[(psq, "a")], writes=[("rows", "v")])
        p.emit("dve", lambda e: e.tensor_scalar(rrow, vrow, eps, None, OP.add),
               reads=[("rows", "v")], writes=[("rows", "r")])
        p.emit("act", lambda e: e.activation(out=rrow, in_=rrow, func=AF.Sqrt),
               reads=[("rows", "r")], writes=[("rows", "r")])
        p.emit("dve", lambda e: e.reciprocal(rrow, rrow),
               reads=[("rows", "r")], writes=[("rows", "r")])
        pbm = b.ps["pa"]
        pbr = b.ps["pb"]
        if kind == "ln":
            p.emit("pe", lambda e: e.matmul(pbm[:, :], b.ones_row[0:1, :], mrow,
                                            start=True, stop=True),
                   reads=[("rows", "m"), (b.ones_row, "a")], writes=[(pbm, "a")])
        p.emit("pe", lambda e: e.matmul(pbr[:, :], b.ones_row[0:1, :], rrow,
                                        start=True, stop=True),
               reads=[("rows", "r"), (b.ones_row, "a")], writes=[(pbr, "a")])
        for kc in range(nkc):
            xsl = b.S[:, (kc % 8) * 512:(kc % 8) * 512 + 512]
            ot = b.osf[:, (kc % 4) * 512:(kc % 4) * 512 + 512]
            if kind == "ln":
                p.emit("dve", lambda e, o=ot, i=xsl: e.tensor_sub(o, i, pbm[:, :]),
                       reads=[(b.S, ("g", kc)), (pbm, "a")], writes=[(b.osf, ("o", kc % 4))])
                src2 = ot
            else:
                src2 = xsl
            obf = b.osb[:, (kc % 4) * 512:(kc % 4) * 512 + 512]
            if gamma_pcol is None:
                p.emit("dve", lambda e, o=obf, i=src2, k=kc:
                           e.tensor_mul(o, i, pbr[:, :]),
                       reads=[(b.osf, ("o", kc % 4)) if kind == "ln" else (b.S, ("g", kc)),
                              (pbr, "a")],
                       writes=[(b.osb, ("o", kc % 4))])
            else:
                p.emit("dve", lambda e, o=ot, i=src2:
                           e.tensor_mul(o, i, pbr[:, :]),
                       reads=[(b.osf, ("o", kc % 4)) if kind == "ln" else (b.S, ("g", kc)),
                              (pbr, "a")],
                       writes=[(b.osf, ("o", kc % 4))])
                if beta_pcol is not None:
                    p.emit("dve", lambda e, o=obf, i=ot, k=kc:
                               e.tensor_scalar(o, i, b.gb[:, k:k + 1], b.gb[:, 8 + k:9 + k],
                                               OP.mult, OP.add),
                           reads=[(b.osf, ("o", kc % 4)), (b.gb, "g"), (b.gb, "b")],
                           writes=[(b.osb, ("o", kc % 4))])
                else:
                    p.emit("dve", lambda e, o=obf, i=ot, k=kc:
                               e.tensor_scalar(o, i, b.gb[:, k:k + 1], None, OP.mult),
                           reads=[(b.osf, ("o", kc % 4)), (b.gb, "g")],
                           writes=[(b.osb, ("o", kc % 4))])
            if final_out is not None:
                rsl = b.S[:, 6144 + (kc % 4) * 512: 6144 + (kc % 4) * 512 + 512]
                p.dma("qsp", rsl,
                      final_resid.ap()[kc * 128:(kc + 1) * 128, tt * TTW:(tt + 1) * TTW],
                      reads=dk(final_resid, kc * 128, (kc + 1) * 128, tt * TTW,
                               (tt + 1) * TTW),
                      writes=[(b.S, ("g", 12 + kc % 4))])
                of3 = b.osf[:, (kc % 4) * 512:(kc % 4) * 512 + 512]
                p.emit("dve", lambda e, o=of3, i=obf, r=rsl: e.tensor_add(o, i, r),
                       reads=[(b.osb, ("o", kc % 4)), (b.S, ("g", 12 + kc % 4))],
                       writes=[(b.osf, ("o", kc % 4))])
                p.dma("qact", final_out.ap()[kc * 128:(kc + 1) * 128, tt * TTW:(tt + 1) * TTW],
                      of3, reads=[(b.osf, ("o", kc % 4))],
                      writes=dk(final_out, kc * 128, (kc + 1) * 128, tt * TTW,
                                (tt + 1) * TTW))
            else:
                p.dma("qact", out_bf.ap()[kc * 128:(kc + 1) * 128, tt * TTW:(tt + 1) * TTW],
                      obf, reads=[(b.osb, ("o", kc % 4))],
                      writes=dk(out_bf, kc * 128, (kc + 1) * 128, tt * TTW, (tt + 1) * TTW))
            if out_f32 is not None:
                of2 = b.osf[:, (kc % 4) * 512:(kc % 4) * 512 + 512]
                p.emit("dve", lambda e, o=of2, i=obf: e.tensor_copy(o, i),
                       reads=[(b.osb, ("o", kc % 4))], writes=[(b.osf, ("o", kc % 4))])
                p.dma("qact", out_f32.ap()[kc * 128:(kc + 1) * 128, tt * TTW:(tt + 1) * TTW],
                      of2, reads=[(b.osf, ("o", kc % 4))],
                      writes=dk(out_f32, kc * 128, (kc + 1) * 128, tt * TTW, (tt + 1) * TTW))



def make_consts_host(core_id):
    c = {}
    c["c_idb"] = np.eye(128, dtype=ml_dtypes.bfloat16)
    c["c_idf"] = np.eye(128, dtype=np.float32)
    c["c_iota8"] = np.tile(np.arange(8, dtype=np.float32), (128, 1))
    c["c_sut"] = np.triu(np.ones((128, 128), np.float32), 1)
    cm = np.zeros((128, 4 * 512), ml_dtypes.bfloat16)
    ii = np.arange(128)[:, None]
    jj = np.arange(512)[None, :]
    for v in range(4):
        cm[:, v * 512:(v + 1) * 512] = (ii + v * 128 <= jj).astype(ml_dtypes.bfloat16)
    c["c_cmask"] = cm
    c["c_cminv"] = (1.0 - cm.astype(np.float32)).astype(ml_dtypes.bfloat16)
    msk = np.zeros((1, 64), np.float32)
    for cc in range(core_id):
        msk[0, cc * 8:(cc + 1) * 8] = 1.0
    c["c_coremask"] = msk
    return c

CONST_SPECS = [("c_idb", [128, 128], BF16), ("c_idf", [128, 128], F32),
               ("c_iota8", [128, 8], F32), ("c_sut", [128, 128], F32),
               ("c_cmask", [128, 2048], BF16), ("c_cminv", [128, 2048], BF16),
               ("c_coremask", [1, 64], F32)]


def pcol(v):
    v = np.asarray(v, np.float32)
    n = (v.shape[0] + 127) // 128
    out = np.zeros((128, n), np.float32)
    out[:, :] = v.reshape(n, 128).T
    return out


class P_drain:
    pass


def p_drain(p):
    vals = {}
    for qn in p.QUEUES:
        for c in range(p.NCHAIN):
            nm = qn + str(c)
            if p.count[nm] > 0:
                vals[nm] = p.count[nm]
    pp = p
    def thunk(e):
        for q, v in vals.items():
            e.wait_ge(pp.sems[q], v)
    p.q["pool"].append(thunk)


def build_nc(emit_fn, io_spec, core_id=0):
    from contextlib import ExitStack
    import os
    nc = bass.Bass("TRN2", target_bir_lowering=False, debug=False,
                   detect_race_conditions=bool(os.environ.get("RACEDET")))
    stack = ExitStack()
    with stack:
        p = P(nc)
        t = {}
        for name, shape, dtype, kind in io_spec:
            t[name] = p.dram_tensor(name, shape, dtype, kind=kind)
        for name, shape, dtype in CONST_SPECS:
            t[name] = p.dram_tensor(name, shape, dtype, kind="ExternalInput")
        p.finalize_sems(stack)
        b = setup_ctx(p, stack)
        load_consts(p, b, t)
        emit_fn(p, b, t)
        p_drain(p)
        p.run_blocks()
    return nc, p



def attention(p, b, *, out_dram, q_fm=None, k_fm=None, v_tm, mode,
              q_prep=None, k_prep=None, nheads=16, shift=40.0, tag="att"):
    NKB = T // 128
    p.emit("dve", lambda e: e.memset(
        bass.AP(b.U1, 64, [[33280, 128], [1040, NKB], [65, 16], [1, 1]]), 1.0),
        writes=u1g(0, 33280))
    for kb in range(NKB):
        stg = b.XB[:, (kb % 4) * 1024:(kb % 4) * 1024 + 1024]
        skeys = [(b.XB, ("g", (kb % 4) * 2)), (b.XB, ("g", (kb % 4) * 2 + 1))]
        p.dma("qsp", stg, v_tm.ap()[kb * 128:(kb + 1) * 128, 0:1024],
              reads=dk(v_tm, kb * 128, (kb + 1) * 128, 0, 1024), writes=skeys)
        dst = bass.AP(b.U1, kb * 1040, [[33280, 128], [65, 16], [1, 64]])
        src = bass.AP(b.XB, (kb % 4) * 1024, [[8192, 128], [64, 16], [1, 64]])
        p.emit("dve", lambda e, d=dst, s=src: e.tensor_copy(d, s),
               reads=skeys, writes=u1g(kb * 1040, kb * 1040 + 1040))
    p.emit("dve", lambda e: e.memset(b.kaug[64:65, :], 1.0),
           writes=[(b.kaug, "one")])
    vkeys_all = u1g(0, 33280)

    for h in range(nheads):
        if k_prep is None:
            p.dma("qsp", b.kaug[0:64, :], k_fm.ap()[h * 64:(h + 1) * 64, :],
                  reads=dk(k_fm, h * 64, (h + 1) * 64, 0, T),
                  writes=[(b.kaug, "hd")])
        else:
            k_prep(h)
        if q_prep is None:
            p.dma("qsp", b.qsb[0:64, :], q_fm.ap()[h * 64:(h + 1) * 64, :],
                  reads=dk(q_fm, h * 64, (h + 1) * 64, 0, T),
                  writes=[(b.qsb, "hd")])
        else:
            q_prep(h)
        pr = b.ps["pr"]
        for kt in range(8):
            ksl = b.kaug[0:64, kt * 512:(kt + 1) * 512]
            p.emit("dve", lambda e, k=ksl: e.tensor_mul(b.attsq[0:64, 0:512], k, k),
                   reads=[(b.kaug, "hd")], writes=[(b.attsq, "a")])
            p.emit("pe", lambda e: e.matmul(pr[0:1, :], b.ones_red[0:64, 0:1],
                                            b.attsq[0:64, 0:512], start=True, stop=True),
                   reads=[(b.attsq, "a"), (b.ones_red, "a")], writes=[(pr, "a")])
            p.emit("dve", lambda e, kt=kt: e.reduce_max(b.tiny[0:1, kt:kt + 1],
                                                        pr[0:1, :], axis=AX.X),
                   reads=[(pr, "a")], writes=[(b.tiny, ("km", kt))])
        p.emit("dve", lambda e: e.reduce_max(b.tiny[0:1, 8:9], b.tiny[0:1, 0:8],
                                             axis=AX.X),
               reads=[(b.tiny, ("km", kt)) for kt in range(8)],
               writes=[(b.tiny, "k2")])
        p.emit("act", lambda e: e.activation(out=b.tiny[0:1, 9:10], in_=b.tiny[0:1, 8:9],
                                             func=AF.Sqrt),
               reads=[(b.tiny, "k2")], writes=[(b.tiny, "ks")])
        for qt in range(NTT):
            qsl = b.qsb[0:64, qt * 512:(qt + 1) * 512]
            p.emit("dve", lambda e, q=qsl: e.tensor_mul(b.attsq[0:64, 0:512], q, q),
                   reads=[(b.qsb, "hd")], writes=[(b.attsq, "a")])
            p.emit("pe", lambda e: e.matmul(pr[0:1, :], b.ones_red[0:64, 0:1],
                                            b.attsq[0:64, 0:512], start=True, stop=True),
                   reads=[(b.attsq, "a"), (b.ones_red, "a")], writes=[(pr, "a")])
            brow = b.rows[0:1, 512:1024]
            p.emit("act", lambda e: e.activation(out=brow, in_=pr[0:1, :], func=AF.Sqrt),
                   reads=[(pr, "a")], writes=[("rows", "b")])
            p.emit("dve", lambda e: e.tensor_scalar(brow, brow, b.tiny[0:1, 9:10],
                                                    None, OP.mult),
                   reads=[("rows", "b"), (b.tiny, "ks")], writes=[("rows", "b")])
            p.emit("dve", lambda e, qt=qt: e.tensor_scalar(
                       b.qsb[64:65, qt * 512:(qt + 1) * 512], brow, -1.0, shift,
                       OP.mult, OP.add),
                   reads=[("rows", "b")], writes=[(b.qsb, ("nb", qt))])
        if mode == "zmask":
            p.emit("dve", lambda e: e.memset(b.vfut[:, :], 0.0),
                   writes=[(b.vfut, "a")])
            fut_loaded = NKB
        for qt in range(NTT - 1, -1, -1):
            kb_hi = NKB if mode == "none" else 4 * qt + 4
            if mode == "zmask":
                for kb in range(kb_hi, fut_loaded):
                    p.emit("dve", lambda e, kb=kb: e.tensor_add(
                               b.vfut[:, :], b.vfut[:, :],
                               b.U1[:, kb * 1040:(kb + 1) * 1040]),
                           reads=[(b.vfut, "a")] + u1g(kb * 1040, (kb + 1) * 1040),
                           writes=[(b.vfut, "a")])
                fut_loaded = kb_hi
                pc = b.ps["pc"]
                p.emit("dve", lambda e, qt=qt: e.tensor_copy(
                           b.rowbf[0:1, 0:512], b.qsb[64:65, qt * 512:(qt + 1) * 512]),
                       reads=[(b.qsb, ("nb", qt))], writes=[(b.rowbf, "a")])
                p.emit("pe", lambda e: e.matmul(
                           pc[:, :], b.ones_rowb[0:1, :],
                           b.rowbf[0:1, 0:512], start=True, stop=True),
                       reads=[(b.rowbf, "a"), (b.ones_rowb, "a")],
                       writes=[(pc, "a")])
                p.emit("act", lambda e: e.activation(out=b.ptc[:, :], in_=pc[:, :],
                                                     func=AF.Exp),
                       reads=[(pc, "a")], writes=[(b.ptc, "a")])
            po = b.ps["po"] if qt % 2 == 0 else b.ps["po2"]
            first = True
            last_kb = kb_hi - 1
            for kb in range(kb_hi):
                par = kb % 2
                ps_st = b.ps["pa"] if par == 0 else b.ps["pb"]
                ptt = b.pt[:, par * 512:(par + 1) * 512]
                ptkey = (b.pt, ("t", par))
                p.emit("pe", lambda e, kb=kb, qt=qt, o=ps_st: e.matmul(
                           o[:, :], b.kaug[0:65, kb * 128:(kb + 1) * 128],
                           b.qsb[0:65, qt * 512:(qt + 1) * 512],
                           start=True, stop=True),
                       reads=[(b.kaug, "hd"), (b.kaug, "one"), (b.qsb, "hd"),
                              (b.qsb, ("nb", qt))],
                       writes=[(ps_st, "a")])
                p.emit("act", lambda e, o=ptt, i=ps_st: e.activation(
                           out=o, in_=i[:, :], func=AF.Exp),
                       reads=[(ps_st, "a")], writes=[ptkey])
                if mode != "none" and kb >= 4 * qt:
                    v = kb - 4 * qt
                    msk = b.cmask[:, v * 512:(v + 1) * 512]
                    if mode == "causal":
                        p.emit("dve", lambda e, o=ptt, m=msk: e.tensor_mul(o, o, m),
                               reads=[ptkey, (b.cmask, "a")], writes=[ptkey])
                    else:
                        iv = b.cminv[:, v * 512:(v + 1) * 512]
                        p.emit("dve", lambda e, m=iv: e.tensor_mul(
                                   b.attsq[:, 0:512], b.ptc[:, :], m),
                               reads=[(b.cminv, "a"), (b.ptc, "a")],
                               writes=[(b.attsq, "a")])
                        p.emit("dve", lambda e, o=ptt, m=msk: e.tensor_mul(o, o, m),
                               reads=[ptkey, (b.cmask, "a")], writes=[ptkey])
                        p.emit("dve", lambda e, o=ptt: e.tensor_add(
                                   o, o, b.attsq[:, 0:512]),
                               reads=[ptkey, (b.attsq, "a")], writes=[ptkey])
                stop_here = (kb == last_kb) and (mode != "zmask")
                p.emit("pe", lambda e, kb=kb, o=po, pt_=ptt, s=first, st=stop_here, h=h:
                           e.matmul(
                           o[0:65, :], b.U1[:, kb * 1040 + h * 65: kb * 1040 + h * 65 + 65],
                           pt_, start=s, stop=st),
                       reads=[ptkey] + u1g(kb * 1040 + h * 65, kb * 1040 + h * 65 + 65),
                       writes=[(po, "a")])
                first = False
            if mode == "zmask":
                p.emit("pe", lambda e, o=po, h=h, s=first: e.matmul(
                           o[0:65, :], b.vfut[:, h * 65: h * 65 + 65], b.ptc[:, :],
                           start=s, stop=True),
                       reads=[(b.vfut, "a"), (b.ptc, "a")], writes=[(po, "a")])
            invz = b.rows[0:1, 1024:1536]
            p.emit("dve", lambda e, o=invz, i=po: e.reciprocal(o, i[64:65, :]),
                   reads=[(po, "a")], writes=[("rows", "z")])
            pc2 = b.ps["pg"]
            p.emit("pe", lambda e, o=pc2: e.matmul(o[0:64, :], b.ones_row[0:1, 0:64],
                                                   invz, start=True, stop=True),
                   reads=[("rows", "z"), (b.ones_row, "a")], writes=[(pc2, "a")])
            p.emit("dve", lambda e: e.tensor_copy(b.bsb[0:64, :], pc2[0:64, :]),
                   reads=[(pc2, "a")], writes=[(b.bsb, "a")])
            ob = qt % 4
            osl = b.osb[:, ob * 512:(ob + 1) * 512]
            p.emit("dve", lambda e, o=osl, i=po: e.tensor_mul(o[0:64, :], i[0:64, :],
                                                              b.bsb[0:64, :]),
                   reads=[(po, "a"), (b.bsb, "a")], writes=[(b.osb, ("o", ob))])
            p.dma("qact", out_dram.ap()[h * 64:(h + 1) * 64, qt * 512:(qt + 1) * 512],
                  b.osb[0:64, ob * 512:(ob + 1) * 512], reads=[(b.osb, ("o", ob))],
                  writes=dk(out_dram, h * 64, (h + 1) * 64, qt * 512, (qt + 1) * 512))



def gemm_tm(p, b, in_dram, w_dram, Din, Dout, *,
            out_fm=None, out_tm=None, bias_row=None, rope=None, rope_tab=None,
            scale=1.0, tag="gt"):
    nkc = Din // 128
    assert nkc <= 8
    if bias_row is not None:
        p.dma("qsp", b.rows[0:1, 0:Dout], bias_row.ap()[0:1, 0:Dout],
              writes=[("rows", "br")])
        p.emit("dve", lambda e: e.tensor_copy(b.rowbf[0:1, 0:Dout], b.rows[0:1, 0:Dout]),
               reads=[("rows", "br")], writes=[(b.rowbf, "a")])
    for g0 in range(0, Dout, 512):
        gw = min(512, Dout - g0)
        base = load_w(p, b, w_dram, Din, g0, gw, tag=tag)
        wkeys = u1g(base, base + nkc * gw)
        for tt in range(NTT):
            par = tt % 2
            xoff = par * 4096
            for kc in range(nkc):
                src = in_dram.ap()[kc * 128:(kc + 1) * 128, tt * TTW:(tt + 1) * TTW]
                p.dma("qsp", b.XB[:, xoff + kc * TTW: xoff + (kc + 1) * TTW], src,
                      reads=dk(in_dram, kc * 128, (kc + 1) * 128, tt * TTW, (tt + 1) * TTW),
                      writes=[(b.XB, ("g", xoff // 512 + kc))])
            if rope is not None:
                pass
            for sub in range(4):
                t0 = tt * TTW + sub * 128
                ps = b.ps["pa"] if (sub % 2 == 0) else b.ps["pb"]
                if bias_row is not None:
                    p.emit("pe", lambda e, o=ps, g0=g0, gw=gw: e.matmul(
                               o[:, 0:gw], b.ones_rowb[0:1, :],
                               b.rowbf[0:1, g0:g0 + gw], start=True, stop=False),
                           reads=[(b.rowbf, "a"), (b.ones_rowb, "a")], writes=[(ps, "a")])
                for kc in range(nkc):
                    lhsT = b.XB[:, xoff + kc * TTW + sub * 128: xoff + kc * TTW + sub * 128 + 128]
                    rhs = w_tile(b, base, gw, kc, 0, gw)
                    p.emit("pe", lambda e, o=ps, l=lhsT, r=rhs, gw=gw,
                                  s=(kc == 0 and bias_row is None), t=(kc == nkc - 1):
                               e.matmul(o[:, 0:gw], l, r, start=s, stop=t),
                           reads=wkeys + [(b.XB, ("g", xoff // 512 + kc))],
                           writes=[(ps, "a")])
                ob = sub % 2
                osl = b.osb[:, ob * 512: ob * 512 + gw]
                okey = (b.osb, ("o", ob))
                if rope is None:
                    p.emit("act", lambda e, o=osl, i=ps, gw=gw, sc=scale: e.activation(
                               out=o, in_=i[:, 0:gw], func=AF.Copy, scale=sc),
                           reads=[(ps, "a")], writes=[okey])
                else:
                    import os as _os
                    rmode = _os.environ.get("ROPE_MODE", "full")
                    p.dma("qsp", b.rope[:, 0:512], rope_tab.ap()[t0:t0 + 128, 0:512],
                          writes=[(b.rope, "a")])
                    if rmode == "copy":
                        p.emit("act", lambda e, o=osl, i=ps, gw=gw: e.activation(
                                   out=o, in_=i[:, 0:gw], func=AF.Copy),
                               reads=[(ps, "a"), (b.rope, "a")], writes=[okey])
                    cos = b.rope[:, 0:256]
                    sin = b.rope[:, 256:512]
                    if rope == "il":
                        qe = bass.AP(ps, 0, [[512, 128], [2, 256]])
                        qo = bass.AP(ps, 1, [[512, 128], [2, 256]])
                        oe = bass.AP(b.osb, ob * 512, [[2048, 128], [2, 256]])
                        oo = bass.AP(b.osb, ob * 512 + 1, [[2048, 128], [2, 256]])
                    else:
                        qe = bass.AP(ps, 0, [[512, 128], [64, 8], [1, 32]])
                        qo = bass.AP(ps, 32, [[512, 128], [64, 8], [1, 32]])
                        oe = bass.AP(b.osb, ob * 512, [[2048, 128], [64, 8], [1, 32]])
                        oo = bass.AP(b.osb, ob * 512 + 32, [[2048, 128], [64, 8], [1, 32]])
                    if rmode == "copy":
                        if out_tm is not None:
                            p.dma("qact", out_tm.ap()[t0:t0 + 128, g0:g0 + gw],
                                  b.osb[:, ob * 512: ob * 512 + gw], reads=[okey],
                                  writes=dk(out_tm, t0, t0 + 128, g0, g0 + gw))
                        continue
                    cs3 = bass.AP(b.rope, 0, [[1024, 128], [32, 8], [1, 32]]) \
                        if rope == "half" else cos
                    sn3 = bass.AP(b.rope, 256, [[1024, 128], [32, 8], [1, 32]]) \
                        if rope == "half" else sin
                    r0 = b.rtmp[:, 0:256]
                    r1 = b.rtmp[:, 256:512]
                    r2 = b.rtmp[:, 512:768]
                    r3 = b.rtmp[:, 768:1024]
                    rkey = (b.rtmp, "a")
                    p.emit("dve", lambda e, a=r0, x=qe, c=cs3: e.tensor_mul(a, x, c),
                           reads=[(ps, "a"), (b.rope, "a")], writes=[rkey])
                    p.emit("dve", lambda e, a=r1, x=qo, c=sn3: e.tensor_mul(a, x, c),
                           reads=[(ps, "a"), (b.rope, "a")], writes=[rkey])
                    p.emit("dve", lambda e, a=r2, x=qe, c=sn3: e.tensor_mul(a, x, c),
                           reads=[(ps, "a"), (b.rope, "a")], writes=[rkey])
                    p.emit("dve", lambda e, a=r3, x=qo, c=cs3: e.tensor_mul(a, x, c),
                           reads=[(ps, "a"), (b.rope, "a")], writes=[rkey])
                    if rmode == "muls":
                        p.emit("act", lambda e, o=osl, i=ps, gw=gw: e.activation(
                                   out=o, in_=i[:, 0:gw], func=AF.Copy),
                               reads=[(ps, "a"), rkey], writes=[okey])
                    else:
                        p.emit("dve", lambda e, o=oe, a=r0, c=r1: e.tensor_sub(o, a, c),
                               reads=[rkey], writes=[okey])
                        p.emit("dve", lambda e, o=oo, a=r2, c=r3: e.tensor_add(o, a, c),
                               reads=[rkey], writes=[okey])
                if out_tm is not None:
                    p.dma("qact", out_tm.ap()[t0:t0 + 128, g0:g0 + gw],
                          b.osb[:, ob * 512: ob * 512 + gw], reads=[okey],
                          writes=dk(out_tm, t0, t0 + 128, g0, g0 + gw))
                if out_fm is not None:
                    for fsub in range(gw // 128):
                        tp = b.pst[:, 0:128]
                        p.emit("pe", lambda e, o=tp, i=b.osb[:, ob * 512 + fsub * 128:
                                                             ob * 512 + fsub * 128 + 128]:
                                   e.transpose(o, i, b.idb[:, :]),
                               reads=[okey, (b.idb, "a")], writes=[(b.pst, "t")])
                        ok2 = (b.osb, ("o", 2 + fsub % 2))
                        od = b.osb[:, 1024 + (fsub % 2) * 512: 1024 + (fsub % 2) * 512 + 128]
                        p.emit("dve", lambda e, o=od, i=tp: e.tensor_copy(o, i),
                               reads=[(b.pst, "t")], writes=[ok2])
                        p.dma("qact", out_fm.ap()[g0 + fsub * 128: g0 + fsub * 128 + 128,
                                                  t0:t0 + 128],
                              od, reads=[ok2],
                              writes=dk(out_fm, g0 + fsub * 128, g0 + fsub * 128 + 128,
                                        t0, t0 + 128))


def addpass(p, b, out_dram, a_dram, b_dram, nrow=1024):
    for kc in range(nrow // 128):
        for tt in range(NTT):
            asl = b.S[:, (tt % 4) * 512:(tt % 4) * 512 + 512]
            bsl = b.S[:, 2048 + (tt % 4) * 512: 2048 + (tt % 4) * 512 + 512]
            p.dma("qsp", asl, a_dram.ap()[kc * 128:(kc + 1) * 128, tt * TTW:(tt + 1) * TTW],
                  reads=dk(a_dram, kc * 128, (kc + 1) * 128, tt * TTW, (tt + 1) * TTW),
                  writes=[(b.S, ("g", tt % 4))])
            p.dma("qsp", bsl, b_dram.ap()[kc * 128:(kc + 1) * 128, tt * TTW:(tt + 1) * TTW],
                  reads=dk(b_dram, kc * 128, (kc + 1) * 128, tt * TTW, (tt + 1) * TTW),
                  writes=[(b.S, ("g", 4 + tt % 4))])
            ob = tt % 4
            osl = b.osf[:, ob * 512:(ob + 1) * 512]
            p.emit("dve", lambda e, o=osl, x=asl, y=bsl: e.tensor_add(o, x, y),
                   reads=[(b.S, ("g", tt % 4)), (b.S, ("g", 4 + tt % 4))],
                   writes=[(b.osf, ("o", ob))])
            p.dma("qact", out_dram.ap()[kc * 128:(kc + 1) * 128, tt * TTW:(tt + 1) * TTW],
                  osl, reads=[(b.osf, ("o", ob))],
                  writes=dk(out_dram, kc * 128, (kc + 1) * 128, tt * TTW, (tt + 1) * TTW))



def _bc(b, col, n):
    return bass.AP(b.stash, col, [[640, 128], [0, n]])


def routing(p, b, t, *, x_bf, wg_dram, E, variant, bias_row=None,
            capacity=None, coef_dram=None, cc_idx=0, cc_in=None, cc_out=None,
            tag="rt"):
    wrows = t["wrows"]
    nkc = 8
    for kc in range(nkc):
        p.dma("qsp", b.wstg[:, 0:E], wg_dram.ap()[kc * 128:(kc + 1) * 128, 0:E],
              writes=[(b.wstg, "a")])
        p.emit("dve", lambda e, kc=kc: e.tensor_copy(b.wgt[:, kc * E:(kc + 1) * E],
                                                     b.wstg[:, 0:E]),
               reads=[(b.wstg, "a")], writes=[(b.wgt, "a")])
        if coef_dram is not None:
            p.dma("qsp", b.wstg[:, 0:2], coef_dram.ap()[kc * 128:(kc + 1) * 128, 0:2],
                  writes=[(b.wstg, "a")])
            p.emit("dve", lambda e, kc=kc: e.tensor_copy(b.wgt[:, 64 + kc * 2:64 + kc * 2 + 2],
                                                         b.wstg[:, 0:2]),
                   reads=[(b.wstg, "a")], writes=[(b.wgt, "c")])
    if bias_row is not None:
        p.dma("qsp", b.rows[0:1, 1792:1792 + E], bias_row.ap()[0:1, 0:E],
              writes=[("rows", "gb")])
        p.emit("dve", lambda e: e.tensor_copy(b.rowbf[0:1, 512:512 + E],
                                              b.rows[0:1, 1792:1792 + E]),
               reads=[("rows", "gb")], writes=[(b.rowbf, "gb")])
    cap_mode = capacity is not None
    if cap_mode:
        p.emit("dve", lambda e: e.memset(b.rows[0:1, 1536:1544], 0.0),
               writes=[("rows", "cnt")])
    if E < 8:
        p.emit("dve", lambda e: e.memset(b.stash[:, 320 + E:328], -3.0e38),
               writes=[(b.stash, "lg")])
    for tt in range(NTT):
        for kc in range(nkc):
            src = x_bf.ap()[kc * 128:(kc + 1) * 128, tt * TTW:(tt + 1) * TTW]
            p.dma("qsp", b.XB[:, kc * TTW:(kc + 1) * TTW], src,
                  reads=dk(x_bf, kc * 128, (kc + 1) * 128, tt * TTW, (tt + 1) * TTW),
                  writes=[(b.XB, ("g", kc))])
        for sub in range(4):
            t32 = tt * 4 + sub
            pg = b.ps["pg"]
            if bias_row is not None:
                p.emit("pe", lambda e, E=E: e.matmul(pg[:, 0:E], b.ones_rowb[0:1, :],
                                                     b.rowbf[0:1, 512:512 + E],
                                                     start=True, stop=False),
                       reads=[(b.rowbf, "gb"), (b.ones_rowb, "a")], writes=[(pg, "a")])
            for kc in range(nkc):
                lhsT = b.XB[:, kc * TTW + sub * 128: kc * TTW + sub * 128 + 128]
                p.emit("pe", lambda e, l=lhsT, kc=kc, E=E,
                              s=(kc == 0 and bias_row is None), st=(kc == nkc - 1):
                           e.matmul(pg[:, 0:E], l, b.wgt[:, kc * E:(kc + 1) * E],
                                    start=s, stop=st),
                       reads=[(b.XB, ("g", kc)), (b.wgt, "a")], writes=[(pg, "a")])
            lg = b.stash[:, 320:328]
            p.emit("dve", lambda e, E=E: e.tensor_copy(lg[:, 0:E], pg[:, 0:E]),
                   reads=[(pg, "a")], writes=[(b.stash, "lg")])
            if coef_dram is not None:
                pc = b.ps["pc"]
                for kc in range(nkc):
                    lhsT = b.XB[:, kc * TTW + sub * 128: kc * TTW + sub * 128 + 128]
                    p.emit("pe", lambda e, l=lhsT, kc=kc:
                               e.matmul(pc[:, 0:2], l, b.wgt[:, 64 + kc * 2:64 + kc * 2 + 2],
                                        start=(kc == 0), stop=(kc == nkc - 1)),
                           reads=[(b.XB, ("g", kc)), (b.wgt, "c")], writes=[(pc, "a")])
                cl = b.stash[:, 336:340]
                p.emit("dve", lambda e: e.tensor_copy(cl[:, 3:4], pc[:, 1:2]),
                       reads=[(pc, "a")], writes=[(b.stash, "clt")])
                p.emit("dve", lambda e: e.tensor_sub(cl[:, 0:1], pc[:, 0:1], cl[:, 3:4]),
                       reads=[(pc, "a"), (b.stash, "clt")], writes=[(b.stash, "cl")])
                p.emit("act", lambda e: e.activation(out=cl[:, 1:2], in_=cl[:, 0:1],
                                                     func=AF.Sigmoid),
                       reads=[(b.stash, "cl")], writes=[(b.stash, "cl2")])
                p.emit("dve", lambda e, t32=t32: e.tensor_scalar(
                           b.stash[:, 288 + 0:289] if False else b.stash[:, 416 + t32 // 4 * 0:417] if False else cl[:, 2:3],
                           cl[:, 1:2], -1.0, 1.0, OP.mult, OP.add),
                       reads=[(b.stash, "cl2")], writes=[(b.stash, "cl")])
                p.emit("dve", lambda e, t32=t32: e.tensor_copy(
                           b.stash[:, 448 + t32:449 + t32], cl[:, 1:2]),
                       reads=[(b.stash, "cl2")], writes=[(b.stash, ("c1", t32))])
                p.emit("dve", lambda e, t32=t32: e.tensor_copy(
                           b.stash[:, 480 + t32 * 0 + 0:1] if False else b.stash[:, 512 + t32:513 + t32], cl[:, 2:3]),
                       reads=[(b.stash, "cl")], writes=[(b.stash, ("c2", t32))])
            m8 = b.stash[:, 328:336]
            p.emit("dve", lambda e: e.max(m8, lg[:, 0:8]),
                   reads=[(b.stash, "lg")], writes=[(b.stash, "m8")])
            ohsl = b.stash[:, t32 * 8:(t32 + 1) * 8]
            p.emit("dve", lambda e, o=ohsl, E=E: e.tensor_tensor(
                       o[:, 0:E], lg[:, 0:E], _bc(b, 328, E), OP.is_equal),
                   reads=[(b.stash, "lg"), (b.stash, "m8")],
                   writes=[(b.stash, ("oh", t32))])
            import os as _os
            if _os.environ.get("RT_DBG") and t32 == 5:
                p.dma("qact", t["rtdbg"].ap()[:, 0:640], b.stash[:, 0:640],
                      reads=[(b.stash, sk) for sk in
                             [("oh", 5), "lg", "m8", "wt"]],
                      writes=dk(t["rtdbg"], 0, 128, 0, 640))
            if variant == "top1":
                wt = b.stash[:, 424:436]
                p.emit("dve", lambda e, o=wt, i=ohsl, E=E: e.tensor_copy(o[:, 0:E], i[:, 0:E]),
                       reads=[(b.stash, ("oh", t32))], writes=[(b.stash, "wt")])
                _route_transpose(p, b, t32, E, 0, wrows)
            elif variant == "top2":
                sc = b.stash[:, 352:368]
                p.emit("dve", lambda e: e.tensor_sub(sc[:, 9:10], m8[:, 1:2], m8[:, 0:1]),
                       reads=[(b.stash, "m8")], writes=[(b.stash, "d21")])
                p.emit("act", lambda e: e.activation(out=sc[:, 10:11], in_=sc[:, 9:10],
                                                     func=AF.Exp),
                       reads=[(b.stash, "d21")], writes=[(b.stash, "e21")])
                p.emit("dve", lambda e: e.tensor_scalar(sc[:, 11:12], sc[:, 10:11],
                                                        1.0, None, OP.add),
                       reads=[(b.stash, "e21")], writes=[(b.stash, "s21")])
                p.emit("dve", lambda e: e.reciprocal(sc[:, 12:13], sc[:, 11:12]),
                       reads=[(b.stash, "s21")], writes=[(b.stash, "w1")])
                p.emit("dve", lambda e: e.tensor_scalar(sc[:, 13:14], sc[:, 12:13],
                                                        -1.0, 1.0, OP.mult, OP.add),
                       reads=[(b.stash, "w1")], writes=[(b.stash, "w2")])
                wt = b.stash[:, 424:436]
                p.emit("dve", lambda e, o=wt, i=ohsl, E=E: e.tensor_tensor(
                           o[:, 0:E], i[:, 0:E], _bc(b, 364, E), OP.mult),
                       reads=[(b.stash, ("oh", t32)), (b.stash, "w1")],
                       writes=[(b.stash, "wt")])
                oh2 = b.stash[:, 368:376]
                p.emit("dve", lambda e, o=oh2, E=E: e.tensor_tensor(
                           o[:, 0:E], lg[:, 0:E], _bc(b, 329, E), OP.is_equal),
                       reads=[(b.stash, "lg"), (b.stash, "m8")],
                       writes=[(b.stash, "oh2")])
                p.emit("dve", lambda e, o=oh2, E=E: e.tensor_tensor(
                           o[:, 0:E], o[:, 0:E], _bc(b, 365, E), OP.mult),
                       reads=[(b.stash, "oh2"), (b.stash, "w2")],
                       writes=[(b.stash, "oh2")])
                p.emit("dve", lambda e, o=wt, i=oh2, E=E: e.tensor_add(
                           o[:, 0:E], o[:, 0:E], i[:, 0:E]),
                       reads=[(b.stash, "oh2"), (b.stash, "wt")], writes=[(b.stash, "wt")])
                _route_transpose(p, b, t32, E, 0, wrows)
            else:
                neg = b.stash[:, 360:361]
                p.emit("dve", lambda e: e.tensor_scalar(neg, m8[:, 0:1], -1.0, None, OP.mult),
                       reads=[(b.stash, "m8")], writes=[(b.stash, "neg")])
                e8 = b.stash[:, 368:376]
                p.emit("act", lambda e, E=E: e.activation(out=e8[:, 0:E], in_=lg[:, 0:E],
                                                          func=AF.Exp, bias=neg),
                       reads=[(b.stash, "lg"), (b.stash, "neg")], writes=[(b.stash, "e8")])
                ssum = b.stash[:, 361:362]
                p.emit("dve", lambda e, E=E: e.reduce_sum(ssum, e8[:, 0:E], axis=AX.X),
                       reads=[(b.stash, "e8")], writes=[(b.stash, "ss")])
                rr = b.stash[:, 362:363]
                p.emit("dve", lambda e: e.reciprocal(rr, ssum),
                       reads=[(b.stash, "ss")], writes=[(b.stash, "rr")])
                if variant == "top1cap":
                    tmp8 = b.stash[:, 376:384]
                    p.emit("dve", lambda e, i=ohsl, E=E: e.tensor_mul(
                               tmp8[:, 0:E], e8[:, 0:E], i[:, 0:E]),
                           reads=[(b.stash, "e8"), (b.stash, ("oh", t32))],
                           writes=[(b.stash, "t8")])
                    p.emit("dve", lambda e, E=E: e.reduce_sum(b.stash[:, 363:364],
                                                              tmp8[:, 0:E], axis=AX.X),
                           reads=[(b.stash, "t8")], writes=[(b.stash, "g1s")])
                    p.emit("dve", lambda e, t32=t32: e.tensor_mul(
                               b.stash[:, 256 + t32:257 + t32], b.stash[:, 363:364], rr),
                           reads=[(b.stash, "g1s"), (b.stash, "rr")],
                           sreads=[(b.stash, "rr")],
                           writes=[(b.stash, ("g1", t32))])
                else:
                    p.emit("dve", lambda e, t32=t32: e.tensor_copy(
                               b.stash[:, 256 + t32:257 + t32], rr),
                           reads=[(b.stash, "rr")], writes=[(b.stash, ("g1", t32))])
            if cap_mode:
                pr = b.ps["pr"]
                ohf = ohsl
                p.emit("pe", lambda e, i=ohf, E=E: e.matmul(pr[0:1, 0:E],
                                                            b.ones_redf[:, 0:1], i[:, 0:E],
                                                            start=True, stop=True),
                       reads=[(b.stash, ("oh", t32)), (b.ones_redf, "a")],
                       writes=[(pr, "a")])
                p.emit("dve", lambda e, E=E: e.tensor_add(b.rows[0:1, 1536:1536 + E],
                                                          b.rows[0:1, 1536:1536 + E],
                                                          pr[0:1, 0:E]),
                       reads=[(pr, "a"), ("rows", "cnt")], writes=[("rows", "cnt")])
    if not cap_mode:
        return
    p.emit("dve", lambda e, E=E: e.tensor_copy(b.osf[0:1, 0:E], b.rows[0:1, 1536:1536 + E]),
           reads=[("rows", "cnt")], writes=[(b.osf, ("o", 0))])
    p.dma("qact", cc_in.ap()[0:1, 0:E], b.osf[0:1, 0:E], reads=[(b.osf, ("o", 0))],
          writes=[(cc_in.name, "a")])
    ccs = "cc0" if cc_idx == 0 else "cc1"
    pp = p
    ccin_tok = p.writers.get(cc_in.name, {}).get("a")
    def cc_full(e):
        if ccin_tok:
            e.wait_ge(pp.sems[ccin_tok[0]], ccin_tok[1])
        e.collective_compute(
            "AllGather", OP.bypass, replica_groups=[list(range(8))],
            ins=[cc_in.ap()[0:1, 0:E]],
            outs=[cc_out.ap()[0:8, 0:E]],
        ).then_inc(pp.sems[ccs], 1)
    p.q["pool"].append(cc_full)
    p.writers.setdefault(cc_out.name, {})["a"] = (ccs, 1)
    p.count[ccs] = 1
    p.dma("qsp", b.rows[0:1, 1600:1600 + 8 * E], cc_out.ap()[0:8, 0:E],
          reads=[(cc_out.name, "a")], writes=[("rows", "allg")])
    p.emit("dve", lambda e, E=E: e.tensor_mul(b.rows[0:1, 1664:1664 + 8 * E],
                                              b.rows[0:1, 1600:1600 + 8 * E],
                                              b.coremask[0:1, 0:8 * E]),
           reads=[("rows", "allg"), (b.coremask, "a")], writes=[("rows", "mskd")])
    p.emit("dve", lambda e, E=E: e.tensor_reduce(
               b.rows[0:1, 1728:1728 + E],
               bass.AP(b.rows, 1664, [[2048, 1], [1, E], [E, 8]]),
               axis=AX.X, op=OP.add),
           reads=[("rows", "mskd")], writes=[("rows", "offs")])
    p.emit("dve", lambda e, E=E: e.tensor_copy(b.rows[0:1, 1736:1736 + E],
                                               b.rows[0:1, 1728:1728 + E]),
           reads=[("rows", "offs")], writes=[("rows", "run")])
    for tt in range(NTT):
        for sub in range(4):
            t32 = tt * 4 + sub
            ohsl = b.stash[:, t32 * 8:(t32 + 1) * 8]
            pg = b.ps["pg"]
            p.emit("pe", lambda e, i=ohsl, E=E: e.matmul(pg[:, 0:E], b.sut[:, :],
                                                         i[:, 0:E], start=True, stop=False),
                   reads=[(b.stash, ("oh", t32)), (b.sut, "a")], writes=[(pg, "a")])
            p.emit("pe", lambda e, E=E: e.matmul(pg[:, 0:E], b.ones_row[0:1, :],
                                                 b.rows[0:1, 1736:1736 + E],
                                                 start=False, stop=True),
                   reads=[("rows", "run"), (b.ones_row, "a")], writes=[(pg, "a")])
            pr = b.ps["pr"]
            p.emit("pe", lambda e, i=ohsl, E=E: e.matmul(pr[0:1, 0:E], b.ones_redf[:, 0:1],
                                                         i[:, 0:E], start=True, stop=True),
                   reads=[(b.stash, ("oh", t32)), (b.ones_redf, "a")], writes=[(pr, "a")])
            p.emit("dve", lambda e, E=E: e.tensor_add(b.rows[0:1, 1736:1736 + E],
                                                      b.rows[0:1, 1736:1736 + E],
                                                      pr[0:1, 0:E]),
                   reads=[(pr, "a"), ("rows", "run")], writes=[("rows", "run")])
            sc = b.stash[:, 376:384]
            p.emit("dve", lambda e, i=ohsl, E=E: e.tensor_mul(sc[:, 0:E], pg[:, 0:E],
                                                              i[:, 0:E]),
                   reads=[(pg, "a"), (b.stash, ("oh", t32))], writes=[(b.stash, "po8")])
            p.emit("dve", lambda e, E=E: e.reduce_sum(b.stash[:, 364:365], sc[:, 0:E],
                                                      axis=AX.X),
                   reads=[(b.stash, "po8")], writes=[(b.stash, "pos")])
            p.emit("dve", lambda e, cap=capacity: e.tensor_scalar(
                       b.stash[:, 365:366], b.stash[:, 364:365], float(cap), None, OP.is_lt),
                   reads=[(b.stash, "pos")], writes=[(b.stash, "kept")])
            wt = b.stash[:, 424:436]
            if variant == "top1cap":
                p.emit("dve", lambda e, t32=t32: e.tensor_mul(
                           b.stash[:, 366:367], b.stash[:, 256 + t32:257 + t32],
                           b.stash[:, 365:366]),
                       reads=[(b.stash, ("g1", t32)), (b.stash, "kept")],
                       writes=[(b.stash, "wv")])
                if coef_dram is not None:
                    p.emit("dve", lambda e, t32=t32: e.tensor_mul(
                               b.stash[:, 366:367], b.stash[:, 366:367],
                               b.stash[:, 512 + t32:513 + t32]),
                           reads=[(b.stash, ("c2", t32)), (b.stash, "wv")],
                           writes=[(b.stash, "wv")])
                p.emit("dve", lambda e, i=ohsl, E=E: e.tensor_tensor(
                           wt[:, 0:E], i[:, 0:E], _bc(b, 366, E), OP.mult),
                       reads=[(b.stash, ("oh", t32)), (b.stash, "wv")],
                       writes=[(b.stash, "wt")])
                if coef_dram is not None:
                    p.emit("dve", lambda e, t32=t32: e.tensor_copy(
                               b.stash[:, 416:417], b.stash[:, 448 + t32:449 + t32]),
                           reads=[(b.stash, ("c1", t32))], writes=[(b.stash, "xtr")])
                    _route_transpose(p, b, t32, E, 1, wrows)
                else:
                    _route_transpose(p, b, t32, E, 0, wrows)
            else:
                p.emit("dve", lambda e, t32=t32: e.tensor_mul(
                           b.stash[:, 366:367], b.stash[:, 256 + t32:257 + t32],
                           b.stash[:, 365:366]),
                       reads=[(b.stash, ("g1", t32)), (b.stash, "kept")],
                       writes=[(b.stash, "wv")])
                p.emit("dve", lambda e, i=ohsl, E=E: e.tensor_tensor(
                           wt[:, 0:E], i[:, 0:E], _bc(b, 366, E), OP.mult),
                       reads=[(b.stash, ("oh", t32)), (b.stash, "wv")],
                       writes=[(b.stash, "wt")])
                p.emit("dve", lambda e: e.tensor_scalar(
                           b.stash[:, 367:368], b.stash[:, 365:366], -1.0, 1.0,
                           OP.mult, OP.add),
                       reads=[(b.stash, "kept")], writes=[(b.stash, "ikept")])
                p.emit("dve", lambda e, t32=t32: e.tensor_mul(
                           b.stash[:, 416:417], b.stash[:, 256 + t32:257 + t32],
                           b.stash[:, 367:368]),
                       reads=[(b.stash, ("g1", t32)), (b.stash, "ikept")],
                       writes=[(b.stash, "xtr")])
                _route_transpose(p, b, t32, E, 1, wrows)


def _route_transpose(p, b, t32, E, extra, wrows=None):
    wt = b.stash[:, 424:436]
    ncol = E + extra
    if extra:
        p.emit("dve", lambda e: e.tensor_copy(wt[:, E:E + 1], b.stash[:, 416:417]),
               reads=[(b.stash, "xtr")], writes=[(b.stash, "wt")])
    pc = b.ps["pc"]
    p.emit("pe", lambda e, n=ncol: e.transpose(pc[0:n, 0:128], wt[:, 0:n], b.idf[:, :]),
           reads=[(b.stash, "wt"), (b.idf, "a")], writes=[(pc, "a")])
    p.emit("dve", lambda e, n=ncol, t32=t32: e.tensor_copy(
               b.wfm[0:n, t32 * 128:(t32 + 1) * 128], pc[0:n, 0:128]),
           reads=[(pc, "a")], writes=[(b.wfm, ("w", t32))])
    if wrows is not None:
        p.dma("qact", wrows.ap()[0:ncol, t32 * 128:(t32 + 1) * 128],
              b.wfm[0:ncol, t32 * 128:(t32 + 1) * 128],
              reads=[(b.wfm, ("w", t32))],
              writes=dk(wrows, 0, ncol, t32 * 128, (t32 + 1) * 128))



def moe_dense(p, b, t, *, x_bf, Et, Dh, act, w1_dram, w2_dram, b1_pcol, b2_pk,
              acc_dram, out_dram, final_adds=(), l3f_pass=False, nrows_b=None,
              Din=1024, Dout=1024, tag="moe"):
    nkc = Din // 128
    noc = Dout // 128
    if nrows_b is None:
        nrows_b = Et
    if b2_pk is not None:
        for r in range(nrows_b):
            p.dma("qsp", b.wstg[0:1, 0:Dout], b2_pk.ap()[r:r + 1, 0:Dout],
                  writes=[(b.wstg, "a")])
            p.emit("dve", lambda e: e.tensor_copy(b.rowbf[0:1, 0:Dout],
                                                  b.wstg[0:1, 0:Dout]),
                   reads=[(b.wstg, "a")], writes=[(b.rowbf, "a")])
            p.dma("qsp", b.b2s[r:r + 1, 0:Dout], b.rowbf[0:1, 0:Dout],
                  reads=[(b.rowbf, "a")], writes=[(b.b2s, "a")])
    ngq = (Dh // 1024) if Dh else 1
    W1B = 0
    W2B = 8192
    Y1B = 16384
    steps = [(e, g) for e in range(Et) for g in range(ngq)]
    for si, (e, g) in enumerate(steps):
        first = si == 0
        last = si == len(steps) - 1
        if Dh:
            for kc in range(nkc):
                stg, skey = wstg_slot(b)
                p.dma("qsp", stg[:, 0:1024],
                      w1_dram.ap()[e * Din + kc * 128: e * Din + (kc + 1) * 128,
                                   g * 1024:(g + 1) * 1024],
                      writes=[(b.wstg, skey)])
                dst = b.U1[:, W1B + kc * 1024: W1B + (kc + 1) * 1024]
                p.emit("dve", lambda e2, s=stg[:, 0:1024], d=dst: e2.tensor_copy(d, s),
                       reads=[(b.wstg, skey)],
                       writes=u1g(W1B + kc * 1024, W1B + (kc + 1) * 1024))
            if b1_pcol is not None:
                p.dma("qsp", b.bias[:, 0:8],
                      b1_pcol.ap()[:, e * (Dh // 128) + g * 8: e * (Dh // 128) + g * 8 + 8],
                      writes=[(b.bias, "a")])
            for ic in range(8):
                stg, skey = wstg_slot(b)
                p.dma("qsp", stg[:, 0:Dout],
                      w2_dram.ap()[e * Dh + g * 1024 + ic * 128: e * Dh + g * 1024 + (ic + 1) * 128,
                                   0:Dout],
                      writes=[(b.wstg, skey)])
                dst = b.U1[:, W2B + ic * Dout: W2B + (ic + 1) * Dout]
                p.emit("dve", lambda e2, s=stg[:, 0:Dout], d=dst: e2.tensor_copy(d, s),
                       reads=[(b.wstg, skey)],
                       writes=u1g(W2B + ic * Dout, W2B + (ic + 1) * Dout))
        else:
            for kc in range(nkc):
                stg, skey = wstg_slot(b)
                p.dma("qsp", stg[:, 0:Dout],
                      w2_dram.ap()[e * Din + kc * 128: e * Din + (kc + 1) * 128, 0:Dout],
                      writes=[(b.wstg, skey)])
                dst = b.U1[:, W2B + kc * Dout: W2B + (kc + 1) * Dout]
                p.emit("dve", lambda e2, s=stg[:, 0:Dout], d=dst: e2.tensor_copy(d, s),
                       reads=[(b.wstg, skey)],
                       writes=u1g(W2B + kc * Dout, W2B + (kc + 1) * Dout))
        w1keys = u1g(W1B, W1B + nkc * 1024)
        w2keys = u1g(W2B, W2B + 8 * Dout)
        for tt in range(NTT):
            for kc in range(nkc):
                src = x_bf.ap()[kc * 128:(kc + 1) * 128, tt * TTW:(tt + 1) * TTW]
                p.dma("qsp", b.XB[:, kc * TTW:(kc + 1) * TTW], src,
                      reads=dk(x_bf, kc * 128, (kc + 1) * 128, tt * TTW, (tt + 1) * TTW),
                      writes=[(b.XB, ("g", kc))])
            p.dma("qsp", b.rowbf[0:1, 0:512],
                  t["wrows"].ap()[e:e + 1, tt * TTW:(tt + 1) * TTW],
                  reads=dk(t["wrows"], e, e + 1, tt * TTW, (tt + 1) * TTW),
                  writes=[(b.rowbf, "a")])
            po = b.ps["po"]
            p.emit("pe", lambda e2: e2.matmul(po[:, :], b.ones_rowb[0:1, :],
                                              b.rowbf[0:1, 0:512], start=True, stop=True),
                   reads=[(b.rowbf, "a"), (b.ones_rowb, "a")], writes=[(po, "a")])
            p.emit("dve", lambda e2: e2.tensor_copy(b.bsb[:, :], po[:, :]),
                   reads=[(po, "a")], writes=[(b.bsb, "a")])
            if l3f_pass and first:
                p.dma("qsp", b.rowbf[0:1, 512:1024],
                      t["wrows"].ap()[Et:Et + 1, tt * TTW:(tt + 1) * TTW],
                      reads=dk(t["wrows"], Et, Et + 1, tt * TTW, (tt + 1) * TTW),
                      writes=[(b.rowbf, "p")])
                p.emit("pe", lambda e2: e2.matmul(po[:, :], b.ones_rowb[0:1, :],
                                                  b.rowbf[0:1, 512:1024],
                                                  start=True, stop=True),
                       reads=[(b.rowbf, "p"), (b.ones_rowb, "a")], writes=[(po, "a")])
                p.emit("dve", lambda e2: e2.tensor_copy(b.bsb2[:, :], po[:, :]),
                       reads=[(po, "a")], writes=[(b.bsb2, "a")])
            if Dh:
                for ic in range(8):
                    ps1 = b.ps["pa"] if ic % 2 == 0 else b.ps["pb"]
                    for kc in range(nkc):
                        lhsT = b.U1[:, W1B + kc * 1024 + ic * 128: W1B + kc * 1024 + ic * 128 + 128]
                        rhs = b.XB[:, kc * TTW:(kc + 1) * TTW]
                        p.emit("pe", lambda e2, o=ps1, l=lhsT, r=rhs, s=(kc == 0),
                                      st=(kc == nkc - 1):
                                   e2.matmul(o[:, :], l, r, start=s, stop=st),
                               reads=w1keys + [(b.XB, ("g", kc))], writes=[(ps1, "a")])
                    ysl = b.U1[:, Y1B + ic * 512: Y1B + (ic + 1) * 512]
                    if b1_pcol is not None:
                        p.emit("act", lambda e2, o=ysl, i=ps1, ic=ic: e2.activation(
                                   out=o, in_=i[:, :], func=act, bias=b.bias[:, ic:ic + 1]),
                               reads=[(ps1, "a"), (b.bias, "a")],
                               writes=u1g(Y1B + ic * 512, Y1B + (ic + 1) * 512))
                    else:
                        p.emit("act", lambda e2, o=ysl, i=ps1: e2.activation(
                                   out=o, in_=i[:, :], func=act),
                               reads=[(ps1, "a")],
                               writes=u1g(Y1B + ic * 512, Y1B + (ic + 1) * 512))
                y1keys = u1g(Y1B, Y1B + 4096)
            for oc in range(noc):
                ps2 = b.ps["pc"] if oc % 2 == 0 else b.ps["pr"]
                if Dh:
                    for ic in range(8):
                        lhsT = b.U1[:, W2B + ic * Dout + oc * 128: W2B + ic * Dout + oc * 128 + 128]
                        rhs = b.U1[:, Y1B + ic * 512: Y1B + (ic + 1) * 512]
                        p.emit("pe", lambda e2, o=ps2, l=lhsT, r=rhs, s=(ic == 0),
                                      st=(ic == 7):
                                   e2.matmul(o[:, :], l, r, start=s, stop=st),
                               reads=w2keys + y1keys, writes=[(ps2, "a")])
                else:
                    for kc in range(nkc):
                        lhsT = b.U1[:, W2B + kc * Dout + oc * 128: W2B + kc * Dout + oc * 128 + 128]
                        rhs = b.XB[:, kc * TTW:(kc + 1) * TTW]
                        p.emit("pe", lambda e2, o=ps2, l=lhsT, r=rhs, s=(kc == 0),
                                      st=(kc == nkc - 1):
                                   e2.matmul(o[:, :], l, r, start=s, stop=st),
                               reads=w2keys + [(b.XB, ("g", kc))], writes=[(ps2, "a")])
                ob = oc % 4
                tsl = b.osf[:, ob * 512:(ob + 1) * 512]
                p.emit("dve", lambda e2, o=tsl, i=ps2: e2.tensor_mul(o, i[:, :], b.bsb[:, :]),
                       reads=[(ps2, "a"), (b.bsb, "a")], writes=[(b.osf, ("o", ob))])
                if first:
                    if b2_pk is not None:
                        pg = b.ps["pg"]
                        p.emit("pe", lambda e2, oc=oc, tt=tt, n=nrows_b: e2.matmul(
                                   pg[:, :], b.b2s[0:n, oc * 128:(oc + 1) * 128],
                                   b.wfm[0:n, tt * TTW:(tt + 1) * TTW],
                                   start=True, stop=True),
                               reads=[(b.b2s, "a")] +
                                     [(b.wfm, ("w", tt * 4 + s)) for s in range(4)],
                               writes=[(pg, "a")])
                        p.emit("dve", lambda e2, o=tsl: e2.tensor_add(o, o, pg[:, :]),
                               reads=[(pg, "a"), (b.osf, ("o", ob))],
                               writes=[(b.osf, ("o", ob))])
                    if l3f_pass:
                        xsl = b.XB[:, oc * TTW:(oc + 1) * TTW]
                        psl = b.S[:, 2048 + (ob % 4) * 512: 2048 + (ob % 4) * 512 + 512]
                        p.emit("dve", lambda e2, o=psl, x=xsl: e2.tensor_mul(o, x, b.bsb2[:, :]),
                               reads=[(b.XB, ("g", oc)), (b.bsb2, "a")],
                               writes=[(b.S, ("g", 4 + ob))])
                        p.emit("dve", lambda e2, o=tsl, x2=psl: e2.tensor_add(o, o, x2),
                               reads=[(b.S, ("g", 4 + ob)), (b.osf, ("o", ob))],
                               writes=[(b.osf, ("o", ob))])
                else:
                    asl = b.S[:, 4096 + ob * 512: 4096 + (ob + 1) * 512]
                    p.dma("qsp", asl,
                          acc_dram.ap()[oc * 128:(oc + 1) * 128, tt * TTW:(tt + 1) * TTW],
                          reads=dk(acc_dram, oc * 128, (oc + 1) * 128, tt * TTW,
                                   (tt + 1) * TTW),
                          writes=[(b.S, ("g", 8 + ob))])
                    p.emit("dve", lambda e2, o=tsl, a=asl: e2.tensor_add(o, o, a),
                           reads=[(b.S, ("g", 8 + ob)), (b.osf, ("o", ob))],
                           writes=[(b.osf, ("o", ob))])
                if last:
                    for fi, fd in enumerate(final_adds):
                        fsl = b.S[:, 6144 + ((ob + fi) % 4) * 512: 6144 + ((ob + fi) % 4) * 512 + 512]
                        p.dma("qsp", fsl,
                              fd.ap()[oc * 128:(oc + 1) * 128, tt * TTW:(tt + 1) * TTW],
                              reads=dk(fd, oc * 128, (oc + 1) * 128, tt * TTW, (tt + 1) * TTW),
                              writes=[(b.S, ("g", 12 + (ob + fi) % 4))])
                        p.emit("dve", lambda e2, o=tsl, a=fsl: e2.tensor_add(o, o, a),
                               reads=[(b.S, ("g", 12 + (ob + fi) % 4)), (b.osf, ("o", ob))],
                               writes=[(b.osf, ("o", ob))])
                    tgt = out_dram
                else:
                    tgt = acc_dram
                p.dma("qact", tgt.ap()[oc * 128:(oc + 1) * 128, tt * TTW:(tt + 1) * TTW],
                      tsl, reads=[(b.osf, ("o", ob))],
                      writes=dk(tgt, oc * 128, (oc + 1) * 128, tt * TTW, (tt + 1) * TTW))



def ssm(p, b, t, pj2, A_pcol, C_pcol, out_bf):
    p.dma("qsp", b.dec[:, 16:24], A_pcol.ap()[:, 0:8], writes=[(b.dec, "A")])
    p.dma("qsp", b.dec[:, 8:16], C_pcol.ap()[:, 0:8], writes=[(b.dec, "C")])
    p.emit("act", lambda e: e.activation(out=b.dec[:, 0:8], in_=b.dec[:, 16:24],
                                         func=AF.Sigmoid),
           reads=[(b.dec, "A")], writes=[(b.dec, "d")])
    for cc in range(8):
        p.dma("qsp", b.S[:, 0:4096], pj2.ap()[cc * 128:(cc + 1) * 128, 0:T],
              reads=dk(pj2, cc * 128, (cc + 1) * 128, 0, T),
              writes=[(b.S, ("g", gg)) for gg in range(8)])
        dec_ap = bass.AP(b.dec, cc, [[32, 128], [0, T]])
        p.emit("dve", lambda e, d=dec_ap: e.tensor_tensor_scan(
                   b.S[:, 4096:8192], d, b.S[:, 0:4096], 0.0, OP.mult, OP.add),
               reads=[(b.S, ("g", gg)) for gg in range(8)] + [(b.dec, "d")],
               writes=[(b.S, ("g", gg)) for gg in range(8, 16)])
        p.dma("qsp", b.S[:, 0:4096], pj2.ap()[1024 + cc * 128: 1024 + (cc + 1) * 128, 0:T],
              reads=dk(pj2, 1024 + cc * 128, 1024 + (cc + 1) * 128, 0, T),
              writes=[(b.S, ("g", gg)) for gg in range(8)])
        p.emit("act", lambda e: e.activation(out=b.S[:, 0:4096], in_=b.S[:, 0:4096],
                                             func=AF.Sigmoid),
               reads=[(b.S, ("g", gg)) for gg in range(8)],
               writes=[(b.S, ("g", gg)) for gg in range(8)])
        for tt in range(NTT):
            ysl = b.osf[:, (tt % 4) * 512:(tt % 4) * 512 + 512]
            p.emit("dve", lambda e, o=ysl, tt=tt: e.tensor_mul(
                       o, b.S[:, 4096 + tt * 512:4096 + (tt + 1) * 512],
                       b.S[:, tt * 512:(tt + 1) * 512]),
                   reads=[(b.S, ("g", 8 + tt)), (b.S, ("g", tt))],
                   writes=[(b.osf, ("o", tt % 4))])
            obf = b.osb[:, (tt % 4) * 512:(tt % 4) * 512 + 512]
            p.emit("dve", lambda e, o=obf, i=ysl, cc=cc: e.tensor_scalar(
                       o, i, b.dec[:, 8 + cc:9 + cc], None, OP.mult),
                   reads=[(b.osf, ("o", tt % 4)), (b.dec, "C")],
                   writes=[(b.osb, ("o", tt % 4))])
            p.dma("qact", out_bf.ap()[cc * 128:(cc + 1) * 128, tt * TTW:(tt + 1) * TTW],
                  obf, reads=[(b.osb, ("o", tt % 4))],
                  writes=dk(out_bf, cc * 128, (cc + 1) * 128, tt * TTW, (tt + 1) * TTW))



def emit_l0a(p, b, t):
    y, lnb = t["y"], t["lnb"]
    norm_fm(p, b, lnb, y, 1024)
    gemm_tm(p, b, lnb, t["w_l0a_q"], 1024, 1024, out_fm=t["qf"],
            rope="il", rope_tab=t["c_ropeA_q"])
    gemm_tm(p, b, lnb, t["w_l0a_k"], 1024, 1024, out_fm=t["kf"],
            rope="il", rope_tab=t["c_ropeA_k"])
    gemm_tm(p, b, lnb, t["w_l0a_v"], 1024, 1024, out_tm=t["vtm"])
    attention(p, b, out_dram=t["ao"], q_fm=t["qf"], k_fm=t["kf"], v_tm=t["vtm"],
              mode="none")
    gemm_fm(p, b, y, t["ao"], t["w_l0a_o"], 1024, 1024, resid_dram=y)


def emit_l0f(p, b, t):
    y, lnb, lnf, aob = t["y"], t["lnb"], t["lnf"], t["aob"]
    norm_fm(p, b, lnb, y, 1024, out_f32=lnf)
    norm_fm(p, b, aob, lnf, 1024, eps=1e-12, gamma_pcol=t["p_l0f_nw"],
            beta_pcol=t["p_l0f_nb"])
    routing(p, b, t, x_bf=aob, wg_dram=t["w_l0f_g"], E=8, variant="top1cap",
            capacity=4096, coef_dram=t["w_l0f_c"], cc_idx=0,
            cc_in=t["ccin0"], cc_out=t["ccout0"])
    moe_dense(p, b, t, x_bf=aob, Et=9, Dh=4096, act=AF.Gelu_apprx_tanh,
              w1_dram=t["w_l0f_1"], w2_dram=t["w_l0f_2"], b1_pcol=t["p_l0f_b1"],
              b2_pk=t["b_l0f_2"], acc_dram=t["macc"], out_dram=y,
              final_adds=(lnf, y), nrows_b=9)


def emit_dbg_rt(p, b, t):
    y, lnb = t["y"], t["lnb"]
    norm_fm(p, b, lnb, y, 1024)
    routing(p, b, t, x_bf=lnb, wg_dram=t["w_l1f_g"], E=4, variant="top1",
            bias_row=t["r_l1f_gb"])


def emit_l1a(p, b, t):
    y, lnb = t["y"], t["lnb"]
    norm_fm(p, b, lnb, y, 1024)
    gemm_fm(p, b, t["shf"], lnb, t["w_l1a_sh"], 1024, 128,
            bias_pcol=t["p_l1a_shb"])
    gemm_fm(p, b, t["qpf"], lnb, t["w_l1a_q"], 1024, 768,
            bias_pcol=t["p_l1a_bq"], scale=0.125, bias_scale=0.125)
    gemm_fm(p, b, t["kpf"], lnb, t["w_l1a_k"], 1024, 768,
            bias_pcol=t["p_l1a_bk"])
    gemm_tm(p, b, lnb, t["w_l1a_v"], 1024, 1024, out_tm=t["vtm"],
            bias_row=t["r_l1a_bv"])

    def q_prep(h):
        p.dma("qsp", b.qsb[0:16, :], t["shf"].ap()[0:16, :],
              reads=dk(t["shf"], 0, 16, 0, T), writes=[(b.qsb, "hd")])
        p.emit("act", lambda e: e.activation(out=b.qsb[0:16, :], in_=b.qsb[0:16, :],
                                             func=AF.Copy, scale=0.125),
               reads=[(b.qsb, "hd")], writes=[(b.qsb, "hd")])
        p.dma("qsp", b.qsb[16:64, :], t["qpf"].ap()[h * 48:(h + 1) * 48, :],
              reads=dk(t["qpf"], h * 48, (h + 1) * 48, 0, T), writes=[(b.qsb, "hd")])

    def k_prep(h):
        p.dma("qsp", b.kaug[0:16, :], t["shf"].ap()[0:16, :],
              reads=dk(t["shf"], 0, 16, 0, T), writes=[(b.kaug, "hd")])
        p.dma("qsp", b.kaug[16:64, :], t["kpf"].ap()[h * 48:(h + 1) * 48, :],
              reads=dk(t["kpf"], h * 48, (h + 1) * 48, 0, T), writes=[(b.kaug, "hd")])

    attention(p, b, out_dram=t["ao"], v_tm=t["vtm"], mode="causal",
              q_prep=q_prep, k_prep=k_prep)
    gemm_fm(p, b, y, t["ao"], t["w_l1a_o"], 1024, 1024,
            bias_pcol=t["p_l1a_bo"], resid_dram=y)


def emit_l1f(p, b, t):
    y, lnb, u = t["y"], t["lnb"], t["u"]
    norm_fm(p, b, lnb, y, 1024, out_f32=u)
    norm_fm(p, b, t["r1b"], u, 1024, kind="rms", eps=1e-6, gamma_pcol=t["p_l1f_n1"])
    gemm_tm(p, b, t["r1b"], t["w_l1f_q"], 1024, 1024, out_fm=t["qf"],
            rope="half", rope_tab=t["c_ropeB_q"])
    gemm_fm(p, b, t["latr"], t["r1b"], t["w_l1f_dkv"], 1024, 512, out_dtype="f32")
    norm_fm(p, b, t["latb"], t["latr"], 512, eps=1e-5, gamma_pcol=t["p_l1f_lng"],
            beta_pcol=t["p_l1f_lnb"])
    gemm_tm(p, b, t["latb"], t["w_l1f_uk"], 512, 1024, out_fm=t["kf"],
            rope="half", rope_tab=t["c_ropeB_k"])
    gemm_tm(p, b, t["latb"], t["w_l1f_uv"], 512, 1024, out_tm=t["vtm"])
    attention(p, b, out_dram=t["ao"], q_fm=t["qf"], k_fm=t["kf"], v_tm=t["vtm"],
              mode="causal")
    gemm_fm(p, b, u, t["ao"], t["w_l1f_o"], 1024, 1024, resid_dram=u)
    norm_fm(p, b, t["r2b"], u, 1024, kind="rms", eps=1e-6, gamma_pcol=t["p_l1f_n2"])
    routing(p, b, t, x_bf=t["r2b"], wg_dram=t["w_l1f_g"], E=4, variant="top1",
            bias_row=t["r_l1f_gb"])
    moe_dense(p, b, t, x_bf=t["r2b"], Et=4, Dh=4096, act=AF.Relu,
              w1_dram=t["w_l1f_1"], w2_dram=t["w_l1f_2"], b1_pcol=t["p_l1f_b1"],
              b2_pk=t["b_l1f_2"], acc_dram=t["macc"], out_dram=u, final_adds=(u,))
    addpass(p, b, y, y, u)


def emit_l2a(p, b, t):
    y, lnb = t["y"], t["lnb"]
    norm_fm(p, b, lnb, y, 1024)
    gemm_fm(p, b, t["pj2"], lnb, t["w_l2a_in"], 1024, 2048,
            bias_pcol=t["p_l2a_bin"], out_dtype="f32")
    ssm(p, b, t, t["pj2"], t["p_l2a_A"], t["p_l2a_C"], t["sdy"])
    gemm_fm(p, b, y, t["sdy"], t["w_l2a_out"], 1024, 1024,
            bias_pcol=t["p_l2a_bout"], resid_dram=y)


def emit_l2f(p, b, t):
    y, lnb = t["y"], t["lnb"]
    norm_fm(p, b, lnb, y, 1024)
    routing(p, b, t, x_bf=lnb, wg_dram=t["w_l2f_g"], E=8, variant="top2")
    moe_dense(p, b, t, x_bf=lnb, Et=8, Dh=2048, act=AF.Silu,
              w1_dram=t["w_l2f_1"], w2_dram=t["w_l2f_2"], b1_pcol=None,
              b2_pk=None, acc_dram=t["macc"], out_dram=y, final_adds=(y,))


def emit_l3a(p, b, t):
    y, lnb = t["y"], t["lnb"]
    norm_fm(p, b, lnb, y, 1024)
    gemm_fm(p, b, t["qf"], lnb, t["w_l3a_q"], 1024, 1024, scale=0.125)
    gemm_fm(p, b, t["kf"], lnb, t["w_l3a_k"], 1024, 1024)
    gemm_tm(p, b, lnb, t["w_l3a_v"], 1024, 1024, out_tm=t["vtm"])
    attention(p, b, out_dram=t["ao"], q_fm=t["qf"], k_fm=t["kf"], v_tm=t["vtm"],
              mode="causal")
    gemm_fm(p, b, y, t["ao"], t["w_l3a_o"], 1024, 1024, resid_dram=y)


def emit_l3f(p, b, t):
    y, lnb, lnf = t["y"], t["lnb"], t["lnf"]
    norm_fm(p, b, lnb, y, 1024, out_f32=lnf)
    gemm_fm(p, b, t["qf"], lnb, t["w_l3f_q"], 1024, 1024,
            bias_pcol=t["p_l3f_bq"], scale=1.0 / 32.0, bias_scale=1.0 / 32.0)
    gemm_fm(p, b, t["kf"], lnb, t["w_l3f_k"], 1024, 1024, bias_pcol=t["p_l3f_bk"])
    gemm_tm(p, b, lnb, t["w_l3f_v"], 1024, 1024, out_tm=t["vtm"],
            bias_row=t["r_l3f_bv"])
    attention(p, b, out_dram=t["ao"], q_fm=t["qf"], k_fm=t["kf"], v_tm=t["vtm"],
              mode="zmask")
    routing(p, b, t, x_bf=t["ao"], wg_dram=t["w_l3f_g"], E=8, variant="switch",
            bias_row=t["r_l3f_gb"], capacity=4915, cc_idx=1,
            cc_in=t["ccin1"], cc_out=t["ccout1"])
    moe_dense(p, b, t, x_bf=t["ao"], Et=8, Dh=None, act=None,
              w1_dram=None, w2_dram=t["w_l3f_e"], b1_pcol=None,
              b2_pk=t["b_l3f_e"], acc_dram=t["macc"], out_dram=t["x2t"],
              final_adds=(lnf,), l3f_pass=True)
    norm_fm(p, b, lnb, t["x2t"], 1024, eps=1e-5, gamma_pcol=t["p_l3f_g"],
            beta_pcol=t["p_l3f_b"], final_resid=y, final_out=y)


def emit_l0a_v(p, b, t):
    y, lnb = t["y"], t["lnb"]
    norm_fm(p, b, lnb, y, 1024)
    gemm_tm(p, b, lnb, t["w_l0a_v"], 1024, 1024, out_tm=t["vtm"])


def emit_l0a_q(p, b, t):
    y, lnb = t["y"], t["lnb"]
    norm_fm(p, b, lnb, y, 1024)
    gemm_tm(p, b, lnb, t["w_l0a_q"], 1024, 1024, out_fm=t["qf"],
            rope="il", rope_tab=t["c_ropeA_q"])


def emit_dbg_ropetm(p, b, t):
    y, lnb = t["y"], t["lnb"]
    norm_fm(p, b, lnb, y, 1024)
    gemm_tm(p, b, lnb, t["w_l0a_q"], 1024, 1024, out_tm=t["vtm"],
            rope="il", rope_tab=t["c_ropeA_q"])


def emit_dbg_tp(p, b, t):
    y, lnb = t["y"], t["lnb"]
    norm_fm(p, b, lnb, y, 1024)
    gemm_tm(p, b, lnb, t["w_l0a_q"], 1024, 1024, out_fm=t["qf"])


def emit_l0a_qkv(p, b, t):
    y, lnb = t["y"], t["lnb"]
    norm_fm(p, b, lnb, y, 1024)
    gemm_tm(p, b, lnb, t["w_l0a_q"], 1024, 1024, out_fm=t["qf"],
            rope="il", rope_tab=t["c_ropeA_q"])
    gemm_tm(p, b, lnb, t["w_l0a_k"], 1024, 1024, out_fm=t["kf"],
            rope="il", rope_tab=t["c_ropeA_k"])
    gemm_tm(p, b, lnb, t["w_l0a_v"], 1024, 1024, out_tm=t["vtm"])


def emit_l0a_att(p, b, t):
    emit_l0a_qkv(p, b, t)
    attention(p, b, out_dram=t["ao"], q_fm=t["qf"], k_fm=t["kf"], v_tm=t["vtm"],
              mode="none")


BLOCK_EMIT = [("dbg_rt", emit_dbg_rt), ("dbg_ropetm", emit_dbg_ropetm), ("dbg_tp", emit_dbg_tp),
              ("l0a_v", emit_l0a_v), ("l0a_q", emit_l0a_q),
              ("l0a_qkv", emit_l0a_qkv), ("l0a_att", emit_l0a_att),
              ("l0a", emit_l0a), ("l0f", emit_l0f), ("l1a", emit_l1a),
              ("l1f", emit_l1f), ("l2a", emit_l2a), ("l2f", emit_l2f),
              ("l3a", emit_l3a), ("l3f", emit_l3f)]


WEIGHT_SPECS = [
    ("w_l0a_q", [1024, 1024]), ("w_l0a_k", [1024, 1024]),
    ("w_l0a_v", [1024, 1024]), ("w_l0a_o", [1024, 1024]),
    ("c_ropeA_q", [4096, 512]), ("c_ropeA_k", [4096, 512]),
    ("p_l0f_nw", [128, 8]), ("p_l0f_nb", [128, 8]),
    ("w_l0f_g", [1024, 8]), ("w_l0f_c", [1024, 2]),
    ("w_l0f_1", [9 * 1024, 4096]), ("p_l0f_b1", [128, 9 * 32]),
    ("w_l0f_2", [9 * 4096, 1024]), ("b_l0f_2", [9, 1024]),
    ("w_l1a_sh", [1024, 128]), ("p_l1a_shb", [128, 1]),
    ("w_l1a_q", [1024, 768]), ("p_l1a_bq", [128, 6]),
    ("w_l1a_k", [1024, 768]), ("p_l1a_bk", [128, 6]),
    ("w_l1a_v", [1024, 1024]), ("r_l1a_bv", [1, 1024]),
    ("w_l1a_o", [1024, 1024]), ("p_l1a_bo", [128, 8]),
    ("p_l1f_n1", [128, 8]), ("p_l1f_n2", [128, 8]),
    ("w_l1f_q", [1024, 1024]), ("c_ropeB_q", [4096, 512]), ("c_ropeB_k", [4096, 512]),
    ("w_l1f_dkv", [1024, 512]), ("p_l1f_lng", [128, 4]), ("p_l1f_lnb", [128, 4]),
    ("w_l1f_uk", [512, 1024]), ("w_l1f_uv", [512, 1024]),
    ("w_l1f_o", [1024, 1024]),
    ("w_l1f_g", [1024, 4]), ("r_l1f_gb", [1, 4]),
    ("w_l1f_1", [4 * 1024, 4096]), ("p_l1f_b1", [128, 4 * 32]),
    ("w_l1f_2", [4 * 4096, 1024]), ("b_l1f_2", [4, 1024]),
    ("w_l2a_in", [1024, 2048]), ("p_l2a_bin", [128, 16]),
    ("p_l2a_A", [128, 8]), ("p_l2a_C", [128, 8]),
    ("w_l2a_out", [1024, 1024]), ("p_l2a_bout", [128, 8]),
    ("w_l2f_g", [1024, 8]),
    ("w_l2f_1", [8 * 1024, 2048]), ("w_l2f_2", [8 * 2048, 1024]),
    ("w_l3a_q", [1024, 1024]), ("w_l3a_k", [1024, 1024]),
    ("w_l3a_v", [1024, 1024]), ("w_l3a_o", [1024, 1024]),
    ("w_l3f_q", [1024, 1024]), ("p_l3f_bq", [128, 8]),
    ("w_l3f_k", [1024, 1024]), ("p_l3f_bk", [128, 8]),
    ("w_l3f_v", [1024, 1024]), ("r_l3f_bv", [1, 1024]),
    ("w_l3f_g", [1024, 8]), ("r_l3f_gb", [1, 8]),
    ("w_l3f_e", [8 * 1024, 1024]), ("b_l3f_e", [8, 1024]),
    ("p_l3f_g", [128, 8]), ("p_l3f_b", [128, 8]),
]

SCRATCH_SPECS = [
    ("lnb", [1024, T], "bf"), ("lnf", [1024, T], "f32"), ("aob", [1024, T], "bf"),
    ("u", [1024, T], "f32"), ("r1b", [1024, T], "bf"), ("r2b", [1024, T], "bf"),
    ("qf", [1024, T], "bf"), ("kf", [1024, T], "bf"), ("vtm", [T, 1024], "bf"),
    ("ao", [1024, T], "bf"), ("shf", [128, T], "bf"), ("qpf", [768, T], "bf"),
    ("kpf", [768, T], "bf"), ("latr", [512, T], "f32"), ("latb", [512, T], "bf"),
    ("pj2", [2048, T], "f32"), ("sdy", [1024, T], "bf"), ("macc", [1024, T], "f32"),
    ("x2t", [1024, T], "f32"),
    ("wrows", [16, T], "bf"), ("rtdbg", [128, 640], "f32"), ("ccin0", [1, 8], "f32"), ("ccout0", [8, 8], "f32"),
    ("ccin1", [1, 8], "f32"), ("ccout1", [8, 8], "f32"),
]


def build_module(blocks, core_id=0):
    from contextlib import ExitStack
    import os
    nc = bass.Bass("TRN2", target_bir_lowering=False, debug=False,
                   detect_race_conditions=bool(os.environ.get("RACEDET")))
    stack = ExitStack()
    with stack:
        p = P(nc)
        t = {}
        t["xin"] = p.dram_tensor("xin", [1024, T], F32, kind="ExternalInput")
        t["y"] = p.dram_tensor("y", [1024, T], F32, kind="ExternalOutput")
        for name, shape in WEIGHT_SPECS:
            t[name] = p.dram_tensor(name, shape, F32, kind="ExternalInput")
        for name, shape, dt_ in SCRATCH_SPECS:
            t[name] = p.dram_tensor(name, shape, BF16 if dt_ == "bf" else F32,
                                    kind="Internal")
        for name, shape, dtype in CONST_SPECS:
            t[name] = p.dram_tensor(name, shape, dtype, kind="ExternalInput")
        p.finalize_sems(stack)
        b = setup_ctx(p, stack)
        load_consts(p, b, t)
        for kc in range(8):
            p.dma("qsp", t["y"].ap()[kc * 128:(kc + 1) * 128, :],
                  t["xin"].ap()[kc * 128:(kc + 1) * 128, :],
                  writes=dk(t["y"], kc * 128, (kc + 1) * 128, 0, T))
        import os as _os
        dbg_stop = _os.environ.get("DBG_STOP", "")
        for name, fn in BLOCK_EMIT:
            if name in blocks:
                fn(p, b, t)
        dbg = _os.environ.get("DBG_COPY", "")
        if dbg:
            src = t[dbg]
            nr = src.shape[0] if hasattr(src, "shape") else 1024
            ncol = src.shape[1] if hasattr(src, "shape") else T
            nrr = min(128, nr)
            for kc in range(max(1, min(nr // 128, 8))):
                for tt in range(min(NTT, ncol // TTW)):
                    sl = b.osf[0:nrr, (tt % 4) * 512:(tt % 4) * 512 + 512]
                    p.dma("qsp", b.osb[0:nrr, (tt % 4) * 512:(tt % 4) * 512 + 512]
                          if src.dtype == BF16 else sl,
                          src.ap()[kc * 128:kc * 128 + nrr, tt * TTW:(tt + 1) * TTW],
                          reads=dk(src, kc * 128, kc * 128 + nrr, tt * TTW, (tt + 1) * TTW),
                          writes=[(b.osb, ("o", tt % 4)) if src.dtype == BF16
                                  else (b.osf, ("o", tt % 4))])
                    if src.dtype == BF16:
                        p.emit("dve", lambda e, o=sl, i=b.osb[0:nrr, (tt % 4) * 512:(tt % 4) * 512 + 512]:
                                   e.tensor_copy(o, i),
                               reads=[(b.osb, ("o", tt % 4))], writes=[(b.osf, ("o", tt % 4))])
                    p.dma("qact", t["y"].ap()[kc * 128:kc * 128 + nrr, tt * TTW:(tt + 1) * TTW],
                          sl, reads=[(b.osf, ("o", tt % 4))],
                          writes=dk(t["y"], kc * 128, kc * 128 + nrr, tt * TTW, (tt + 1) * TTW))
        p_drain(p)
        p.run_blocks()
    return nc, p



def rope_tables_A():
    pos = np.arange(4096, dtype=np.float64)
    i = np.arange(32, dtype=np.float64)
    coss, sins = [], []
    for s in (1.0, 4.0, 16.0):
        freqs = 1.0 / (10000.0 * s) ** (2.0 * i / 64.0)
        ang = pos[:, None] * freqs[None, :]
        coss.append(np.cos(ang)); sins.append(np.sin(ang))
    cos = np.mean(coss, 0).astype(np.float32)
    sin = np.mean(sins, 0).astype(np.float32)
    cq = np.concatenate([np.tile(cos, (1, 8)) / 8.0, np.tile(sin, (1, 8)) / 8.0], 1)
    ck = np.concatenate([np.tile(cos, (1, 8)), np.tile(sin, (1, 8))], 1)
    return cq.astype(np.float32), ck.astype(np.float32)


def rope_tables_B():
    pos = np.arange(4096, dtype=np.float64)
    half = 32
    inv = 1.0 / 10000.0 ** (np.arange(half, dtype=np.float64) / half)
    ang = pos[:, None] * inv[None, :]
    cos = np.cos(ang).astype(np.float32)
    sin = np.sin(ang).astype(np.float32)
    cq = np.concatenate([np.tile(cos, (1, 8)) / 8.0, np.tile(sin, (1, 8)) / 8.0], 1)
    ck = np.concatenate([np.tile(cos, (1, 8)), np.tile(sin, (1, 8))], 1)
    return cq.astype(np.float32), ck.astype(np.float32)


def host_weights(i):
    w = {}
    f = np.float32
    A = lambda x: np.ascontiguousarray(np.asarray(x, f))
    w["w_l0a_q"] = A(i["L0_attn_Wq"]); w["w_l0a_k"] = A(i["L0_attn_Wk"])
    w["w_l0a_v"] = A(i["L0_attn_Wv"]); w["w_l0a_o"] = A(i["L0_attn_Wo"])
    w["c_ropeA_q"], w["c_ropeA_k"] = rope_tables_A()
    w["p_l0f_nw"] = pcol(i["L0_ffn_attn_nw"]); w["p_l0f_nb"] = pcol(i["L0_ffn_attn_nb"])
    w["w_l0f_g"] = A(i["L0_ffn_wg"]); w["w_l0f_c"] = A(i["L0_ffn_res_coef"])
    w1 = np.concatenate([np.asarray(i["L0_ffn_inter_w"], f),
                         np.asarray(i["L0_ffn_res_inter_w"], f)[None]], 0)
    w["w_l0f_1"] = A(w1.reshape(9 * 1024, 4096))
    b1 = np.concatenate([np.asarray(i["L0_ffn_inter_b"], f),
                         np.asarray(i["L0_ffn_res_inter_b"], f)[None]], 0)
    w["p_l0f_b1"] = np.concatenate([pcol(b1[e]) for e in range(9)], 1)
    w2 = np.concatenate([np.asarray(i["L0_ffn_output_w"], f),
                         np.asarray(i["L0_ffn_res_output_w"], f)[None]], 0)
    w["w_l0f_2"] = A(w2.reshape(9 * 4096, 1024))
    b2 = np.concatenate([np.asarray(i["L0_ffn_output_b"], f),
                         np.asarray(i["L0_ffn_res_output_b"], f)[None]], 0)
    w["b_l0f_2"] = A(b2)
    sh = np.zeros((1024, 128), f); sh[:, 0:16] = np.asarray(i["L1_attn_W_share"], f)
    w["w_l1a_sh"] = sh
    shb = np.zeros(128, f); shb[0:16] = np.asarray(i["L1_attn_b_share"], f)
    w["p_l1a_shb"] = pcol(shb)[:, 0:1]
    w["w_l1a_q"] = A(i["L1_attn_Wq"]); w["p_l1a_bq"] = pcol(i["L1_attn_bq"])
    w["w_l1a_k"] = A(i["L1_attn_Wk"]); w["p_l1a_bk"] = pcol(i["L1_attn_bk"])
    w["w_l1a_v"] = A(i["L1_attn_Wv"]); w["r_l1a_bv"] = A(i["L1_attn_bv"])[None, :]
    w["w_l1a_o"] = A(i["L1_attn_Wo"]); w["p_l1a_bo"] = pcol(i["L1_attn_bo"])
    w["p_l1f_n1"] = pcol(i["L1_ffn_norm1_w"]); w["p_l1f_n2"] = pcol(i["L1_ffn_norm2_w"])
    w["w_l1f_q"] = A(np.asarray(i["L1_ffn_Wq"], f).T)
    w["c_ropeB_q"], w["c_ropeB_k"] = rope_tables_B()
    w["w_l1f_dkv"] = A(np.asarray(i["L1_ffn_Wdkv"], f).T)
    w["p_l1f_lng"] = pcol(i["L1_ffn_ln_g"]); w["p_l1f_lnb"] = pcol(i["L1_ffn_ln_b"])
    w["w_l1f_uk"] = A(np.asarray(i["L1_ffn_Wuk"], f).T)
    w["w_l1f_uv"] = A(np.asarray(i["L1_ffn_Wuv"], f).T)
    w["w_l1f_o"] = A(np.asarray(i["L1_ffn_Wo"], f).T)
    w["w_l1f_g"] = A(np.asarray(i["L1_ffn_gate_w"], f).T)
    w["r_l1f_gb"] = A(i["L1_ffn_gate_b"])[None, :]
    W1 = np.asarray(i["L1_ffn_W1"], f)
    w["w_l1f_1"] = A(W1.transpose(0, 2, 1).reshape(4 * 1024, 4096))
    b1 = np.asarray(i["L1_ffn_b1"], f)
    w["p_l1f_b1"] = np.concatenate([pcol(b1[e]) for e in range(4)], 1)
    W2 = np.asarray(i["L1_ffn_W2"], f)
    w["w_l1f_2"] = A(W2.transpose(0, 2, 1).reshape(4 * 4096, 1024))
    w["b_l1f_2"] = A(i["L1_ffn_b2"])
    w["w_l2a_in"] = A(i["L2_attn_W_in"]); w["p_l2a_bin"] = pcol(i["L2_attn_b_in"])
    w["p_l2a_A"] = pcol(i["L2_attn_A"]); w["p_l2a_C"] = pcol(i["L2_attn_C"])
    w["w_l2a_out"] = A(i["L2_attn_W_out"]); w["p_l2a_bout"] = pcol(i["L2_attn_b_out"])
    w["w_l2f_g"] = A(i["L2_ffn_gate_w"])
    w["w_l2f_1"] = A(np.asarray(i["L2_ffn_w1"], f).reshape(8 * 1024, 2048))
    w["w_l2f_2"] = A(np.asarray(i["L2_ffn_w2"], f).reshape(8 * 2048, 1024))
    w["w_l3a_q"] = A(np.asarray(i["L3_attn_w_q"], f).reshape(1024, 1024))
    w["w_l3a_k"] = A(np.asarray(i["L3_attn_w_k"], f).reshape(1024, 1024))
    w["w_l3a_v"] = A(np.asarray(i["L3_attn_w_v"], f).reshape(1024, 1024))
    w["w_l3a_o"] = A(np.asarray(i["L3_attn_w_o"], f).transpose(1, 2, 0).reshape(1024, 1024))
    w["w_l3f_q"] = A(np.asarray(i["L3_ffn_Wq"], f).transpose(2, 0, 1).reshape(1024, 1024))
    w["p_l3f_bq"] = pcol(np.asarray(i["L3_ffn_bq"], f).reshape(1024))
    w["w_l3f_k"] = A(np.asarray(i["L3_ffn_Wk"], f).transpose(2, 0, 1).reshape(1024, 1024))
    w["p_l3f_bk"] = pcol(np.asarray(i["L3_ffn_bk"], f).reshape(1024))
    w["w_l3f_v"] = A(np.asarray(i["L3_ffn_Wv"], f).transpose(2, 0, 1).reshape(1024, 1024))
    w["r_l3f_bv"] = A(np.asarray(i["L3_ffn_bv"], f).reshape(1024))[None, :]
    w["w_l3f_g"] = A(np.asarray(i["L3_ffn_Wsw"], f).T)
    w["r_l3f_gb"] = A(i["L3_ffn_bsw"])[None, :]
    We = np.asarray(i["L3_ffn_We"], f)
    w["w_l3f_e"] = A(We.transpose(0, 2, 1).reshape(8 * 1024, 1024))
    w["b_l3f_e"] = A(i["L3_ffn_be"])
    w["p_l3f_g"] = pcol(i["L3_ffn_gamma"]); w["p_l3f_b"] = pcol(i["L3_ffn_beta"])
    return w


ALL_BLOCKS = ["l0a", "l0f", "l1a", "l1f", "l2a", "l2f", "l3a", "l3f"]


def kernel(**inputs):
    x = np.asarray(inputs["L0_attn_x"], np.float32)
    w = host_weights(inputs)
    nc, p = build_module(set(ALL_BLOCKS))
    in_maps = []
    for c in range(8):
        m = dict(w)
        m["xin"] = np.ascontiguousarray(x[c].T)
        m.update(make_consts_host(c))
        in_maps.append(m)
    res = run_bass_kernel_spmd(nc, in_maps, core_ids=list(range(8)))
    out = np.stack([np.ascontiguousarray(res.results[c]["y"].T) for c in range(8)], 0)
    return out


# version 03 (final)
# speedup vs baseline: 1.3563x; 1.3066x over previous
import math
import numpy as np
import ml_dtypes
import concourse.bass as bass
import concourse.mybir as mybir
from concourse.bass_utils import run_bass_kernel_spmd

F32 = mybir.dt.float32
BF16 = mybir.dt.bfloat16
U32 = mybir.dt.uint32
AF = mybir.ActivationFunctionType
OP = mybir.AluOpType
AX = mybir.AxisListType

T = 4096
D = 1024
NTT = 8
TTW = 512
NKC = 8

GELU_OK = True



class P:
    COMPUTE = ("pe", "act", "dve", "pool")
    SELF_ORDERED = ("pe",)
    QUEUES = ("qsp", "qact", "qpool")
    QUEUE_ENG = {"qsp": "sp", "qact": "act", "qpool": "pool"}

    NCHAIN = 8

    def __init__(self, nc):
        self.nc = nc
        self.q = {e: [] for e in ("pe", "act", "dve", "pool", "sp")}
        self.sems = {}
        self.count = {}
        sem_names = list(self.COMPUTE) + ["cc0", "cc1"]
        for qn in self.QUEUES:
            for c in range(self.NCHAIN):
                sem_names.append(qn + str(c))
        for s in sem_names:
            self.sems[s] = None
            self.count[s] = 0
        self.qi = {qn: 0 for qn in self.QUEUES}
        self.writers = {}
        self.readers = {}
        self.observed = {e: {} for e in ("pe", "act", "dve", "pool", "sp")}
        self.dram = {}
        self.sbuf = {}
        self.n_emitted = 0

    def dram_tensor(self, name, shape, dtype, kind="Internal"):
        t = self.nc.dram_tensor(name, list(shape), dtype, kind=kind)
        self.dram[name] = t
        return t

    def sbuf_tensor(self, name, shape, dtype):
        t = self.nc.alloc_sbuf_tensor(name, list(shape), dtype)
        self.sbuf[name] = t
        return t

    def _tok(self, sem_name):
        return (sem_name, self.count[sem_name])

    def _need_wait(self, eng, tok, my_chain=None):
        sem_name, val = tok
        if val <= 0:
            return None
        if sem_name == eng and eng in self.SELF_ORDERED:
            return None
        if my_chain is not None and sem_name == my_chain:
            return None
        got = self.observed[eng].get(sem_name, 0)
        if got >= val:
            return None
        return (sem_name, val)

    def emit(self, eng, build, reads=(), writes=(), queue=None, tag="", sreads=()):
        waits = {}
        my_chain = None
        if queue is not None:
            i = self.qi[queue]
            self.qi[queue] = i + 1
            my_chain = queue + str(i % self.NCHAIN)
        def add_tok(tok):
            w = self._need_wait(eng, tok, my_chain=my_chain)
            if w:
                waits[w[0]] = max(waits.get(w[0], 0), w[1])
        rs = [(self._key(b), sk) for (b, sk) in reads]
        srs = [(self._key(b), sk) for (b, sk) in sreads]
        ws = [(self._key(b), sk) for (b, sk) in writes]
        for (b, sk) in rs:
            for tok in self._write_toks(b, sk):
                add_tok(tok)
        for (b, sk) in srs:
            for tok in self._write_toks(b, sk):
                sem_name, val = tok
                if val <= 0:
                    continue
                if my_chain is not None and sem_name == my_chain:
                    continue
                got = self.observed[eng].get(sem_name, 0)
                if got < val:
                    waits[sem_name] = max(waits.get(sem_name, 0), val)
        rs = rs + srs
        for (b, sk) in ws:
            for tok in self._write_toks(b, sk):
                add_tok(tok)
            for tok in self._read_toks(b, sk):
                add_tok(tok)
        sem_name = my_chain if queue else eng
        step = 16 if queue else 1
        if queue is not None and self.count[sem_name] > 0:
            waits[sem_name] = max(waits.get(sem_name, 0), self.count[sem_name])
        self.count[sem_name] += step
        val_after = self.count[sem_name]
        for (b, sk) in ws:
            d = self.writers.setdefault(b, {})
            if sk == "a":
                d.clear()
            d[sk] = (sem_name, val_after)
            rd = self.readers.setdefault(b, {})
            if sk == "a":
                rd.clear()
            else:
                rd.pop(sk, None)
        for (b, sk) in rs:
            d = self.readers.setdefault(b, {}).setdefault(sk, {})
            d[sem_name] = val_after if False else max(d.get(sem_name, 0), val_after)
        wl = sorted(waits.items())
        obs = self.observed[eng]
        for s, v in wl:
            obs[s] = max(obs.get(s, 0), v)
        pp = self
        def thunk(engine_obj):
            for s, v in wl:
                engine_obj.wait_ge(pp.sems[s], v)
            inst = build(engine_obj)
            inst.then_inc(pp.sems[sem_name], step)
        self.q[eng].append(thunk)
        self.n_emitted += 1

    def _key(self, buf):
        if isinstance(buf, str):
            return buf
        return getattr(buf, "name", None) or str(id(buf))

    def _write_toks(self, b, sk):
        d = self.writers.get(b)
        if not d:
            return []
        if sk == "a":
            return list(d.values())
        toks = []
        if "a" in d:
            toks.append(d["a"])
        if sk in d:
            toks.append(d[sk])
        return toks

    def _read_toks(self, b, sk):
        d = self.readers.get(b)
        if not d:
            return []
        keys = list(d.keys()) if sk == "a" else (["a", sk] if "a" in d else [sk])
        toks = []
        for k in keys:
            if k in d:
                for s, v in d[k].items():
                    toks.append((s, v))
        return toks

    def dma(self, queue, out, in_, reads=(), writes=(), **kw):
        eng = self.QUEUE_ENG[queue]
        self.emit(eng, lambda e: e.dma_start(out=out, in_=in_, **kw),
                  reads=reads, writes=writes, queue=queue)

    def finalize_sems(self, stack):
        for s in list(self.sems.keys()):
            self.sems[s] = stack.enter_context(self.nc.semaphore("sem_" + s))

    def run_blocks(self):
        nc = self.nc
        with nc.Block() as block:
            @block.tensor
            def _(e):
                for f in self.q["pe"]:
                    f(e)

            @block.scalar
            def _(e):
                for f in self.q["act"]:
                    f(e)

            @block.vector
            def _(e):
                for f in self.q["dve"]:
                    f(e)

            @block.gpsimd
            def _(e):
                for f in self.q["pool"]:
                    f(e)

            @block.sync
            def _(e):
                for f in self.q["sp"]:
                    f(e)



class Ctx:
    pass


def setup_ctx(p, stack):
    nc = p.nc
    b = Ctx()
    b.U1 = p.sbuf_tensor("U1", [128, 33280], BF16)
    b.vaug = lambda: b.U1
    b.WSB_OFF = 0
    b.Y1_OFF = 16384
    b.S = p.sbuf_tensor("Sf", [128, 8192], F32)
    b.XB = p.sbuf_tensor("XB", [128, 8192], BF16)
    b.qsb = p.sbuf_tensor("qsb", [128, 4096], BF16)
    b.kaug = p.sbuf_tensor("kaug", [128, 4096], BF16)
    b.pt = p.sbuf_tensor("pt", [128, 1536], BF16)
    b.ptc = p.sbuf_tensor("ptc", [128, 512], BF16)
    b.vfut = p.sbuf_tensor("vfut", [128, 1040], BF16)
    b.attsq = p.sbuf_tensor("attsq", [128, 512], BF16)
    b.bsb = p.sbuf_tensor("bsb", [128, 512], F32)
    b.bsb2 = p.sbuf_tensor("bsb2", [128, 512], F32)
    b.osb = p.sbuf_tensor("osb", [128, 2048], BF16)
    b.osf = p.sbuf_tensor("osf", [128, 2048], F32)
    b.wstg = p.sbuf_tensor("wstg", [128, 2048], F32)
    b.wstg_i = [0]
    b.bias = p.sbuf_tensor("biasc", [128, 64], F32)
    b.gb = p.sbuf_tensor("gbc", [128, 16], F32)
    b.wfm = p.sbuf_tensor("wfm", [16, 4096], BF16)
    b.stash = p.sbuf_tensor("stash", [128, 640], F32)
    b.rows = p.sbuf_tensor("rows", [1, 2048], F32)
    b.rowbf = p.sbuf_tensor("rowbf", [1, 1024], BF16)
    b.tiny = p.sbuf_tensor("tiny", [1, 64], F32)
    b.wgt = p.sbuf_tensor("wgt", [128, 80], BF16)
    b.mi32 = p.sbuf_tensor("mi32", [128, 8], U32)
    b.b2s = p.sbuf_tensor("b2s", [12, 1024], BF16)
    b.ones_red = p.sbuf_tensor("ones_red", [128, 2], BF16)
    b.ones_redf = p.sbuf_tensor("ones_redf", [128, 2], F32)
    b.ones_row = p.sbuf_tensor("ones_row", [1, 128], F32)
    b.ones_rowb = p.sbuf_tensor("ones_rowb", [1, 128], BF16)
    b.rope = p.sbuf_tensor("ropec", [128, 1024], F32)
    b.rtmp = p.sbuf_tensor("rtmp", [128, 1024], F32)
    b.dec = p.sbuf_tensor("decc", [128, 32], F32)
    b.idb = p.sbuf_tensor("idb", [128, 128], BF16)
    b.idf = p.sbuf_tensor("idf", [128, 128], F32)
    b.iota8 = p.sbuf_tensor("iota8", [128, 8], F32)
    b.sut = p.sbuf_tensor("sut", [128, 128], F32)
    b.cmask = p.sbuf_tensor("cmask", [128, 2048], BF16)
    b.cminv = p.sbuf_tensor("cminv", [128, 2048], BF16)
    b.coremask = p.sbuf_tensor("coremask", [1, 64], F32)
    b.zero512 = p.sbuf_tensor("zero512", [128, 512], BF16)
    b.ps = {}
    for nm in ["pa", "pb", "po", "po2", "pc", "pr", "pg"]:
        b.ps[nm] = stack.enter_context(nc.psum_tensor(nm, [128, 512], F32))
    b.pst = stack.enter_context(nc.psum_tensor("pst", [128, 1024], BF16))
    return b


def load_consts(p, b, cin):
    for nm, dst in [("c_idb", b.idb), ("c_idf", b.idf), ("c_iota8", b.iota8),
                    ("c_sut", b.sut), ("c_cmask", b.cmask),
                    ("c_cminv", b.cminv), ("c_coremask", b.coremask)]:
        h = cin[nm]
        p.dma("qsp", dst[:, :], h.ap()[:, :], reads=(), writes=[(dst, "a")])
    p.emit("dve", lambda e: e.memset(b.ones_red[:, :], 1.0), writes=[(b.ones_red, "a")])
    p.emit("dve", lambda e: e.memset(b.ones_redf[:, :], 1.0), writes=[(b.ones_redf, "a")])
    p.emit("dve", lambda e: e.memset(b.ones_row[:, :], 1.0), writes=[(b.ones_row, "a")])
    p.emit("dve", lambda e: e.memset(b.ones_rowb[:, :], 1.0), writes=[(b.ones_rowb, "a")])
    p.emit("dve", lambda e: e.memset(b.zero512[:, :], 0.0), writes=[(b.zero512, "a")])


def dk(handle, r0, r1, c0, c1):
    nm = handle.name if hasattr(handle, "name") else str(handle)
    out = []
    for r in range(r0 // 64, (r1 + 63) // 64):
        for c in range(c0 // 512, (c1 + 511) // 512):
            out.append((nm, ("b", r, c)))
    return out


def u1g(lo, hi):
    return [("U1", ("g", g)) for g in range(lo // 2048, (hi + 2047) // 2048)]



def wstg_slot(b):
    i = b.wstg_i[0] % 2
    b.wstg_i[0] += 1
    return b.wstg[:, i * 1024:(i + 1) * 1024], ("s", i)


def load_w(p, b, w_dram, Din, oc0, ocw, tag="w"):
    nkc = Din // 128
    assert nkc * ocw <= 16384
    base = b.WSB_OFF
    for kc in range(nkc):
        for c0 in range(0, ocw, 1024):
            cw = min(1024, ocw - c0)
            src = w_dram.ap()[kc * 128:(kc + 1) * 128, oc0 + c0: oc0 + c0 + cw]
            stg, skey = wstg_slot(b)
            p.dma("qsp", stg[:, 0:cw], src, writes=[(b.wstg, skey)])
            dst = b.U1[:, base + kc * ocw + c0: base + kc * ocw + c0 + cw]
            p.emit("dve", lambda e, s=stg[:, 0:cw], d=dst: e.tensor_copy(d, s),
                   reads=[(b.wstg, skey)],
                   writes=u1g(base + kc * ocw + c0, base + kc * ocw + c0 + cw))
    return base


def w_tile(b, base, ocw, kc, j0, m):
    off = base + kc * ocw + j0
    return b.U1[:, off: off + m]



def gemm_fm(p, b, out_dram, in_dram, w_dram, Din, Dout, *,
            bias_pcol=None,
            act=None,
            scale=1.0, bias_scale=None,
            out_dtype="bf16",
            resid_dram=None,
            out_row0=0,
            in_row0=0,
            tag="g"):
    nkc = Din // 128
    noc = Dout // 128
    if bias_pcol is not None:
        p.dma("qsp", b.bias[:, 0:noc], bias_pcol.ap()[:, 0:noc],
              writes=[(b.bias, "a")])
        if bias_scale is not None:
            p.emit("dve", lambda e: e.tensor_scalar(b.bias[:, 0:noc], b.bias[:, 0:noc],
                                                    bias_scale, None, OP.mult),
                   reads=[(b.bias, "a")], writes=[(b.bias, "a")])
    ocg = max(1, min(Dout, 16384 * 128 // Din))
    ocg = (ocg // 128) * 128
    psr = ["pa", "pb", "pc", "pr"]
    pi = 0
    for g0 in range(0, Dout, ocg):
        gw = min(ocg, Dout - g0)
        base = load_w(p, b, w_dram, Din, g0, gw, tag=tag)
        wkeys = u1g(base, base + nkc * gw)
        for tt in range(NTT):
            assert nkc <= 16
            par = (tt % 2) if nkc <= 8 else 0
            xoff = par * 4096
            for kc in range(nkc):
                src = in_dram.ap()[in_row0 + kc * 128: in_row0 + (kc + 1) * 128,
                                   tt * TTW:(tt + 1) * TTW]
                p.dma("qsp", b.XB[:, xoff + kc * TTW: xoff + (kc + 1) * TTW], src,
                      reads=dk(in_dram, in_row0 + kc * 128, in_row0 + (kc + 1) * 128,
                               tt * TTW, (tt + 1) * TTW),
                      writes=[(b.XB, ("g", xoff // 512 + kc))])
            for j0 in range(0, gw, 128):
                oc = (g0 + j0) // 128
                ps = b.ps[psr[pi % 4]]
                pi += 1
                for kc in range(nkc):
                    lhsT = w_tile(b, base, gw, kc, j0, 128)
                    rhs = b.XB[:, xoff + kc * TTW: xoff + (kc + 1) * TTW]
                    p.emit("pe",
                           lambda e, o=ps[:, :], l=lhsT, r=rhs, s=(kc == 0),
                                  t=(kc == nkc - 1):
                               e.matmul(o, l, r, start=s, stop=t),
                           reads=wkeys + [(b.XB, ("g", xoff // 512 + kc))],
                           writes=[(ps, "a")])
                ob = (pi % 4)
                if resid_dram is not None:
                    rsl = b.S[:, (oc % 4) * 512: (oc % 4) * 512 + 512]
                    p.dma("qsp", rsl,
                          resid_dram.ap()[out_row0 + oc * 128: out_row0 + oc * 128 + 128,
                                          tt * TTW:(tt + 1) * TTW],
                          reads=dk(resid_dram, out_row0 + oc * 128,
                                   out_row0 + oc * 128 + 128, tt * TTW, (tt + 1) * TTW),
                          writes=[(b.S, ("g", oc % 4))])
                    osl = b.osf[:, ob * 512:(ob + 1) * 512]
                    if bias_pcol is not None:
                        p.emit("dve",
                               lambda e, o=osl, i=ps[:, :], sc=b.bias[:, oc:oc + 1]:
                                   e.tensor_scalar(o, i, sc, None, OP.add),
                               reads=[(ps, "a"), (b.bias, "a")],
                               writes=[(b.osf, ("o", ob))])
                        p.emit("dve",
                               lambda e, o=osl, i=osl, r=rsl: e.tensor_add(o, i, r),
                               reads=[(b.S, ("g", oc % 4)), (b.osf, ("o", ob))],
                               writes=[(b.osf, ("o", ob))])
                    else:
                        p.emit("dve",
                               lambda e, o=osl, i=ps[:, :], r=rsl: e.tensor_add(o, i, r),
                               reads=[(ps, "a"), (b.S, ("g", oc % 4))],
                               writes=[(b.osf, ("o", ob))])
                    p.dma("qact", out_dram.ap()[out_row0 + oc * 128: out_row0 + oc * 128 + 128,
                                                tt * TTW:(tt + 1) * TTW],
                          osl, reads=[(b.osf, ("o", ob))],
                          writes=dk(out_dram, out_row0 + oc * 128,
                                    out_row0 + oc * 128 + 128, tt * TTW, (tt + 1) * TTW))
                else:
                    osl = (b.osb if out_dtype == "bf16" else b.osf)[:, ob * 512:(ob + 1) * 512]
                    okey = (b.osb if out_dtype == "bf16" else b.osf, ("o", ob))
                    fn = act if act is not None else AF.Copy
                    if bias_pcol is not None and act is None:
                        p.emit("dve",
                               lambda e, o=osl, i=ps[:, :], sc=scale,
                                      ba=b.bias[:, oc:oc + 1]:
                                   e.tensor_scalar(o, i, sc, ba, OP.mult, OP.add)
                                   if sc != 1.0 else
                                   e.tensor_scalar(o, i, ba, None, OP.add),
                               reads=[(ps, "a"), (b.bias, "a")], writes=[okey])
                    elif bias_pcol is not None:
                        p.emit("act",
                               lambda e, o=osl, i=ps[:, :], f=fn, sc=scale,
                                      ba=b.bias[:, oc:oc + 1]:
                                   e.activation(out=o, in_=i, func=f, bias=ba, scale=sc),
                               reads=[(ps, "a"), (b.bias, "a")], writes=[okey])
                    else:
                        p.emit("act",
                               lambda e, o=osl, i=ps[:, :], f=fn, sc=scale:
                                   e.activation(out=o, in_=i, func=f, scale=sc),
                               reads=[(ps, "a")], writes=[okey])
                    p.dma("qact", out_dram.ap()[out_row0 + oc * 128: out_row0 + oc * 128 + 128,
                                                tt * TTW:(tt + 1) * TTW],
                          osl, reads=[okey],
                          writes=dk(out_dram, out_row0 + oc * 128,
                                    out_row0 + oc * 128 + 128, tt * TTW, (tt + 1) * TTW))



def norm_fm(p, b, out_bf, in_f32, Dn, *, eps=1e-5, kind="ln",
            gamma_pcol=None, beta_pcol=None, out_f32=None,
            final_resid=None, final_out=None, tag="ln"):
    nkc = Dn // 128
    if gamma_pcol is not None:
        p.dma("qsp", b.gb[:, 0:nkc], gamma_pcol.ap()[:, 0:nkc], writes=[(b.gb, "g")])
    if beta_pcol is not None:
        p.dma("qsp", b.gb[:, 8:8 + nkc], beta_pcol.ap()[:, 0:nkc], writes=[(b.gb, "b")])
    for tt in range(NTT):
        for kc in range(nkc):
            p.dma("qsp", b.S[:, (kc % 8) * 512:(kc % 8) * 512 + 512],
                  in_f32.ap()[kc * 128:(kc + 1) * 128, tt * TTW:(tt + 1) * TTW],
                  reads=dk(in_f32, kc * 128, (kc + 1) * 128, tt * TTW, (tt + 1) * TTW),
                  writes=[(b.S, ("g", kc))])
        psq = b.ps["pr"]
        pss = b.ps["pc"]
        for kc in range(nkc):
            xsl = b.S[:, (kc % 8) * 512:(kc % 8) * 512 + 512]
            xbk = b.XB[:, kc * 512: kc * 512 + 512]
            p.emit("dve", lambda e, o=xbk, i=xsl: e.tensor_copy(o, i),
                   reads=[(b.S, ("g", kc))], writes=[(b.XB, ("g", kc))])
            sqk = b.XB[:, 4096 + (kc % 8) * 512: 4096 + (kc % 8) * 512 + 512]
            p.emit("dve", lambda e, o=sqk, i=xbk: e.tensor_mul(o, i, i),
                   reads=[(b.XB, ("g", kc))], writes=[(b.XB, ("g", 8 + kc))])
            if kind == "ln":
                p.emit("pe", lambda e, o=pss[0:1, :], l=b.ones_red[:, 0:1], r=xbk,
                              s=(kc == 0), t=(kc == nkc - 1):
                           e.matmul(o, l, r, start=s, stop=t),
                       reads=[(b.XB, ("g", kc)), (b.ones_red, "a")],
                       writes=[(pss, "a")])
            p.emit("pe", lambda e, o=psq[0:1, :], l=b.ones_red[:, 0:1], r=sqk,
                          s=(kc == 0), t=(kc == nkc - 1):
                       e.matmul(o, l, r, start=s, stop=t),
                   reads=[(b.XB, ("g", 8 + kc)), (b.ones_red, "a")],
                   writes=[(psq, "a")])
        mrow = b.rows[0:1, 0:512]
        vrow = b.rows[0:1, 512:1024]
        rrow = b.rows[0:1, 1024:1536]
        if kind == "ln":
            p.emit("dve", lambda e: e.tensor_scalar(mrow, pss[0:1, :], 1.0 / Dn, None, OP.mult),
                   reads=[(pss, "a")], writes=[("rows", "m")])
            p.emit("dve", lambda e: e.tensor_mul(vrow, mrow, mrow),
                   reads=[("rows", "m")], writes=[("rows", "v")])
            p.emit("dve", lambda e: e.scalar_tensor_tensor(vrow, psq[0:1, :], 1.0 / Dn, vrow,
                                                           OP.mult, OP.subtract),
                   reads=[(psq, "a"), ("rows", "v")], writes=[("rows", "v")])
        else:
            p.emit("dve", lambda e: e.tensor_scalar(vrow, psq[0:1, :], 1.0 / Dn, None, OP.mult),
                   reads=[(psq, "a")], writes=[("rows", "v")])
        p.emit("dve", lambda e: e.tensor_scalar(rrow, vrow, eps, None, OP.add),
               reads=[("rows", "v")], writes=[("rows", "r")])
        p.emit("act", lambda e: e.activation(out=rrow, in_=rrow, func=AF.Sqrt),
               reads=[("rows", "r")], writes=[("rows", "r")])
        p.emit("dve", lambda e: e.reciprocal(rrow, rrow),
               reads=[("rows", "r")], writes=[("rows", "r")])
        pbm = b.ps["pa"]
        pbr = b.ps["pb"]
        if kind == "ln":
            p.emit("pe", lambda e: e.matmul(pbm[:, :], b.ones_row[0:1, :], mrow,
                                            start=True, stop=True),
                   reads=[("rows", "m"), (b.ones_row, "a")], writes=[(pbm, "a")])
        p.emit("pe", lambda e: e.matmul(pbr[:, :], b.ones_row[0:1, :], rrow,
                                        start=True, stop=True),
               reads=[("rows", "r"), (b.ones_row, "a")], writes=[(pbr, "a")])
        for kc in range(nkc):
            xsl = b.S[:, (kc % 8) * 512:(kc % 8) * 512 + 512]
            ot = b.osf[:, (kc % 4) * 512:(kc % 4) * 512 + 512]
            if kind == "ln":
                p.emit("dve", lambda e, o=ot, i=xsl: e.tensor_sub(o, i, pbm[:, :]),
                       reads=[(b.S, ("g", kc)), (pbm, "a")], writes=[(b.osf, ("o", kc % 4))])
                src2 = ot
            else:
                src2 = xsl
            obf = b.osb[:, (kc % 4) * 512:(kc % 4) * 512 + 512]
            if gamma_pcol is None:
                p.emit("dve", lambda e, o=obf, i=src2, k=kc:
                           e.tensor_mul(o, i, pbr[:, :]),
                       reads=[(b.osf, ("o", kc % 4)) if kind == "ln" else (b.S, ("g", kc)),
                              (pbr, "a")],
                       writes=[(b.osb, ("o", kc % 4))])
            else:
                p.emit("dve", lambda e, o=ot, i=src2:
                           e.tensor_mul(o, i, pbr[:, :]),
                       reads=[(b.osf, ("o", kc % 4)) if kind == "ln" else (b.S, ("g", kc)),
                              (pbr, "a")],
                       writes=[(b.osf, ("o", kc % 4))])
                if beta_pcol is not None:
                    p.emit("dve", lambda e, o=obf, i=ot, k=kc:
                               e.tensor_scalar(o, i, b.gb[:, k:k + 1], b.gb[:, 8 + k:9 + k],
                                               OP.mult, OP.add),
                           reads=[(b.osf, ("o", kc % 4)), (b.gb, "g"), (b.gb, "b")],
                           writes=[(b.osb, ("o", kc % 4))])
                else:
                    p.emit("dve", lambda e, o=obf, i=ot, k=kc:
                               e.tensor_scalar(o, i, b.gb[:, k:k + 1], None, OP.mult),
                           reads=[(b.osf, ("o", kc % 4)), (b.gb, "g")],
                           writes=[(b.osb, ("o", kc % 4))])
            if final_out is not None:
                rsl = b.S[:, 6144 + (kc % 4) * 512: 6144 + (kc % 4) * 512 + 512]
                p.dma("qsp", rsl,
                      final_resid.ap()[kc * 128:(kc + 1) * 128, tt * TTW:(tt + 1) * TTW],
                      reads=dk(final_resid, kc * 128, (kc + 1) * 128, tt * TTW,
                               (tt + 1) * TTW),
                      writes=[(b.S, ("g", 12 + kc % 4))])
                of3 = b.osf[:, (kc % 4) * 512:(kc % 4) * 512 + 512]
                p.emit("dve", lambda e, o=of3, i=obf, r=rsl: e.tensor_add(o, i, r),
                       reads=[(b.osb, ("o", kc % 4)), (b.S, ("g", 12 + kc % 4))],
                       writes=[(b.osf, ("o", kc % 4))])
                p.dma("qact", final_out.ap()[kc * 128:(kc + 1) * 128, tt * TTW:(tt + 1) * TTW],
                      of3, reads=[(b.osf, ("o", kc % 4))],
                      writes=dk(final_out, kc * 128, (kc + 1) * 128, tt * TTW,
                                (tt + 1) * TTW))
            else:
                p.dma("qact", out_bf.ap()[kc * 128:(kc + 1) * 128, tt * TTW:(tt + 1) * TTW],
                      obf, reads=[(b.osb, ("o", kc % 4))],
                      writes=dk(out_bf, kc * 128, (kc + 1) * 128, tt * TTW, (tt + 1) * TTW))
            if out_f32 is not None:
                of2 = b.osf[:, (kc % 4) * 512:(kc % 4) * 512 + 512]
                p.emit("dve", lambda e, o=of2, i=obf: e.tensor_copy(o, i),
                       reads=[(b.osb, ("o", kc % 4))], writes=[(b.osf, ("o", kc % 4))])
                p.dma("qact", out_f32.ap()[kc * 128:(kc + 1) * 128, tt * TTW:(tt + 1) * TTW],
                      of2, reads=[(b.osf, ("o", kc % 4))],
                      writes=dk(out_f32, kc * 128, (kc + 1) * 128, tt * TTW, (tt + 1) * TTW))



def make_consts_host(core_id):
    c = {}
    c["c_idb"] = np.eye(128, dtype=ml_dtypes.bfloat16)
    c["c_idf"] = np.eye(128, dtype=np.float32)
    c["c_iota8"] = np.tile(np.arange(8, dtype=np.float32), (128, 1))
    c["c_sut"] = np.triu(np.ones((128, 128), np.float32), 1)
    cm = np.zeros((128, 4 * 512), ml_dtypes.bfloat16)
    ii = np.arange(128)[:, None]
    jj = np.arange(512)[None, :]
    for v in range(4):
        cm[:, v * 512:(v + 1) * 512] = (ii + v * 128 <= jj).astype(ml_dtypes.bfloat16)
    c["c_cmask"] = cm
    c["c_cminv"] = (1.0 - cm.astype(np.float32)).astype(ml_dtypes.bfloat16)
    msk = np.zeros((1, 64), np.float32)
    for cc in range(core_id):
        msk[0, cc * 8:(cc + 1) * 8] = 1.0
    c["c_coremask"] = msk
    return c

CONST_SPECS = [("c_idb", [128, 128], BF16), ("c_idf", [128, 128], F32),
               ("c_iota8", [128, 8], F32), ("c_sut", [128, 128], F32),
               ("c_cmask", [128, 2048], BF16), ("c_cminv", [128, 2048], BF16),
               ("c_coremask", [1, 64], F32)]


def pcol(v):
    v = np.asarray(v, np.float32)
    n = (v.shape[0] + 127) // 128
    out = np.zeros((128, n), np.float32)
    out[:, :] = v.reshape(n, 128).T
    return out


class P_drain:
    pass


def p_drain(p):
    vals = {}
    for qn in p.QUEUES:
        for c in range(p.NCHAIN):
            nm = qn + str(c)
            if p.count[nm] > 0:
                vals[nm] = p.count[nm]
    pp = p
    def thunk(e):
        for q, v in vals.items():
            e.wait_ge(pp.sems[q], v)
    p.q["pool"].append(thunk)


def build_nc(emit_fn, io_spec, core_id=0):
    from contextlib import ExitStack
    import os
    nc = bass.Bass("TRN2", target_bir_lowering=False, debug=False,
                   detect_race_conditions=bool(os.environ.get("RACEDET")))
    stack = ExitStack()
    with stack:
        p = P(nc)
        t = {}
        for name, shape, dtype, kind in io_spec:
            t[name] = p.dram_tensor(name, shape, dtype, kind=kind)
        for name, shape, dtype in CONST_SPECS:
            t[name] = p.dram_tensor(name, shape, dtype, kind="ExternalInput")
        p.finalize_sems(stack)
        b = setup_ctx(p, stack)
        load_consts(p, b, t)
        emit_fn(p, b, t)
        p_drain(p)
        p.run_blocks()
    return nc, p



def attention(p, b, *, out_dram, q_fm=None, k_fm=None, v_tm, mode,
              q_prep=None, k_prep=None, nheads=16, shift=40.0, tag="att"):
    NKB = T // 128
    p.emit("dve", lambda e: e.memset(
        bass.AP(b.U1, 64, [[33280, 128], [1040, NKB], [65, 16], [1, 1]]), 1.0),
        writes=u1g(0, 33280))
    for kb in range(NKB):
        stg = b.XB[:, (kb % 4) * 1024:(kb % 4) * 1024 + 1024]
        skeys = [(b.XB, ("g", (kb % 4) * 2)), (b.XB, ("g", (kb % 4) * 2 + 1))]
        p.dma("qsp", stg, v_tm.ap()[kb * 128:(kb + 1) * 128, 0:1024],
              reads=dk(v_tm, kb * 128, (kb + 1) * 128, 0, 1024), writes=skeys)
        dst = bass.AP(b.U1, kb * 1040, [[33280, 128], [65, 16], [1, 64]])
        src = bass.AP(b.XB, (kb % 4) * 1024, [[8192, 128], [64, 16], [1, 64]])
        p.emit("dve", lambda e, d=dst, s=src: e.tensor_copy(d, s),
               reads=skeys, writes=u1g(kb * 1040, kb * 1040 + 1040))
    p.emit("dve", lambda e: e.memset(b.kaug[64:65, :], 1.0),
           writes=[(b.kaug, "one")])
    vkeys_all = u1g(0, 33280)

    for h in range(nheads):
        if k_prep is None:
            p.dma("qsp", b.kaug[0:64, :], k_fm.ap()[h * 64:(h + 1) * 64, :],
                  reads=dk(k_fm, h * 64, (h + 1) * 64, 0, T),
                  writes=[(b.kaug, "hd")])
        else:
            k_prep(h)
        if q_prep is None:
            p.dma("qsp", b.qsb[0:64, :], q_fm.ap()[h * 64:(h + 1) * 64, :],
                  reads=dk(q_fm, h * 64, (h + 1) * 64, 0, T),
                  writes=[(b.qsb, "hd")])
        else:
            q_prep(h)
        pr = b.ps["pr"]
        for kt in range(8):
            ksl = b.kaug[0:64, kt * 512:(kt + 1) * 512]
            p.emit("dve", lambda e, k=ksl: e.tensor_mul(b.attsq[0:64, 0:512], k, k),
                   reads=[(b.kaug, "hd")], writes=[(b.attsq, "a")])
            p.emit("pe", lambda e: e.matmul(pr[0:1, :], b.ones_red[0:64, 0:1],
                                            b.attsq[0:64, 0:512], start=True, stop=True),
                   reads=[(b.attsq, "a"), (b.ones_red, "a")], writes=[(pr, "a")])
            p.emit("dve", lambda e, kt=kt: e.reduce_max(b.tiny[0:1, kt:kt + 1],
                                                        pr[0:1, :], axis=AX.X),
                   reads=[(pr, "a")], writes=[(b.tiny, ("km", kt))])
        p.emit("dve", lambda e: e.reduce_max(b.tiny[0:1, 8:9], b.tiny[0:1, 0:8],
                                             axis=AX.X),
               reads=[(b.tiny, ("km", kt)) for kt in range(8)],
               writes=[(b.tiny, "k2")])
        p.emit("act", lambda e: e.activation(out=b.tiny[0:1, 9:10], in_=b.tiny[0:1, 8:9],
                                             func=AF.Sqrt),
               reads=[(b.tiny, "k2")], writes=[(b.tiny, "ks")])
        for qt in range(NTT):
            qsl = b.qsb[0:64, qt * 512:(qt + 1) * 512]
            p.emit("dve", lambda e, q=qsl: e.tensor_mul(b.attsq[0:64, 0:512], q, q),
                   reads=[(b.qsb, "hd")], writes=[(b.attsq, "a")])
            p.emit("pe", lambda e: e.matmul(pr[0:1, :], b.ones_red[0:64, 0:1],
                                            b.attsq[0:64, 0:512], start=True, stop=True),
                   reads=[(b.attsq, "a"), (b.ones_red, "a")], writes=[(pr, "a")])
            brow = b.rows[0:1, 512:1024]
            p.emit("act", lambda e: e.activation(out=brow, in_=pr[0:1, :], func=AF.Sqrt),
                   reads=[(pr, "a")], writes=[("rows", "b")])
            p.emit("dve", lambda e: e.tensor_scalar(brow, brow, b.tiny[0:1, 9:10],
                                                    None, OP.mult),
                   reads=[("rows", "b"), (b.tiny, "ks")], writes=[("rows", "b")])
            p.emit("dve", lambda e, qt=qt: e.tensor_scalar(
                       b.qsb[64:65, qt * 512:(qt + 1) * 512], brow, -1.0, shift,
                       OP.mult, OP.add),
                   reads=[("rows", "b")], writes=[(b.qsb, ("nb", qt))])
        if mode == "zmask":
            p.emit("dve", lambda e: e.memset(b.vfut[:, :], 0.0),
                   writes=[(b.vfut, "a")])
            fut_loaded = NKB
        for qt in range(NTT - 1, -1, -1):
            kb_hi = NKB if mode == "none" else 4 * qt + 4
            if mode == "zmask":
                for kb in range(kb_hi, fut_loaded):
                    p.emit("dve", lambda e, kb=kb: e.tensor_add(
                               b.vfut[:, :], b.vfut[:, :],
                               b.U1[:, kb * 1040:(kb + 1) * 1040]),
                           reads=[(b.vfut, "a")] + u1g(kb * 1040, (kb + 1) * 1040),
                           writes=[(b.vfut, "a")])
                fut_loaded = kb_hi
                pc = b.ps["pc"]
                p.emit("dve", lambda e, qt=qt: e.tensor_copy(
                           b.rowbf[0:1, 0:512], b.qsb[64:65, qt * 512:(qt + 1) * 512]),
                       reads=[(b.qsb, ("nb", qt))], writes=[(b.rowbf, "a")])
                p.emit("pe", lambda e: e.matmul(
                           pc[:, :], b.ones_rowb[0:1, :],
                           b.rowbf[0:1, 0:512], start=True, stop=True),
                       reads=[(b.rowbf, "a"), (b.ones_rowb, "a")],
                       writes=[(pc, "a")])
                p.emit("act", lambda e: e.activation(out=b.ptc[:, :], in_=pc[:, :],
                                                     func=AF.Exp),
                       reads=[(pc, "a")], writes=[(b.ptc, "a")])
            po = b.ps["po"] if qt % 2 == 0 else b.ps["po2"]
            first = True
            last_kb = kb_hi - 1
            STB = ["pa", "pb", "pr"]
            LAG = 8

            def emit_st_exp(kb):
                ps_st = b.ps[STB[kb % 3]]
                ptt = b.XB[:, (kb % 16) * 512:(kb % 16) * 512 + 512]
                ptkey = (b.XB, ("g", kb % 16))
                p.emit("pe", lambda e, kb=kb, qt=qt, o=ps_st: e.matmul(
                           o[:, :], b.kaug[0:65, kb * 128:(kb + 1) * 128],
                           b.qsb[0:65, qt * 512:(qt + 1) * 512],
                           start=True, stop=True),
                       reads=[(b.kaug, "hd"), (b.kaug, "one"), (b.qsb, "hd"),
                              (b.qsb, ("nb", qt))],
                       writes=[(ps_st, "a")])
                p.emit("act", lambda e, o=ptt, i=ps_st: e.activation(
                           out=o, in_=i[:, :], func=AF.Exp),
                       reads=[(ps_st, "a")], writes=[ptkey])
                if mode != "none" and kb >= 4 * qt:
                    v = kb - 4 * qt
                    msk = b.cmask[:, v * 512:(v + 1) * 512]
                    if mode == "causal":
                        p.emit("dve", lambda e, o=ptt, m=msk: e.tensor_mul(o, o, m),
                               reads=[ptkey, (b.cmask, "a")], writes=[ptkey])
                    else:
                        iv = b.cminv[:, v * 512:(v + 1) * 512]
                        p.emit("dve", lambda e, m=iv, v2=v: e.tensor_mul(
                                   b.attsq[:, 0:512], b.ptc[:, :], m),
                               reads=[(b.cminv, "a"), (b.ptc, "a")],
                               writes=[(b.attsq, "a")])
                        p.emit("dve", lambda e, o=ptt, m=msk: e.tensor_mul(o, o, m),
                               reads=[ptkey, (b.cmask, "a")], writes=[ptkey])
                        p.emit("dve", lambda e, o=ptt: e.tensor_add(
                                   o, o, b.attsq[:, 0:512]),
                               reads=[ptkey, (b.attsq, "a")], writes=[ptkey])

            def emit_av(kb, first_, stop_):
                ptt = b.XB[:, (kb % 16) * 512:(kb % 16) * 512 + 512]
                ptkey = (b.XB, ("g", kb % 16))
                p.emit("pe", lambda e, kb=kb, o=po, pt_=ptt, s=first_, st=stop_, h=h:
                           e.matmul(
                           o[0:65, :], b.U1[:, kb * 1040 + h * 65: kb * 1040 + h * 65 + 65],
                           pt_, start=s, stop=st),
                       reads=[ptkey] + u1g(kb * 1040 + h * 65, kb * 1040 + h * 65 + 65),
                       writes=[(po, "a")])

            for kb in range(kb_hi):
                emit_st_exp(kb)
                if kb >= LAG:
                    av = kb - LAG
                    emit_av(av, first, (av == last_kb) and (mode != "zmask"))
                    first = False
            for av in range(max(0, kb_hi - LAG), kb_hi):
                emit_av(av, first, (av == last_kb) and (mode != "zmask"))
                first = False
            if mode == "zmask":
                p.emit("pe", lambda e, o=po, h=h, s=first: e.matmul(
                           o[0:65, :], b.vfut[:, h * 65: h * 65 + 65], b.ptc[:, :],
                           start=s, stop=True),
                       reads=[(b.vfut, "a"), (b.ptc, "a")], writes=[(po, "a")])
            invz = b.rows[0:1, 1024:1536]
            p.emit("dve", lambda e, o=invz, i=po: e.reciprocal(o, i[64:65, :]),
                   reads=[(po, "a")], writes=[("rows", "z")])
            pc2 = b.ps["pg"]
            p.emit("pe", lambda e, o=pc2: e.matmul(o[0:64, :], b.ones_row[0:1, 0:64],
                                                   invz, start=True, stop=True),
                   reads=[("rows", "z"), (b.ones_row, "a")], writes=[(pc2, "a")])
            p.emit("dve", lambda e: e.tensor_copy(b.bsb[0:64, :], pc2[0:64, :]),
                   reads=[(pc2, "a")], writes=[(b.bsb, "a")])
            ob = qt % 4
            osl = b.osb[:, ob * 512:(ob + 1) * 512]
            p.emit("dve", lambda e, o=osl, i=po: e.tensor_mul(o[0:64, :], i[0:64, :],
                                                              b.bsb[0:64, :]),
                   reads=[(po, "a"), (b.bsb, "a")], writes=[(b.osb, ("o", ob))])
            p.dma("qact", out_dram.ap()[h * 64:(h + 1) * 64, qt * 512:(qt + 1) * 512],
                  b.osb[0:64, ob * 512:(ob + 1) * 512], reads=[(b.osb, ("o", ob))],
                  writes=dk(out_dram, h * 64, (h + 1) * 64, qt * 512, (qt + 1) * 512))



def gemm_tm(p, b, in_dram, w_dram, Din, Dout, *,
            out_fm=None, out_tm=None, bias_row=None, rope=None, rope_tab=None,
            scale=1.0, tag="gt"):
    nkc = Din // 128
    assert nkc <= 8
    if bias_row is not None:
        p.dma("qsp", b.rows[0:1, 0:Dout], bias_row.ap()[0:1, 0:Dout],
              writes=[("rows", "br")])
        p.emit("dve", lambda e: e.tensor_copy(b.rowbf[0:1, 0:Dout], b.rows[0:1, 0:Dout]),
               reads=[("rows", "br")], writes=[(b.rowbf, "a")])
    for g0 in range(0, Dout, 512):
        gw = min(512, Dout - g0)
        base = load_w(p, b, w_dram, Din, g0, gw, tag=tag)
        wkeys = u1g(base, base + nkc * gw)
        for tt in range(NTT):
            par = tt % 2
            xoff = par * 4096
            for kc in range(nkc):
                src = in_dram.ap()[kc * 128:(kc + 1) * 128, tt * TTW:(tt + 1) * TTW]
                p.dma("qsp", b.XB[:, xoff + kc * TTW: xoff + (kc + 1) * TTW], src,
                      reads=dk(in_dram, kc * 128, (kc + 1) * 128, tt * TTW, (tt + 1) * TTW),
                      writes=[(b.XB, ("g", xoff // 512 + kc))])
            if rope is not None:
                pass
            for sub in range(4):
                t0 = tt * TTW + sub * 128
                ps = b.ps["pa"] if (sub % 2 == 0) else b.ps["pb"]
                if bias_row is not None:
                    p.emit("pe", lambda e, o=ps, g0=g0, gw=gw: e.matmul(
                               o[:, 0:gw], b.ones_rowb[0:1, :],
                               b.rowbf[0:1, g0:g0 + gw], start=True, stop=False),
                           reads=[(b.rowbf, "a"), (b.ones_rowb, "a")], writes=[(ps, "a")])
                for kc in range(nkc):
                    lhsT = b.XB[:, xoff + kc * TTW + sub * 128: xoff + kc * TTW + sub * 128 + 128]
                    rhs = w_tile(b, base, gw, kc, 0, gw)
                    p.emit("pe", lambda e, o=ps, l=lhsT, r=rhs, gw=gw,
                                  s=(kc == 0 and bias_row is None), t=(kc == nkc - 1):
                               e.matmul(o[:, 0:gw], l, r, start=s, stop=t),
                           reads=wkeys + [(b.XB, ("g", xoff // 512 + kc))],
                           writes=[(ps, "a")])
                ob = sub % 2
                osl = b.osb[:, ob * 512: ob * 512 + gw]
                okey = (b.osb, ("o", ob))
                if rope is None:
                    p.emit("act", lambda e, o=osl, i=ps, gw=gw, sc=scale: e.activation(
                               out=o, in_=i[:, 0:gw], func=AF.Copy, scale=sc),
                           reads=[(ps, "a")], writes=[okey])
                else:
                    import os as _os
                    rmode = _os.environ.get("ROPE_MODE", "full")
                    p.dma("qsp", b.rope[:, 0:512], rope_tab.ap()[t0:t0 + 128, 0:512],
                          writes=[(b.rope, "a")])
                    if rmode == "copy":
                        p.emit("act", lambda e, o=osl, i=ps, gw=gw: e.activation(
                                   out=o, in_=i[:, 0:gw], func=AF.Copy),
                               reads=[(ps, "a"), (b.rope, "a")], writes=[okey])
                    cos = b.rope[:, 0:256]
                    sin = b.rope[:, 256:512]
                    if rope == "il":
                        qe = bass.AP(ps, 0, [[512, 128], [2, 256]])
                        qo = bass.AP(ps, 1, [[512, 128], [2, 256]])
                        oe = bass.AP(b.osb, ob * 512, [[2048, 128], [2, 256]])
                        oo = bass.AP(b.osb, ob * 512 + 1, [[2048, 128], [2, 256]])
                    else:
                        qe = bass.AP(ps, 0, [[512, 128], [64, 8], [1, 32]])
                        qo = bass.AP(ps, 32, [[512, 128], [64, 8], [1, 32]])
                        oe = bass.AP(b.osb, ob * 512, [[2048, 128], [64, 8], [1, 32]])
                        oo = bass.AP(b.osb, ob * 512 + 32, [[2048, 128], [64, 8], [1, 32]])
                    if rmode == "copy":
                        if out_tm is not None:
                            p.dma("qact", out_tm.ap()[t0:t0 + 128, g0:g0 + gw],
                                  b.osb[:, ob * 512: ob * 512 + gw], reads=[okey],
                                  writes=dk(out_tm, t0, t0 + 128, g0, g0 + gw))
                        continue
                    cs3 = bass.AP(b.rope, 0, [[1024, 128], [32, 8], [1, 32]]) \
                        if rope == "half" else cos
                    sn3 = bass.AP(b.rope, 256, [[1024, 128], [32, 8], [1, 32]]) \
                        if rope == "half" else sin
                    r0 = b.rtmp[:, 0:256]
                    r1 = b.rtmp[:, 256:512]
                    r2 = b.rtmp[:, 512:768]
                    r3 = b.rtmp[:, 768:1024]
                    rkey = (b.rtmp, "a")
                    p.emit("dve", lambda e, a=r0, x=qe, c=cs3: e.tensor_mul(a, x, c),
                           reads=[(ps, "a"), (b.rope, "a")], writes=[rkey])
                    p.emit("dve", lambda e, a=r1, x=qo, c=sn3: e.tensor_mul(a, x, c),
                           reads=[(ps, "a"), (b.rope, "a")], writes=[rkey])
                    p.emit("dve", lambda e, a=r2, x=qe, c=sn3: e.tensor_mul(a, x, c),
                           reads=[(ps, "a"), (b.rope, "a")], writes=[rkey])
                    p.emit("dve", lambda e, a=r3, x=qo, c=cs3: e.tensor_mul(a, x, c),
                           reads=[(ps, "a"), (b.rope, "a")], writes=[rkey])
                    if rmode == "muls":
                        p.emit("act", lambda e, o=osl, i=ps, gw=gw: e.activation(
                                   out=o, in_=i[:, 0:gw], func=AF.Copy),
                               reads=[(ps, "a"), rkey], writes=[okey])
                    else:
                        p.emit("dve", lambda e, o=oe, a=r0, c=r1: e.tensor_sub(o, a, c),
                               reads=[rkey], writes=[okey])
                        p.emit("dve", lambda e, o=oo, a=r2, c=r3: e.tensor_add(o, a, c),
                               reads=[rkey], writes=[okey])
                if out_tm is not None:
                    p.dma("qact", out_tm.ap()[t0:t0 + 128, g0:g0 + gw],
                          b.osb[:, ob * 512: ob * 512 + gw], reads=[okey],
                          writes=dk(out_tm, t0, t0 + 128, g0, g0 + gw))
                if out_fm is not None:
                    for fsub in range(gw // 128):
                        tp = b.pst[:, 0:128]
                        p.emit("pe", lambda e, o=tp, i=b.osb[:, ob * 512 + fsub * 128:
                                                             ob * 512 + fsub * 128 + 128]:
                                   e.transpose(o, i, b.idb[:, :]),
                               reads=[okey, (b.idb, "a")], writes=[(b.pst, "t")])
                        ok2 = (b.osb, ("o", 2 + fsub % 2))
                        od = b.osb[:, 1024 + (fsub % 2) * 512: 1024 + (fsub % 2) * 512 + 128]
                        p.emit("dve", lambda e, o=od, i=tp: e.tensor_copy(o, i),
                               reads=[(b.pst, "t")], writes=[ok2])
                        p.dma("qact", out_fm.ap()[g0 + fsub * 128: g0 + fsub * 128 + 128,
                                                  t0:t0 + 128],
                              od, reads=[ok2],
                              writes=dk(out_fm, g0 + fsub * 128, g0 + fsub * 128 + 128,
                                        t0, t0 + 128))


def addpass(p, b, out_dram, a_dram, b_dram, nrow=1024):
    for kc in range(nrow // 128):
        for tt in range(NTT):
            asl = b.S[:, (tt % 4) * 512:(tt % 4) * 512 + 512]
            bsl = b.S[:, 2048 + (tt % 4) * 512: 2048 + (tt % 4) * 512 + 512]
            p.dma("qsp", asl, a_dram.ap()[kc * 128:(kc + 1) * 128, tt * TTW:(tt + 1) * TTW],
                  reads=dk(a_dram, kc * 128, (kc + 1) * 128, tt * TTW, (tt + 1) * TTW),
                  writes=[(b.S, ("g", tt % 4))])
            p.dma("qsp", bsl, b_dram.ap()[kc * 128:(kc + 1) * 128, tt * TTW:(tt + 1) * TTW],
                  reads=dk(b_dram, kc * 128, (kc + 1) * 128, tt * TTW, (tt + 1) * TTW),
                  writes=[(b.S, ("g", 4 + tt % 4))])
            ob = tt % 4
            osl = b.osf[:, ob * 512:(ob + 1) * 512]
            p.emit("dve", lambda e, o=osl, x=asl, y=bsl: e.tensor_add(o, x, y),
                   reads=[(b.S, ("g", tt % 4)), (b.S, ("g", 4 + tt % 4))],
                   writes=[(b.osf, ("o", ob))])
            p.dma("qact", out_dram.ap()[kc * 128:(kc + 1) * 128, tt * TTW:(tt + 1) * TTW],
                  osl, reads=[(b.osf, ("o", ob))],
                  writes=dk(out_dram, kc * 128, (kc + 1) * 128, tt * TTW, (tt + 1) * TTW))



def _bc(b, col, n):
    return bass.AP(b.stash, col, [[640, 128], [0, n]])


def routing(p, b, t, *, x_bf, wg_dram, E, variant, bias_row=None,
            capacity=None, coef_dram=None, cc_idx=0, cc_in=None, cc_out=None,
            tag="rt"):
    wrows = t["wrows"]
    nkc = 8
    for kc in range(nkc):
        p.dma("qsp", b.wstg[:, 0:E], wg_dram.ap()[kc * 128:(kc + 1) * 128, 0:E],
              writes=[(b.wstg, "a")])
        p.emit("dve", lambda e, kc=kc: e.tensor_copy(b.wgt[:, kc * E:(kc + 1) * E],
                                                     b.wstg[:, 0:E]),
               reads=[(b.wstg, "a")], writes=[(b.wgt, "a")])
        if coef_dram is not None:
            p.dma("qsp", b.wstg[:, 0:2], coef_dram.ap()[kc * 128:(kc + 1) * 128, 0:2],
                  writes=[(b.wstg, "a")])
            p.emit("dve", lambda e, kc=kc: e.tensor_copy(b.wgt[:, 64 + kc * 2:64 + kc * 2 + 2],
                                                         b.wstg[:, 0:2]),
                   reads=[(b.wstg, "a")], writes=[(b.wgt, "c")])
    if bias_row is not None:
        p.dma("qsp", b.rows[0:1, 1792:1792 + E], bias_row.ap()[0:1, 0:E],
              writes=[("rows", "gb")])
        p.emit("dve", lambda e: e.tensor_copy(b.rowbf[0:1, 512:512 + E],
                                              b.rows[0:1, 1792:1792 + E]),
               reads=[("rows", "gb")], writes=[(b.rowbf, "gb")])
    cap_mode = capacity is not None
    if cap_mode:
        p.emit("dve", lambda e: e.memset(b.rows[0:1, 1536:1544], 0.0),
               writes=[("rows", "cnt")])
    if E < 8:
        p.emit("dve", lambda e: e.memset(b.stash[:, 320 + E:328], -3.0e38),
               writes=[(b.stash, "lg")])
    for tt in range(NTT):
        for kc in range(nkc):
            src = x_bf.ap()[kc * 128:(kc + 1) * 128, tt * TTW:(tt + 1) * TTW]
            p.dma("qsp", b.XB[:, kc * TTW:(kc + 1) * TTW], src,
                  reads=dk(x_bf, kc * 128, (kc + 1) * 128, tt * TTW, (tt + 1) * TTW),
                  writes=[(b.XB, ("g", kc))])
        for sub in range(4):
            t32 = tt * 4 + sub
            pg = b.ps["pg"]
            if bias_row is not None:
                p.emit("pe", lambda e, E=E: e.matmul(pg[:, 0:E], b.ones_rowb[0:1, :],
                                                     b.rowbf[0:1, 512:512 + E],
                                                     start=True, stop=False),
                       reads=[(b.rowbf, "gb"), (b.ones_rowb, "a")], writes=[(pg, "a")])
            for kc in range(nkc):
                lhsT = b.XB[:, kc * TTW + sub * 128: kc * TTW + sub * 128 + 128]
                p.emit("pe", lambda e, l=lhsT, kc=kc, E=E,
                              s=(kc == 0 and bias_row is None), st=(kc == nkc - 1):
                           e.matmul(pg[:, 0:E], l, b.wgt[:, kc * E:(kc + 1) * E],
                                    start=s, stop=st),
                       reads=[(b.XB, ("g", kc)), (b.wgt, "a")], writes=[(pg, "a")])
            lg = b.stash[:, 320:328]
            p.emit("dve", lambda e, E=E: e.tensor_copy(lg[:, 0:E], pg[:, 0:E]),
                   reads=[(pg, "a")], writes=[(b.stash, "lg")])
            if coef_dram is not None:
                pc = b.ps["pc"]
                for kc in range(nkc):
                    lhsT = b.XB[:, kc * TTW + sub * 128: kc * TTW + sub * 128 + 128]
                    p.emit("pe", lambda e, l=lhsT, kc=kc:
                               e.matmul(pc[:, 0:2], l, b.wgt[:, 64 + kc * 2:64 + kc * 2 + 2],
                                        start=(kc == 0), stop=(kc == nkc - 1)),
                           reads=[(b.XB, ("g", kc)), (b.wgt, "c")], writes=[(pc, "a")])
                cl = b.stash[:, 336:340]
                p.emit("dve", lambda e: e.tensor_copy(cl[:, 3:4], pc[:, 1:2]),
                       reads=[(pc, "a")], writes=[(b.stash, "clt")])
                p.emit("dve", lambda e: e.tensor_sub(cl[:, 0:1], pc[:, 0:1], cl[:, 3:4]),
                       reads=[(pc, "a"), (b.stash, "clt")], writes=[(b.stash, "cl")])
                p.emit("act", lambda e: e.activation(out=cl[:, 1:2], in_=cl[:, 0:1],
                                                     func=AF.Sigmoid),
                       reads=[(b.stash, "cl")], writes=[(b.stash, "cl2")])
                p.emit("dve", lambda e, t32=t32: e.tensor_scalar(
                           b.stash[:, 288 + 0:289] if False else b.stash[:, 416 + t32 // 4 * 0:417] if False else cl[:, 2:3],
                           cl[:, 1:2], -1.0, 1.0, OP.mult, OP.add),
                       reads=[(b.stash, "cl2")], writes=[(b.stash, "cl")])
                p.emit("dve", lambda e, t32=t32: e.tensor_copy(
                           b.stash[:, 448 + t32:449 + t32], cl[:, 1:2]),
                       reads=[(b.stash, "cl2")], writes=[(b.stash, ("c1", t32))])
                p.emit("dve", lambda e, t32=t32: e.tensor_copy(
                           b.stash[:, 480 + t32 * 0 + 0:1] if False else b.stash[:, 512 + t32:513 + t32], cl[:, 2:3]),
                       reads=[(b.stash, "cl")], writes=[(b.stash, ("c2", t32))])
            m8 = b.stash[:, 328:336]
            p.emit("dve", lambda e: e.max(m8, lg[:, 0:8]),
                   reads=[(b.stash, "lg")], writes=[(b.stash, "m8")])
            ohsl = b.stash[:, t32 * 8:(t32 + 1) * 8]
            p.emit("dve", lambda e, o=ohsl, E=E: e.tensor_tensor(
                       o[:, 0:E], lg[:, 0:E], _bc(b, 328, E), OP.is_equal),
                   reads=[(b.stash, "lg"), (b.stash, "m8")],
                   writes=[(b.stash, ("oh", t32))])
            import os as _os
            if _os.environ.get("RT_DBG") and t32 == 5:
                p.dma("qact", t["rtdbg"].ap()[:, 0:640], b.stash[:, 0:640],
                      reads=[(b.stash, sk) for sk in
                             [("oh", 5), "lg", "m8", "wt"]],
                      writes=dk(t["rtdbg"], 0, 128, 0, 640))
            if variant == "top1":
                wt = b.stash[:, 424:436]
                p.emit("dve", lambda e, o=wt, i=ohsl, E=E: e.tensor_copy(o[:, 0:E], i[:, 0:E]),
                       reads=[(b.stash, ("oh", t32))], writes=[(b.stash, "wt")])
                _route_transpose(p, b, t32, E, 0, wrows)
            elif variant == "top2":
                sc = b.stash[:, 352:368]
                p.emit("dve", lambda e: e.tensor_sub(sc[:, 9:10], m8[:, 1:2], m8[:, 0:1]),
                       reads=[(b.stash, "m8")], writes=[(b.stash, "d21")])
                p.emit("act", lambda e: e.activation(out=sc[:, 10:11], in_=sc[:, 9:10],
                                                     func=AF.Exp),
                       reads=[(b.stash, "d21")], writes=[(b.stash, "e21")])
                p.emit("dve", lambda e: e.tensor_scalar(sc[:, 11:12], sc[:, 10:11],
                                                        1.0, None, OP.add),
                       reads=[(b.stash, "e21")], writes=[(b.stash, "s21")])
                p.emit("dve", lambda e: e.reciprocal(sc[:, 12:13], sc[:, 11:12]),
                       reads=[(b.stash, "s21")], writes=[(b.stash, "w1")])
                p.emit("dve", lambda e: e.tensor_scalar(sc[:, 13:14], sc[:, 12:13],
                                                        -1.0, 1.0, OP.mult, OP.add),
                       reads=[(b.stash, "w1")], writes=[(b.stash, "w2")])
                wt = b.stash[:, 424:436]
                p.emit("dve", lambda e, o=wt, i=ohsl, E=E: e.tensor_tensor(
                           o[:, 0:E], i[:, 0:E], _bc(b, 364, E), OP.mult),
                       reads=[(b.stash, ("oh", t32)), (b.stash, "w1")],
                       writes=[(b.stash, "wt")])
                oh2 = b.stash[:, 368:376]
                p.emit("dve", lambda e, o=oh2, E=E: e.tensor_tensor(
                           o[:, 0:E], lg[:, 0:E], _bc(b, 329, E), OP.is_equal),
                       reads=[(b.stash, "lg"), (b.stash, "m8")],
                       writes=[(b.stash, "oh2")])
                p.emit("dve", lambda e, o=oh2, E=E: e.tensor_tensor(
                           o[:, 0:E], o[:, 0:E], _bc(b, 365, E), OP.mult),
                       reads=[(b.stash, "oh2"), (b.stash, "w2")],
                       writes=[(b.stash, "oh2")])
                p.emit("dve", lambda e, o=wt, i=oh2, E=E: e.tensor_add(
                           o[:, 0:E], o[:, 0:E], i[:, 0:E]),
                       reads=[(b.stash, "oh2"), (b.stash, "wt")], writes=[(b.stash, "wt")])
                _route_transpose(p, b, t32, E, 0, wrows)
            else:
                neg = b.stash[:, 360:361]
                p.emit("dve", lambda e: e.tensor_scalar(neg, m8[:, 0:1], -1.0, None, OP.mult),
                       reads=[(b.stash, "m8")], writes=[(b.stash, "neg")])
                e8 = b.stash[:, 368:376]
                p.emit("act", lambda e, E=E: e.activation(out=e8[:, 0:E], in_=lg[:, 0:E],
                                                          func=AF.Exp, bias=neg),
                       reads=[(b.stash, "lg"), (b.stash, "neg")], writes=[(b.stash, "e8")])
                ssum = b.stash[:, 361:362]
                p.emit("dve", lambda e, E=E: e.reduce_sum(ssum, e8[:, 0:E], axis=AX.X),
                       reads=[(b.stash, "e8")], writes=[(b.stash, "ss")])
                rr = b.stash[:, 362:363]
                p.emit("dve", lambda e: e.reciprocal(rr, ssum),
                       reads=[(b.stash, "ss")], writes=[(b.stash, "rr")])
                if variant == "top1cap":
                    tmp8 = b.stash[:, 376:384]
                    p.emit("dve", lambda e, i=ohsl, E=E: e.tensor_mul(
                               tmp8[:, 0:E], e8[:, 0:E], i[:, 0:E]),
                           reads=[(b.stash, "e8"), (b.stash, ("oh", t32))],
                           writes=[(b.stash, "t8")])
                    p.emit("dve", lambda e, E=E: e.reduce_sum(b.stash[:, 363:364],
                                                              tmp8[:, 0:E], axis=AX.X),
                           reads=[(b.stash, "t8")], writes=[(b.stash, "g1s")])
                    p.emit("dve", lambda e, t32=t32: e.tensor_mul(
                               b.stash[:, 256 + t32:257 + t32], b.stash[:, 363:364], rr),
                           reads=[(b.stash, "g1s"), (b.stash, "rr")],
                           sreads=[(b.stash, "rr")],
                           writes=[(b.stash, ("g1", t32))])
                else:
                    p.emit("dve", lambda e, t32=t32: e.tensor_copy(
                               b.stash[:, 256 + t32:257 + t32], rr),
                           reads=[(b.stash, "rr")], writes=[(b.stash, ("g1", t32))])
            if cap_mode:
                pr = b.ps["pr"]
                ohf = ohsl
                p.emit("pe", lambda e, i=ohf, E=E: e.matmul(pr[0:1, 0:E],
                                                            b.ones_redf[:, 0:1], i[:, 0:E],
                                                            start=True, stop=True),
                       reads=[(b.stash, ("oh", t32)), (b.ones_redf, "a")],
                       writes=[(pr, "a")])
                p.emit("dve", lambda e, E=E: e.tensor_add(b.rows[0:1, 1536:1536 + E],
                                                          b.rows[0:1, 1536:1536 + E],
                                                          pr[0:1, 0:E]),
                       reads=[(pr, "a"), ("rows", "cnt")], writes=[("rows", "cnt")])
    if not cap_mode:
        return
    p.emit("dve", lambda e, E=E: e.tensor_copy(b.osf[0:1, 0:E], b.rows[0:1, 1536:1536 + E]),
           reads=[("rows", "cnt")], writes=[(b.osf, ("o", 0))])
    p.dma("qact", cc_in.ap()[0:1, 0:E], b.osf[0:1, 0:E], reads=[(b.osf, ("o", 0))],
          writes=[(cc_in.name, "a")])
    ccs = "cc0" if cc_idx == 0 else "cc1"
    pp = p
    ccin_tok = p.writers.get(cc_in.name, {}).get("a")
    def cc_full(e):
        if ccin_tok:
            e.wait_ge(pp.sems[ccin_tok[0]], ccin_tok[1])
        e.collective_compute(
            "AllGather", OP.bypass, replica_groups=[list(range(8))],
            ins=[cc_in.ap()[0:1, 0:E]],
            outs=[cc_out.ap()[0:8, 0:E]],
        ).then_inc(pp.sems[ccs], 1)
    p.q["pool"].append(cc_full)
    p.writers.setdefault(cc_out.name, {})["a"] = (ccs, 1)
    p.count[ccs] = 1
    p.dma("qsp", b.rows[0:1, 1600:1600 + 8 * E], cc_out.ap()[0:8, 0:E],
          reads=[(cc_out.name, "a")], writes=[("rows", "allg")])
    p.emit("dve", lambda e, E=E: e.tensor_mul(b.rows[0:1, 1664:1664 + 8 * E],
                                              b.rows[0:1, 1600:1600 + 8 * E],
                                              b.coremask[0:1, 0:8 * E]),
           reads=[("rows", "allg"), (b.coremask, "a")], writes=[("rows", "mskd")])
    p.emit("dve", lambda e, E=E: e.tensor_reduce(
               b.rows[0:1, 1728:1728 + E],
               bass.AP(b.rows, 1664, [[2048, 1], [1, E], [E, 8]]),
               axis=AX.X, op=OP.add),
           reads=[("rows", "mskd")], writes=[("rows", "offs")])
    p.emit("dve", lambda e, E=E: e.tensor_copy(b.rows[0:1, 1736:1736 + E],
                                               b.rows[0:1, 1728:1728 + E]),
           reads=[("rows", "offs")], writes=[("rows", "run")])
    for tt in range(NTT):
        for sub in range(4):
            t32 = tt * 4 + sub
            ohsl = b.stash[:, t32 * 8:(t32 + 1) * 8]
            pg = b.ps["pg"]
            p.emit("pe", lambda e, i=ohsl, E=E: e.matmul(pg[:, 0:E], b.sut[:, :],
                                                         i[:, 0:E], start=True, stop=False),
                   reads=[(b.stash, ("oh", t32)), (b.sut, "a")], writes=[(pg, "a")])
            p.emit("pe", lambda e, E=E: e.matmul(pg[:, 0:E], b.ones_row[0:1, :],
                                                 b.rows[0:1, 1736:1736 + E],
                                                 start=False, stop=True),
                   reads=[("rows", "run"), (b.ones_row, "a")], writes=[(pg, "a")])
            pr = b.ps["pr"]
            p.emit("pe", lambda e, i=ohsl, E=E: e.matmul(pr[0:1, 0:E], b.ones_redf[:, 0:1],
                                                         i[:, 0:E], start=True, stop=True),
                   reads=[(b.stash, ("oh", t32)), (b.ones_redf, "a")], writes=[(pr, "a")])
            p.emit("dve", lambda e, E=E: e.tensor_add(b.rows[0:1, 1736:1736 + E],
                                                      b.rows[0:1, 1736:1736 + E],
                                                      pr[0:1, 0:E]),
                   reads=[(pr, "a"), ("rows", "run")], writes=[("rows", "run")])
            sc = b.stash[:, 376:384]
            p.emit("dve", lambda e, i=ohsl, E=E: e.tensor_mul(sc[:, 0:E], pg[:, 0:E],
                                                              i[:, 0:E]),
                   reads=[(pg, "a"), (b.stash, ("oh", t32))], writes=[(b.stash, "po8")])
            p.emit("dve", lambda e, E=E: e.reduce_sum(b.stash[:, 364:365], sc[:, 0:E],
                                                      axis=AX.X),
                   reads=[(b.stash, "po8")], writes=[(b.stash, "pos")])
            p.emit("dve", lambda e, cap=capacity: e.tensor_scalar(
                       b.stash[:, 365:366], b.stash[:, 364:365], float(cap), None, OP.is_lt),
                   reads=[(b.stash, "pos")], writes=[(b.stash, "kept")])
            wt = b.stash[:, 424:436]
            if variant == "top1cap":
                p.emit("dve", lambda e, t32=t32: e.tensor_mul(
                           b.stash[:, 366:367], b.stash[:, 256 + t32:257 + t32],
                           b.stash[:, 365:366]),
                       reads=[(b.stash, ("g1", t32)), (b.stash, "kept")],
                       writes=[(b.stash, "wv")])
                if coef_dram is not None:
                    p.emit("dve", lambda e, t32=t32: e.tensor_mul(
                               b.stash[:, 366:367], b.stash[:, 366:367],
                               b.stash[:, 512 + t32:513 + t32]),
                           reads=[(b.stash, ("c2", t32)), (b.stash, "wv")],
                           writes=[(b.stash, "wv")])
                p.emit("dve", lambda e, i=ohsl, E=E: e.tensor_tensor(
                           wt[:, 0:E], i[:, 0:E], _bc(b, 366, E), OP.mult),
                       reads=[(b.stash, ("oh", t32)), (b.stash, "wv")],
                       writes=[(b.stash, "wt")])
                if coef_dram is not None:
                    p.emit("dve", lambda e, t32=t32: e.tensor_copy(
                               b.stash[:, 416:417], b.stash[:, 448 + t32:449 + t32]),
                           reads=[(b.stash, ("c1", t32))], writes=[(b.stash, "xtr")])
                    _route_transpose(p, b, t32, E, 1, wrows)
                else:
                    _route_transpose(p, b, t32, E, 0, wrows)
            else:
                p.emit("dve", lambda e, t32=t32: e.tensor_mul(
                           b.stash[:, 366:367], b.stash[:, 256 + t32:257 + t32],
                           b.stash[:, 365:366]),
                       reads=[(b.stash, ("g1", t32)), (b.stash, "kept")],
                       writes=[(b.stash, "wv")])
                p.emit("dve", lambda e, i=ohsl, E=E: e.tensor_tensor(
                           wt[:, 0:E], i[:, 0:E], _bc(b, 366, E), OP.mult),
                       reads=[(b.stash, ("oh", t32)), (b.stash, "wv")],
                       writes=[(b.stash, "wt")])
                p.emit("dve", lambda e: e.tensor_scalar(
                           b.stash[:, 367:368], b.stash[:, 365:366], -1.0, 1.0,
                           OP.mult, OP.add),
                       reads=[(b.stash, "kept")], writes=[(b.stash, "ikept")])
                p.emit("dve", lambda e, t32=t32: e.tensor_mul(
                           b.stash[:, 416:417], b.stash[:, 256 + t32:257 + t32],
                           b.stash[:, 367:368]),
                       reads=[(b.stash, ("g1", t32)), (b.stash, "ikept")],
                       writes=[(b.stash, "xtr")])
                _route_transpose(p, b, t32, E, 1, wrows)


def _route_transpose(p, b, t32, E, extra, wrows=None):
    wt = b.stash[:, 424:436]
    ncol = E + extra
    if extra:
        p.emit("dve", lambda e: e.tensor_copy(wt[:, E:E + 1], b.stash[:, 416:417]),
               reads=[(b.stash, "xtr")], writes=[(b.stash, "wt")])
    pc = b.ps["pc"]
    p.emit("pe", lambda e, n=ncol: e.transpose(pc[0:n, 0:128], wt[:, 0:n], b.idf[:, :]),
           reads=[(b.stash, "wt"), (b.idf, "a")], writes=[(pc, "a")])
    p.emit("dve", lambda e, n=ncol, t32=t32: e.tensor_copy(
               b.wfm[0:n, t32 * 128:(t32 + 1) * 128], pc[0:n, 0:128]),
           reads=[(pc, "a")], writes=[(b.wfm, ("w", t32))])
    if wrows is not None:
        p.dma("qact", wrows.ap()[0:ncol, t32 * 128:(t32 + 1) * 128],
              b.wfm[0:ncol, t32 * 128:(t32 + 1) * 128],
              reads=[(b.wfm, ("w", t32))],
              writes=dk(wrows, 0, ncol, t32 * 128, (t32 + 1) * 128))



def moe_dense(p, b, t, *, x_bf, Et, Dh, act, w1_dram, w2_dram, b1_pcol, b2_pk,
              acc_dram, out_dram, final_adds=(), l3f_pass=False, nrows_b=None,
              Din=1024, Dout=1024, tag="moe"):
    nkc = Din // 128
    noc = Dout // 128
    if nrows_b is None:
        nrows_b = Et
    if b2_pk is not None:
        for r in range(nrows_b):
            p.dma("qsp", b.wstg[0:1, 0:Dout], b2_pk.ap()[r:r + 1, 0:Dout],
                  writes=[(b.wstg, "a")])
            p.emit("dve", lambda e: e.tensor_copy(b.rowbf[0:1, 0:Dout],
                                                  b.wstg[0:1, 0:Dout]),
                   reads=[(b.wstg, "a")], writes=[(b.rowbf, "a")])
            p.dma("qsp", b.b2s[r:r + 1, 0:Dout], b.rowbf[0:1, 0:Dout],
                  reads=[(b.rowbf, "a")], writes=[(b.b2s, "a")])
    ngq = (Dh // 1024) if Dh else 1
    W1B = 0
    W2B = 8192
    Y1B = 16384
    steps = [(e, g) for e in range(Et) for g in range(ngq)]
    for si, (e, g) in enumerate(steps):
        first = si == 0
        last = si == len(steps) - 1
        if Dh:
            for kc in range(nkc):
                stg, skey = wstg_slot(b)
                p.dma("qsp", stg[:, 0:1024],
                      w1_dram.ap()[e * Din + kc * 128: e * Din + (kc + 1) * 128,
                                   g * 1024:(g + 1) * 1024],
                      writes=[(b.wstg, skey)])
                dst = b.U1[:, W1B + kc * 1024: W1B + (kc + 1) * 1024]
                p.emit("dve", lambda e2, s=stg[:, 0:1024], d=dst: e2.tensor_copy(d, s),
                       reads=[(b.wstg, skey)],
                       writes=u1g(W1B + kc * 1024, W1B + (kc + 1) * 1024))
            if b1_pcol is not None:
                p.dma("qsp", b.bias[:, 0:8],
                      b1_pcol.ap()[:, e * (Dh // 128) + g * 8: e * (Dh // 128) + g * 8 + 8],
                      writes=[(b.bias, "a")])
            for ic in range(8):
                stg, skey = wstg_slot(b)
                p.dma("qsp", stg[:, 0:Dout],
                      w2_dram.ap()[e * Dh + g * 1024 + ic * 128: e * Dh + g * 1024 + (ic + 1) * 128,
                                   0:Dout],
                      writes=[(b.wstg, skey)])
                dst = b.U1[:, W2B + ic * Dout: W2B + (ic + 1) * Dout]
                p.emit("dve", lambda e2, s=stg[:, 0:Dout], d=dst: e2.tensor_copy(d, s),
                       reads=[(b.wstg, skey)],
                       writes=u1g(W2B + ic * Dout, W2B + (ic + 1) * Dout))
        else:
            for kc in range(nkc):
                stg, skey = wstg_slot(b)
                p.dma("qsp", stg[:, 0:Dout],
                      w2_dram.ap()[e * Din + kc * 128: e * Din + (kc + 1) * 128, 0:Dout],
                      writes=[(b.wstg, skey)])
                dst = b.U1[:, W2B + kc * Dout: W2B + (kc + 1) * Dout]
                p.emit("dve", lambda e2, s=stg[:, 0:Dout], d=dst: e2.tensor_copy(d, s),
                       reads=[(b.wstg, skey)],
                       writes=u1g(W2B + kc * Dout, W2B + (kc + 1) * Dout))
        w1keys = u1g(W1B, W1B + nkc * 1024)
        w2keys = u1g(W2B, W2B + 8 * Dout)
        for tt in range(NTT):
            for kc in range(nkc):
                src = x_bf.ap()[kc * 128:(kc + 1) * 128, tt * TTW:(tt + 1) * TTW]
                p.dma("qsp", b.XB[:, kc * TTW:(kc + 1) * TTW], src,
                      reads=dk(x_bf, kc * 128, (kc + 1) * 128, tt * TTW, (tt + 1) * TTW),
                      writes=[(b.XB, ("g", kc))])
            p.dma("qsp", b.rowbf[0:1, 0:512],
                  t["wrows"].ap()[e:e + 1, tt * TTW:(tt + 1) * TTW],
                  reads=dk(t["wrows"], e, e + 1, tt * TTW, (tt + 1) * TTW),
                  writes=[(b.rowbf, "a")])
            po = b.ps["po"]
            p.emit("pe", lambda e2: e2.matmul(po[:, :], b.ones_rowb[0:1, :],
                                              b.rowbf[0:1, 0:512], start=True, stop=True),
                   reads=[(b.rowbf, "a"), (b.ones_rowb, "a")], writes=[(po, "a")])
            p.emit("dve", lambda e2: e2.tensor_copy(b.bsb[:, :], po[:, :]),
                   reads=[(po, "a")], writes=[(b.bsb, "a")])
            if l3f_pass and first:
                p.dma("qsp", b.rowbf[0:1, 512:1024],
                      t["wrows"].ap()[Et:Et + 1, tt * TTW:(tt + 1) * TTW],
                      reads=dk(t["wrows"], Et, Et + 1, tt * TTW, (tt + 1) * TTW),
                      writes=[(b.rowbf, "p")])
                p.emit("pe", lambda e2: e2.matmul(po[:, :], b.ones_rowb[0:1, :],
                                                  b.rowbf[0:1, 512:1024],
                                                  start=True, stop=True),
                       reads=[(b.rowbf, "p"), (b.ones_rowb, "a")], writes=[(po, "a")])
                p.emit("dve", lambda e2: e2.tensor_copy(b.bsb2[:, :], po[:, :]),
                       reads=[(po, "a")], writes=[(b.bsb2, "a")])
            if Dh:
                for ic in range(8):
                    ps1 = b.ps["pa"] if ic % 2 == 0 else b.ps["pb"]
                    for kc in range(nkc):
                        lhsT = b.U1[:, W1B + kc * 1024 + ic * 128: W1B + kc * 1024 + ic * 128 + 128]
                        rhs = b.XB[:, kc * TTW:(kc + 1) * TTW]
                        p.emit("pe", lambda e2, o=ps1, l=lhsT, r=rhs, s=(kc == 0),
                                      st=(kc == nkc - 1):
                                   e2.matmul(o[:, :], l, r, start=s, stop=st),
                               reads=w1keys + [(b.XB, ("g", kc))], writes=[(ps1, "a")])
                    ysl = b.U1[:, Y1B + ic * 512: Y1B + (ic + 1) * 512]
                    if b1_pcol is not None:
                        p.emit("act", lambda e2, o=ysl, i=ps1, ic=ic: e2.activation(
                                   out=o, in_=i[:, :], func=act, bias=b.bias[:, ic:ic + 1]),
                               reads=[(ps1, "a"), (b.bias, "a")],
                               writes=u1g(Y1B + ic * 512, Y1B + (ic + 1) * 512))
                    else:
                        p.emit("act", lambda e2, o=ysl, i=ps1: e2.activation(
                                   out=o, in_=i[:, :], func=act),
                               reads=[(ps1, "a")],
                               writes=u1g(Y1B + ic * 512, Y1B + (ic + 1) * 512))
                y1keys = u1g(Y1B, Y1B + 4096)
            for oc in range(noc):
                ps2 = b.ps["pc"] if oc % 2 == 0 else b.ps["pr"]
                if Dh:
                    for ic in range(8):
                        lhsT = b.U1[:, W2B + ic * Dout + oc * 128: W2B + ic * Dout + oc * 128 + 128]
                        rhs = b.U1[:, Y1B + ic * 512: Y1B + (ic + 1) * 512]
                        p.emit("pe", lambda e2, o=ps2, l=lhsT, r=rhs, s=(ic == 0),
                                      st=(ic == 7):
                                   e2.matmul(o[:, :], l, r, start=s, stop=st),
                               reads=w2keys + y1keys, writes=[(ps2, "a")])
                else:
                    for kc in range(nkc):
                        lhsT = b.U1[:, W2B + kc * Dout + oc * 128: W2B + kc * Dout + oc * 128 + 128]
                        rhs = b.XB[:, kc * TTW:(kc + 1) * TTW]
                        p.emit("pe", lambda e2, o=ps2, l=lhsT, r=rhs, s=(kc == 0),
                                      st=(kc == nkc - 1):
                                   e2.matmul(o[:, :], l, r, start=s, stop=st),
                               reads=w2keys + [(b.XB, ("g", kc))], writes=[(ps2, "a")])
                ob = oc % 4
                tsl = b.osf[:, ob * 512:(ob + 1) * 512]
                p.emit("dve", lambda e2, o=tsl, i=ps2: e2.tensor_mul(o, i[:, :], b.bsb[:, :]),
                       reads=[(ps2, "a"), (b.bsb, "a")], writes=[(b.osf, ("o", ob))])
                if first:
                    if b2_pk is not None:
                        pg = b.ps["pg"]
                        p.emit("pe", lambda e2, oc=oc, tt=tt, n=nrows_b: e2.matmul(
                                   pg[:, :], b.b2s[0:n, oc * 128:(oc + 1) * 128],
                                   b.wfm[0:n, tt * TTW:(tt + 1) * TTW],
                                   start=True, stop=True),
                               reads=[(b.b2s, "a")] +
                                     [(b.wfm, ("w", tt * 4 + s)) for s in range(4)],
                               writes=[(pg, "a")])
                        p.emit("dve", lambda e2, o=tsl: e2.tensor_add(o, o, pg[:, :]),
                               reads=[(pg, "a"), (b.osf, ("o", ob))],
                               writes=[(b.osf, ("o", ob))])
                    if l3f_pass:
                        xsl = b.XB[:, oc * TTW:(oc + 1) * TTW]
                        psl = b.S[:, 2048 + (ob % 4) * 512: 2048 + (ob % 4) * 512 + 512]
                        p.emit("dve", lambda e2, o=psl, x=xsl: e2.tensor_mul(o, x, b.bsb2[:, :]),
                               reads=[(b.XB, ("g", oc)), (b.bsb2, "a")],
                               writes=[(b.S, ("g", 4 + ob))])
                        p.emit("dve", lambda e2, o=tsl, x2=psl: e2.tensor_add(o, o, x2),
                               reads=[(b.S, ("g", 4 + ob)), (b.osf, ("o", ob))],
                               writes=[(b.osf, ("o", ob))])
                else:
                    asl = b.S[:, 4096 + ob * 512: 4096 + (ob + 1) * 512]
                    p.dma("qsp", asl,
                          acc_dram.ap()[oc * 128:(oc + 1) * 128, tt * TTW:(tt + 1) * TTW],
                          reads=dk(acc_dram, oc * 128, (oc + 1) * 128, tt * TTW,
                                   (tt + 1) * TTW),
                          writes=[(b.S, ("g", 8 + ob))])
                    p.emit("dve", lambda e2, o=tsl, a=asl: e2.tensor_add(o, o, a),
                           reads=[(b.S, ("g", 8 + ob)), (b.osf, ("o", ob))],
                           writes=[(b.osf, ("o", ob))])
                if last:
                    for fi, fd in enumerate(final_adds):
                        fsl = b.S[:, 6144 + ((ob + fi) % 4) * 512: 6144 + ((ob + fi) % 4) * 512 + 512]
                        p.dma("qsp", fsl,
                              fd.ap()[oc * 128:(oc + 1) * 128, tt * TTW:(tt + 1) * TTW],
                              reads=dk(fd, oc * 128, (oc + 1) * 128, tt * TTW, (tt + 1) * TTW),
                              writes=[(b.S, ("g", 12 + (ob + fi) % 4))])
                        p.emit("dve", lambda e2, o=tsl, a=fsl: e2.tensor_add(o, o, a),
                               reads=[(b.S, ("g", 12 + (ob + fi) % 4)), (b.osf, ("o", ob))],
                               writes=[(b.osf, ("o", ob))])
                    tgt = out_dram
                else:
                    tgt = acc_dram
                p.dma("qact", tgt.ap()[oc * 128:(oc + 1) * 128, tt * TTW:(tt + 1) * TTW],
                      tsl, reads=[(b.osf, ("o", ob))],
                      writes=dk(tgt, oc * 128, (oc + 1) * 128, tt * TTW, (tt + 1) * TTW))



def ssm(p, b, t, pj2, A_pcol, C_pcol, out_bf):
    p.dma("qsp", b.dec[:, 16:24], A_pcol.ap()[:, 0:8], writes=[(b.dec, "A")])
    p.dma("qsp", b.dec[:, 8:16], C_pcol.ap()[:, 0:8], writes=[(b.dec, "C")])
    p.emit("act", lambda e: e.activation(out=b.dec[:, 0:8], in_=b.dec[:, 16:24],
                                         func=AF.Sigmoid),
           reads=[(b.dec, "A")], writes=[(b.dec, "d")])
    for cc in range(8):
        p.dma("qsp", b.S[:, 0:4096], pj2.ap()[cc * 128:(cc + 1) * 128, 0:T],
              reads=dk(pj2, cc * 128, (cc + 1) * 128, 0, T),
              writes=[(b.S, ("g", gg)) for gg in range(8)])
        dec_ap = bass.AP(b.dec, cc, [[32, 128], [0, T]])
        p.emit("dve", lambda e, d=dec_ap: e.tensor_tensor_scan(
                   b.S[:, 4096:8192], d, b.S[:, 0:4096], 0.0, OP.mult, OP.add),
               reads=[(b.S, ("g", gg)) for gg in range(8)] + [(b.dec, "d")],
               writes=[(b.S, ("g", gg)) for gg in range(8, 16)])
        p.dma("qsp", b.S[:, 0:4096], pj2.ap()[1024 + cc * 128: 1024 + (cc + 1) * 128, 0:T],
              reads=dk(pj2, 1024 + cc * 128, 1024 + (cc + 1) * 128, 0, T),
              writes=[(b.S, ("g", gg)) for gg in range(8)])
        p.emit("act", lambda e: e.activation(out=b.S[:, 0:4096], in_=b.S[:, 0:4096],
                                             func=AF.Sigmoid),
               reads=[(b.S, ("g", gg)) for gg in range(8)],
               writes=[(b.S, ("g", gg)) for gg in range(8)])
        for tt in range(NTT):
            ysl = b.osf[:, (tt % 4) * 512:(tt % 4) * 512 + 512]
            p.emit("dve", lambda e, o=ysl, tt=tt: e.tensor_mul(
                       o, b.S[:, 4096 + tt * 512:4096 + (tt + 1) * 512],
                       b.S[:, tt * 512:(tt + 1) * 512]),
                   reads=[(b.S, ("g", 8 + tt)), (b.S, ("g", tt))],
                   writes=[(b.osf, ("o", tt % 4))])
            obf = b.osb[:, (tt % 4) * 512:(tt % 4) * 512 + 512]
            p.emit("dve", lambda e, o=obf, i=ysl, cc=cc: e.tensor_scalar(
                       o, i, b.dec[:, 8 + cc:9 + cc], None, OP.mult),
                   reads=[(b.osf, ("o", tt % 4)), (b.dec, "C")],
                   writes=[(b.osb, ("o", tt % 4))])
            p.dma("qact", out_bf.ap()[cc * 128:(cc + 1) * 128, tt * TTW:(tt + 1) * TTW],
                  obf, reads=[(b.osb, ("o", tt % 4))],
                  writes=dk(out_bf, cc * 128, (cc + 1) * 128, tt * TTW, (tt + 1) * TTW))



def emit_l0a(p, b, t):
    y, lnb = t["y"], t["lnb"]
    norm_fm(p, b, lnb, y, 1024)
    gemm_tm(p, b, lnb, t["w_l0a_q"], 1024, 1024, out_fm=t["qf"],
            rope="il", rope_tab=t["c_ropeA_q"])
    gemm_tm(p, b, lnb, t["w_l0a_k"], 1024, 1024, out_fm=t["kf"],
            rope="il", rope_tab=t["c_ropeA_k"])
    gemm_tm(p, b, lnb, t["w_l0a_v"], 1024, 1024, out_tm=t["vtm"])
    attention(p, b, out_dram=t["ao"], q_fm=t["qf"], k_fm=t["kf"], v_tm=t["vtm"],
              mode="none")
    gemm_fm(p, b, y, t["ao"], t["w_l0a_o"], 1024, 1024, resid_dram=y)


def emit_l0f(p, b, t):
    y, lnb, lnf, aob = t["y"], t["lnb"], t["lnf"], t["aob"]
    norm_fm(p, b, lnb, y, 1024, out_f32=lnf)
    norm_fm(p, b, aob, lnf, 1024, eps=1e-12, gamma_pcol=t["p_l0f_nw"],
            beta_pcol=t["p_l0f_nb"])
    routing(p, b, t, x_bf=aob, wg_dram=t["w_l0f_g"], E=8, variant="top1cap",
            capacity=4096, coef_dram=t["w_l0f_c"], cc_idx=0,
            cc_in=t["ccin0"], cc_out=t["ccout0"])
    moe_dense(p, b, t, x_bf=aob, Et=9, Dh=4096, act=AF.Gelu_apprx_tanh,
              w1_dram=t["w_l0f_1"], w2_dram=t["w_l0f_2"], b1_pcol=t["p_l0f_b1"],
              b2_pk=t["b_l0f_2"], acc_dram=t["macc"], out_dram=y,
              final_adds=(lnf, y), nrows_b=9)


def emit_dbg_rt(p, b, t):
    y, lnb = t["y"], t["lnb"]
    norm_fm(p, b, lnb, y, 1024)
    routing(p, b, t, x_bf=lnb, wg_dram=t["w_l1f_g"], E=4, variant="top1",
            bias_row=t["r_l1f_gb"])


def emit_l1a(p, b, t):
    y, lnb = t["y"], t["lnb"]
    norm_fm(p, b, lnb, y, 1024)
    gemm_fm(p, b, t["shf"], lnb, t["w_l1a_sh"], 1024, 128,
            bias_pcol=t["p_l1a_shb"])
    gemm_fm(p, b, t["qpf"], lnb, t["w_l1a_q"], 1024, 768,
            bias_pcol=t["p_l1a_bq"], scale=0.125, bias_scale=0.125)
    gemm_fm(p, b, t["kpf"], lnb, t["w_l1a_k"], 1024, 768,
            bias_pcol=t["p_l1a_bk"])
    gemm_tm(p, b, lnb, t["w_l1a_v"], 1024, 1024, out_tm=t["vtm"],
            bias_row=t["r_l1a_bv"])

    def q_prep(h):
        p.dma("qsp", b.qsb[0:16, :], t["shf"].ap()[0:16, :],
              reads=dk(t["shf"], 0, 16, 0, T), writes=[(b.qsb, "hd")])
        p.emit("act", lambda e: e.activation(out=b.qsb[0:16, :], in_=b.qsb[0:16, :],
                                             func=AF.Copy, scale=0.125),
               reads=[(b.qsb, "hd")], writes=[(b.qsb, "hd")])
        p.dma("qsp", b.qsb[16:64, :], t["qpf"].ap()[h * 48:(h + 1) * 48, :],
              reads=dk(t["qpf"], h * 48, (h + 1) * 48, 0, T), writes=[(b.qsb, "hd")])

    def k_prep(h):
        p.dma("qsp", b.kaug[0:16, :], t["shf"].ap()[0:16, :],
              reads=dk(t["shf"], 0, 16, 0, T), writes=[(b.kaug, "hd")])
        p.dma("qsp", b.kaug[16:64, :], t["kpf"].ap()[h * 48:(h + 1) * 48, :],
              reads=dk(t["kpf"], h * 48, (h + 1) * 48, 0, T), writes=[(b.kaug, "hd")])

    attention(p, b, out_dram=t["ao"], v_tm=t["vtm"], mode="causal",
              q_prep=q_prep, k_prep=k_prep)
    gemm_fm(p, b, y, t["ao"], t["w_l1a_o"], 1024, 1024,
            bias_pcol=t["p_l1a_bo"], resid_dram=y)


def emit_l1f(p, b, t):
    y, lnb, u = t["y"], t["lnb"], t["u"]
    norm_fm(p, b, lnb, y, 1024, out_f32=u)
    norm_fm(p, b, t["r1b"], u, 1024, kind="rms", eps=1e-6, gamma_pcol=t["p_l1f_n1"])
    gemm_tm(p, b, t["r1b"], t["w_l1f_q"], 1024, 1024, out_fm=t["qf"],
            rope="half", rope_tab=t["c_ropeB_q"])
    gemm_fm(p, b, t["latr"], t["r1b"], t["w_l1f_dkv"], 1024, 512, out_dtype="f32")
    norm_fm(p, b, t["latb"], t["latr"], 512, eps=1e-5, gamma_pcol=t["p_l1f_lng"],
            beta_pcol=t["p_l1f_lnb"])
    gemm_tm(p, b, t["latb"], t["w_l1f_uk"], 512, 1024, out_fm=t["kf"],
            rope="half", rope_tab=t["c_ropeB_k"])
    gemm_tm(p, b, t["latb"], t["w_l1f_uv"], 512, 1024, out_tm=t["vtm"])
    attention(p, b, out_dram=t["ao"], q_fm=t["qf"], k_fm=t["kf"], v_tm=t["vtm"],
              mode="causal")
    gemm_fm(p, b, u, t["ao"], t["w_l1f_o"], 1024, 1024, resid_dram=u)
    norm_fm(p, b, t["r2b"], u, 1024, kind="rms", eps=1e-6, gamma_pcol=t["p_l1f_n2"])
    routing(p, b, t, x_bf=t["r2b"], wg_dram=t["w_l1f_g"], E=4, variant="top1",
            bias_row=t["r_l1f_gb"])
    moe_dense(p, b, t, x_bf=t["r2b"], Et=4, Dh=4096, act=AF.Relu,
              w1_dram=t["w_l1f_1"], w2_dram=t["w_l1f_2"], b1_pcol=t["p_l1f_b1"],
              b2_pk=t["b_l1f_2"], acc_dram=t["macc"], out_dram=u, final_adds=(u,))
    addpass(p, b, y, y, u)


def emit_l2a(p, b, t):
    y, lnb = t["y"], t["lnb"]
    norm_fm(p, b, lnb, y, 1024)
    gemm_fm(p, b, t["pj2"], lnb, t["w_l2a_in"], 1024, 2048,
            bias_pcol=t["p_l2a_bin"], out_dtype="f32")
    ssm(p, b, t, t["pj2"], t["p_l2a_A"], t["p_l2a_C"], t["sdy"])
    gemm_fm(p, b, y, t["sdy"], t["w_l2a_out"], 1024, 1024,
            bias_pcol=t["p_l2a_bout"], resid_dram=y)


def emit_l2f(p, b, t):
    y, lnb = t["y"], t["lnb"]
    norm_fm(p, b, lnb, y, 1024)
    routing(p, b, t, x_bf=lnb, wg_dram=t["w_l2f_g"], E=8, variant="top2")
    moe_dense(p, b, t, x_bf=lnb, Et=8, Dh=2048, act=AF.Silu,
              w1_dram=t["w_l2f_1"], w2_dram=t["w_l2f_2"], b1_pcol=None,
              b2_pk=None, acc_dram=t["macc"], out_dram=y, final_adds=(y,))


def emit_l3a(p, b, t):
    y, lnb = t["y"], t["lnb"]
    norm_fm(p, b, lnb, y, 1024)
    gemm_fm(p, b, t["qf"], lnb, t["w_l3a_q"], 1024, 1024, scale=0.125)
    gemm_fm(p, b, t["kf"], lnb, t["w_l3a_k"], 1024, 1024)
    gemm_tm(p, b, lnb, t["w_l3a_v"], 1024, 1024, out_tm=t["vtm"])
    attention(p, b, out_dram=t["ao"], q_fm=t["qf"], k_fm=t["kf"], v_tm=t["vtm"],
              mode="causal")
    gemm_fm(p, b, y, t["ao"], t["w_l3a_o"], 1024, 1024, resid_dram=y)


def emit_l3f(p, b, t):
    y, lnb, lnf = t["y"], t["lnb"], t["lnf"]
    norm_fm(p, b, lnb, y, 1024, out_f32=lnf)
    gemm_fm(p, b, t["qf"], lnb, t["w_l3f_q"], 1024, 1024,
            bias_pcol=t["p_l3f_bq"], scale=1.0 / 32.0, bias_scale=1.0 / 32.0)
    gemm_fm(p, b, t["kf"], lnb, t["w_l3f_k"], 1024, 1024, bias_pcol=t["p_l3f_bk"])
    gemm_tm(p, b, lnb, t["w_l3f_v"], 1024, 1024, out_tm=t["vtm"],
            bias_row=t["r_l3f_bv"])
    attention(p, b, out_dram=t["ao"], q_fm=t["qf"], k_fm=t["kf"], v_tm=t["vtm"],
              mode="zmask")
    routing(p, b, t, x_bf=t["ao"], wg_dram=t["w_l3f_g"], E=8, variant="switch",
            bias_row=t["r_l3f_gb"], capacity=4915, cc_idx=1,
            cc_in=t["ccin1"], cc_out=t["ccout1"])
    moe_dense(p, b, t, x_bf=t["ao"], Et=8, Dh=None, act=None,
              w1_dram=None, w2_dram=t["w_l3f_e"], b1_pcol=None,
              b2_pk=t["b_l3f_e"], acc_dram=t["macc"], out_dram=t["x2t"],
              final_adds=(lnf,), l3f_pass=True)
    norm_fm(p, b, lnb, t["x2t"], 1024, eps=1e-5, gamma_pcol=t["p_l3f_g"],
            beta_pcol=t["p_l3f_b"], final_resid=y, final_out=y)


def emit_l0a_v(p, b, t):
    y, lnb = t["y"], t["lnb"]
    norm_fm(p, b, lnb, y, 1024)
    gemm_tm(p, b, lnb, t["w_l0a_v"], 1024, 1024, out_tm=t["vtm"])


def emit_l0a_q(p, b, t):
    y, lnb = t["y"], t["lnb"]
    norm_fm(p, b, lnb, y, 1024)
    gemm_tm(p, b, lnb, t["w_l0a_q"], 1024, 1024, out_fm=t["qf"],
            rope="il", rope_tab=t["c_ropeA_q"])


def emit_dbg_ropetm(p, b, t):
    y, lnb = t["y"], t["lnb"]
    norm_fm(p, b, lnb, y, 1024)
    gemm_tm(p, b, lnb, t["w_l0a_q"], 1024, 1024, out_tm=t["vtm"],
            rope="il", rope_tab=t["c_ropeA_q"])


def emit_dbg_tp(p, b, t):
    y, lnb = t["y"], t["lnb"]
    norm_fm(p, b, lnb, y, 1024)
    gemm_tm(p, b, lnb, t["w_l0a_q"], 1024, 1024, out_fm=t["qf"])


def emit_l0a_qkv(p, b, t):
    y, lnb = t["y"], t["lnb"]
    norm_fm(p, b, lnb, y, 1024)
    gemm_tm(p, b, lnb, t["w_l0a_q"], 1024, 1024, out_fm=t["qf"],
            rope="il", rope_tab=t["c_ropeA_q"])
    gemm_tm(p, b, lnb, t["w_l0a_k"], 1024, 1024, out_fm=t["kf"],
            rope="il", rope_tab=t["c_ropeA_k"])
    gemm_tm(p, b, lnb, t["w_l0a_v"], 1024, 1024, out_tm=t["vtm"])


def emit_l0a_att(p, b, t):
    emit_l0a_qkv(p, b, t)
    attention(p, b, out_dram=t["ao"], q_fm=t["qf"], k_fm=t["kf"], v_tm=t["vtm"],
              mode="none")


BLOCK_EMIT = [("dbg_rt", emit_dbg_rt), ("dbg_ropetm", emit_dbg_ropetm), ("dbg_tp", emit_dbg_tp),
              ("l0a_v", emit_l0a_v), ("l0a_q", emit_l0a_q),
              ("l0a_qkv", emit_l0a_qkv), ("l0a_att", emit_l0a_att),
              ("l0a", emit_l0a), ("l0f", emit_l0f), ("l1a", emit_l1a),
              ("l1f", emit_l1f), ("l2a", emit_l2a), ("l2f", emit_l2f),
              ("l3a", emit_l3a), ("l3f", emit_l3f)]


WEIGHT_SPECS = [
    ("w_l0a_q", [1024, 1024]), ("w_l0a_k", [1024, 1024]),
    ("w_l0a_v", [1024, 1024]), ("w_l0a_o", [1024, 1024]),
    ("c_ropeA_q", [4096, 512]), ("c_ropeA_k", [4096, 512]),
    ("p_l0f_nw", [128, 8]), ("p_l0f_nb", [128, 8]),
    ("w_l0f_g", [1024, 8]), ("w_l0f_c", [1024, 2]),
    ("w_l0f_1", [9 * 1024, 4096]), ("p_l0f_b1", [128, 9 * 32]),
    ("w_l0f_2", [9 * 4096, 1024]), ("b_l0f_2", [9, 1024]),
    ("w_l1a_sh", [1024, 128]), ("p_l1a_shb", [128, 1]),
    ("w_l1a_q", [1024, 768]), ("p_l1a_bq", [128, 6]),
    ("w_l1a_k", [1024, 768]), ("p_l1a_bk", [128, 6]),
    ("w_l1a_v", [1024, 1024]), ("r_l1a_bv", [1, 1024]),
    ("w_l1a_o", [1024, 1024]), ("p_l1a_bo", [128, 8]),
    ("p_l1f_n1", [128, 8]), ("p_l1f_n2", [128, 8]),
    ("w_l1f_q", [1024, 1024]), ("c_ropeB_q", [4096, 512]), ("c_ropeB_k", [4096, 512]),
    ("w_l1f_dkv", [1024, 512]), ("p_l1f_lng", [128, 4]), ("p_l1f_lnb", [128, 4]),
    ("w_l1f_uk", [512, 1024]), ("w_l1f_uv", [512, 1024]),
    ("w_l1f_o", [1024, 1024]),
    ("w_l1f_g", [1024, 4]), ("r_l1f_gb", [1, 4]),
    ("w_l1f_1", [4 * 1024, 4096]), ("p_l1f_b1", [128, 4 * 32]),
    ("w_l1f_2", [4 * 4096, 1024]), ("b_l1f_2", [4, 1024]),
    ("w_l2a_in", [1024, 2048]), ("p_l2a_bin", [128, 16]),
    ("p_l2a_A", [128, 8]), ("p_l2a_C", [128, 8]),
    ("w_l2a_out", [1024, 1024]), ("p_l2a_bout", [128, 8]),
    ("w_l2f_g", [1024, 8]),
    ("w_l2f_1", [8 * 1024, 2048]), ("w_l2f_2", [8 * 2048, 1024]),
    ("w_l3a_q", [1024, 1024]), ("w_l3a_k", [1024, 1024]),
    ("w_l3a_v", [1024, 1024]), ("w_l3a_o", [1024, 1024]),
    ("w_l3f_q", [1024, 1024]), ("p_l3f_bq", [128, 8]),
    ("w_l3f_k", [1024, 1024]), ("p_l3f_bk", [128, 8]),
    ("w_l3f_v", [1024, 1024]), ("r_l3f_bv", [1, 1024]),
    ("w_l3f_g", [1024, 8]), ("r_l3f_gb", [1, 8]),
    ("w_l3f_e", [8 * 1024, 1024]), ("b_l3f_e", [8, 1024]),
    ("p_l3f_g", [128, 8]), ("p_l3f_b", [128, 8]),
]

SCRATCH_SPECS = [
    ("lnb", [1024, T], "bf"), ("lnf", [1024, T], "f32"), ("aob", [1024, T], "bf"),
    ("u", [1024, T], "f32"), ("r1b", [1024, T], "bf"), ("r2b", [1024, T], "bf"),
    ("qf", [1024, T], "bf"), ("kf", [1024, T], "bf"), ("vtm", [T, 1024], "bf"),
    ("ao", [1024, T], "bf"), ("shf", [128, T], "bf"), ("qpf", [768, T], "bf"),
    ("kpf", [768, T], "bf"), ("latr", [512, T], "f32"), ("latb", [512, T], "bf"),
    ("pj2", [2048, T], "f32"), ("sdy", [1024, T], "bf"), ("macc", [1024, T], "f32"),
    ("x2t", [1024, T], "f32"),
    ("wrows", [16, T], "bf"), ("rtdbg", [128, 640], "f32"), ("ccin0", [1, 8], "f32"), ("ccout0", [8, 8], "f32"),
    ("ccin1", [1, 8], "f32"), ("ccout1", [8, 8], "f32"),
]


def build_module(blocks, core_id=0):
    from contextlib import ExitStack
    import os
    nc = bass.Bass("TRN2", target_bir_lowering=False, debug=False,
                   detect_race_conditions=bool(os.environ.get("RACEDET")))
    stack = ExitStack()
    with stack:
        p = P(nc)
        t = {}
        t["xin"] = p.dram_tensor("xin", [1024, T], F32, kind="ExternalInput")
        t["y"] = p.dram_tensor("y", [1024, T], F32, kind="ExternalOutput")
        for name, shape in WEIGHT_SPECS:
            t[name] = p.dram_tensor(name, shape, F32, kind="ExternalInput")
        for name, shape, dt_ in SCRATCH_SPECS:
            t[name] = p.dram_tensor(name, shape, BF16 if dt_ == "bf" else F32,
                                    kind="Internal")
        for name, shape, dtype in CONST_SPECS:
            t[name] = p.dram_tensor(name, shape, dtype, kind="ExternalInput")
        p.finalize_sems(stack)
        b = setup_ctx(p, stack)
        load_consts(p, b, t)
        for kc in range(8):
            p.dma("qsp", t["y"].ap()[kc * 128:(kc + 1) * 128, :],
                  t["xin"].ap()[kc * 128:(kc + 1) * 128, :],
                  writes=dk(t["y"], kc * 128, (kc + 1) * 128, 0, T))
        import os as _os
        dbg_stop = _os.environ.get("DBG_STOP", "")
        for name, fn in BLOCK_EMIT:
            if name in blocks:
                fn(p, b, t)
        dbg = _os.environ.get("DBG_COPY", "")
        if dbg:
            src = t[dbg]
            nr = src.shape[0] if hasattr(src, "shape") else 1024
            ncol = src.shape[1] if hasattr(src, "shape") else T
            nrr = min(128, nr)
            for kc in range(max(1, min(nr // 128, 8))):
                for tt in range(min(NTT, ncol // TTW)):
                    sl = b.osf[0:nrr, (tt % 4) * 512:(tt % 4) * 512 + 512]
                    p.dma("qsp", b.osb[0:nrr, (tt % 4) * 512:(tt % 4) * 512 + 512]
                          if src.dtype == BF16 else sl,
                          src.ap()[kc * 128:kc * 128 + nrr, tt * TTW:(tt + 1) * TTW],
                          reads=dk(src, kc * 128, kc * 128 + nrr, tt * TTW, (tt + 1) * TTW),
                          writes=[(b.osb, ("o", tt % 4)) if src.dtype == BF16
                                  else (b.osf, ("o", tt % 4))])
                    if src.dtype == BF16:
                        p.emit("dve", lambda e, o=sl, i=b.osb[0:nrr, (tt % 4) * 512:(tt % 4) * 512 + 512]:
                                   e.tensor_copy(o, i),
                               reads=[(b.osb, ("o", tt % 4))], writes=[(b.osf, ("o", tt % 4))])
                    p.dma("qact", t["y"].ap()[kc * 128:kc * 128 + nrr, tt * TTW:(tt + 1) * TTW],
                          sl, reads=[(b.osf, ("o", tt % 4))],
                          writes=dk(t["y"], kc * 128, kc * 128 + nrr, tt * TTW, (tt + 1) * TTW))
        p_drain(p)
        p.run_blocks()
    return nc, p



def rope_tables_A():
    pos = np.arange(4096, dtype=np.float64)
    i = np.arange(32, dtype=np.float64)
    coss, sins = [], []
    for s in (1.0, 4.0, 16.0):
        freqs = 1.0 / (10000.0 * s) ** (2.0 * i / 64.0)
        ang = pos[:, None] * freqs[None, :]
        coss.append(np.cos(ang)); sins.append(np.sin(ang))
    cos = np.mean(coss, 0).astype(np.float32)
    sin = np.mean(sins, 0).astype(np.float32)
    cq = np.concatenate([np.tile(cos, (1, 8)) / 8.0, np.tile(sin, (1, 8)) / 8.0], 1)
    ck = np.concatenate([np.tile(cos, (1, 8)), np.tile(sin, (1, 8))], 1)
    return cq.astype(np.float32), ck.astype(np.float32)


def rope_tables_B():
    pos = np.arange(4096, dtype=np.float64)
    half = 32
    inv = 1.0 / 10000.0 ** (np.arange(half, dtype=np.float64) / half)
    ang = pos[:, None] * inv[None, :]
    cos = np.cos(ang).astype(np.float32)
    sin = np.sin(ang).astype(np.float32)
    cq = np.concatenate([np.tile(cos, (1, 8)) / 8.0, np.tile(sin, (1, 8)) / 8.0], 1)
    ck = np.concatenate([np.tile(cos, (1, 8)), np.tile(sin, (1, 8))], 1)
    return cq.astype(np.float32), ck.astype(np.float32)


def host_weights(i):
    w = {}
    f = np.float32
    A = lambda x: np.ascontiguousarray(np.asarray(x, f))
    w["w_l0a_q"] = A(i["L0_attn_Wq"]); w["w_l0a_k"] = A(i["L0_attn_Wk"])
    w["w_l0a_v"] = A(i["L0_attn_Wv"]); w["w_l0a_o"] = A(i["L0_attn_Wo"])
    w["c_ropeA_q"], w["c_ropeA_k"] = rope_tables_A()
    w["p_l0f_nw"] = pcol(i["L0_ffn_attn_nw"]); w["p_l0f_nb"] = pcol(i["L0_ffn_attn_nb"])
    w["w_l0f_g"] = A(i["L0_ffn_wg"]); w["w_l0f_c"] = A(i["L0_ffn_res_coef"])
    w1 = np.concatenate([np.asarray(i["L0_ffn_inter_w"], f),
                         np.asarray(i["L0_ffn_res_inter_w"], f)[None]], 0)
    w["w_l0f_1"] = A(w1.reshape(9 * 1024, 4096))
    b1 = np.concatenate([np.asarray(i["L0_ffn_inter_b"], f),
                         np.asarray(i["L0_ffn_res_inter_b"], f)[None]], 0)
    w["p_l0f_b1"] = np.concatenate([pcol(b1[e]) for e in range(9)], 1)
    w2 = np.concatenate([np.asarray(i["L0_ffn_output_w"], f),
                         np.asarray(i["L0_ffn_res_output_w"], f)[None]], 0)
    w["w_l0f_2"] = A(w2.reshape(9 * 4096, 1024))
    b2 = np.concatenate([np.asarray(i["L0_ffn_output_b"], f),
                         np.asarray(i["L0_ffn_res_output_b"], f)[None]], 0)
    w["b_l0f_2"] = A(b2)
    sh = np.zeros((1024, 128), f); sh[:, 0:16] = np.asarray(i["L1_attn_W_share"], f)
    w["w_l1a_sh"] = sh
    shb = np.zeros(128, f); shb[0:16] = np.asarray(i["L1_attn_b_share"], f)
    w["p_l1a_shb"] = pcol(shb)[:, 0:1]
    w["w_l1a_q"] = A(i["L1_attn_Wq"]); w["p_l1a_bq"] = pcol(i["L1_attn_bq"])
    w["w_l1a_k"] = A(i["L1_attn_Wk"]); w["p_l1a_bk"] = pcol(i["L1_attn_bk"])
    w["w_l1a_v"] = A(i["L1_attn_Wv"]); w["r_l1a_bv"] = A(i["L1_attn_bv"])[None, :]
    w["w_l1a_o"] = A(i["L1_attn_Wo"]); w["p_l1a_bo"] = pcol(i["L1_attn_bo"])
    w["p_l1f_n1"] = pcol(i["L1_ffn_norm1_w"]); w["p_l1f_n2"] = pcol(i["L1_ffn_norm2_w"])
    w["w_l1f_q"] = A(np.asarray(i["L1_ffn_Wq"], f).T)
    w["c_ropeB_q"], w["c_ropeB_k"] = rope_tables_B()
    w["w_l1f_dkv"] = A(np.asarray(i["L1_ffn_Wdkv"], f).T)
    w["p_l1f_lng"] = pcol(i["L1_ffn_ln_g"]); w["p_l1f_lnb"] = pcol(i["L1_ffn_ln_b"])
    w["w_l1f_uk"] = A(np.asarray(i["L1_ffn_Wuk"], f).T)
    w["w_l1f_uv"] = A(np.asarray(i["L1_ffn_Wuv"], f).T)
    w["w_l1f_o"] = A(np.asarray(i["L1_ffn_Wo"], f).T)
    w["w_l1f_g"] = A(np.asarray(i["L1_ffn_gate_w"], f).T)
    w["r_l1f_gb"] = A(i["L1_ffn_gate_b"])[None, :]
    W1 = np.asarray(i["L1_ffn_W1"], f)
    w["w_l1f_1"] = A(W1.transpose(0, 2, 1).reshape(4 * 1024, 4096))
    b1 = np.asarray(i["L1_ffn_b1"], f)
    w["p_l1f_b1"] = np.concatenate([pcol(b1[e]) for e in range(4)], 1)
    W2 = np.asarray(i["L1_ffn_W2"], f)
    w["w_l1f_2"] = A(W2.transpose(0, 2, 1).reshape(4 * 4096, 1024))
    w["b_l1f_2"] = A(i["L1_ffn_b2"])
    w["w_l2a_in"] = A(i["L2_attn_W_in"]); w["p_l2a_bin"] = pcol(i["L2_attn_b_in"])
    w["p_l2a_A"] = pcol(i["L2_attn_A"]); w["p_l2a_C"] = pcol(i["L2_attn_C"])
    w["w_l2a_out"] = A(i["L2_attn_W_out"]); w["p_l2a_bout"] = pcol(i["L2_attn_b_out"])
    w["w_l2f_g"] = A(i["L2_ffn_gate_w"])
    w["w_l2f_1"] = A(np.asarray(i["L2_ffn_w1"], f).reshape(8 * 1024, 2048))
    w["w_l2f_2"] = A(np.asarray(i["L2_ffn_w2"], f).reshape(8 * 2048, 1024))
    w["w_l3a_q"] = A(np.asarray(i["L3_attn_w_q"], f).reshape(1024, 1024))
    w["w_l3a_k"] = A(np.asarray(i["L3_attn_w_k"], f).reshape(1024, 1024))
    w["w_l3a_v"] = A(np.asarray(i["L3_attn_w_v"], f).reshape(1024, 1024))
    w["w_l3a_o"] = A(np.asarray(i["L3_attn_w_o"], f).transpose(1, 2, 0).reshape(1024, 1024))
    w["w_l3f_q"] = A(np.asarray(i["L3_ffn_Wq"], f).transpose(2, 0, 1).reshape(1024, 1024))
    w["p_l3f_bq"] = pcol(np.asarray(i["L3_ffn_bq"], f).reshape(1024))
    w["w_l3f_k"] = A(np.asarray(i["L3_ffn_Wk"], f).transpose(2, 0, 1).reshape(1024, 1024))
    w["p_l3f_bk"] = pcol(np.asarray(i["L3_ffn_bk"], f).reshape(1024))
    w["w_l3f_v"] = A(np.asarray(i["L3_ffn_Wv"], f).transpose(2, 0, 1).reshape(1024, 1024))
    w["r_l3f_bv"] = A(np.asarray(i["L3_ffn_bv"], f).reshape(1024))[None, :]
    w["w_l3f_g"] = A(np.asarray(i["L3_ffn_Wsw"], f).T)
    w["r_l3f_gb"] = A(i["L3_ffn_bsw"])[None, :]
    We = np.asarray(i["L3_ffn_We"], f)
    w["w_l3f_e"] = A(We.transpose(0, 2, 1).reshape(8 * 1024, 1024))
    w["b_l3f_e"] = A(i["L3_ffn_be"])
    w["p_l3f_g"] = pcol(i["L3_ffn_gamma"]); w["p_l3f_b"] = pcol(i["L3_ffn_beta"])
    return w


ALL_BLOCKS = ["l0a", "l0f", "l1a", "l1f", "l2a", "l2f", "l3a", "l3f"]


def kernel(**inputs):
    x = np.asarray(inputs["L0_attn_x"], np.float32)
    w = host_weights(inputs)
    nc, p = build_module(set(ALL_BLOCKS))
    in_maps = []
    for c in range(8):
        m = dict(w)
        m["xin"] = np.ascontiguousarray(x[c].T)
        m.update(make_consts_host(c))
        in_maps.append(m)
    res = run_bass_kernel_spmd(nc, in_maps, core_ids=list(range(8)))
    out = np.stack([np.ascontiguousarray(res.results[c]["y"].T) for c in range(8)], 0)
    return out
